# version 17
# baseline (speedup 1.0000x reference)
import os
from contextlib import ExitStack

import numpy as np
import concourse.bass as bass
import concourse.mybir as mybir
from concourse.bass import IndirectOffsetOnAxis
from concourse.bass_utils import run_bass_kernel_spmd

F32 = mybir.dt.float32
BF16 = mybir.dt.bfloat16
U32 = mybir.dt.uint32
AF = mybir.ActivationFunctionType
ALU = mybir.AluOpType
AX = mybir.AxisListType

ENGINES = ("sync", "scalar", "vector", "tensor", "gpsimd")
EPS = 1e-5


class Sem:
    def __init__(self, nc, ctx, name):
        self.h = ctx.enter_context(nc.semaphore(name))
        self.n = 0

    def inc(self, k=1):
        self.n += k
        return self.n


class Prog:
    def __init__(self, nc, ctx):
        self.nc = nc
        self.ctx = ctx
        self.q = {e: [] for e in ENGINES}

    def sem(self, name):
        return Sem(self.nc, self.ctx, name)

    def op(self, engine, fn, inc=None, k=1):
        def wrapped(eng, fn=fn, inc=inc, k=k):
            inst = fn(eng)
            if inc is not None:
                inst.then_inc(inc.h, k)
        self.q[engine].append(wrapped)
        return inc.inc(k) if inc is not None else None

    def seq(self, engine, sem, fn):
        self.wait(engine, sem, sem.n)
        return self.op(engine, fn, inc=sem)

    def wait(self, engine, sem, val):
        if val > 0:
            self.q[engine].append(lambda eng, s=sem, v=val: eng.wait_ge(s.h, v))

    def install(self, block):
        for name in ENGINES:
            fns = self.q[name]
            getattr(block, name)(lambda eng, fns=fns: [fn(eng) for fn in fns])


def build(D=1024, HH=4096, E=8, T=2048, NSEQ=2, HS=1024, W=512, CAP=512, stage="ALL"):
    SL = {"INIT": 0, "A0": 1, "FFN": 2, "ALL": 5}[stage]
    DK = D // 128
    HJ = HH // 128
    NHS = HH // HS
    HJR = HS // 128
    NG = T // 128
    DH = max(1, D // 512)
    DW = min(512, D)
    NCG = CAP // 128
    SLOTS = E * CAP
    GPW = W // 128
    CHS = min(512, HS)
    CHJR = CHS // 128
    NCB = HH // CHS * E
    CPG = max(1, NCB // (NSEQ * NG))
    assert CAP == W

    nc = bass.Bass("TRN2", target_bir_lowering=False,
                   detect_race_conditions=os.environ.get("BASS_RD", "0") == "1")
    x_d = nc.dram_tensor("x", [NSEQ, T, D], F32, kind="ExternalInput").ap()
    wd_d = nc.dram_tensor("wd", [D, D], F32, kind="ExternalInput").ap()
    bd_d = nc.dram_tensor("bd", [D], F32, kind="ExternalInput").ap()
    wi_d = nc.dram_tensor("wi", [D, D], F32, kind="ExternalInput").ap()
    bi_d = nc.dram_tensor("bi", [D], F32, kind="ExternalInput").ap()
    w1_d = nc.dram_tensor("w1", [E, D, HH], F32, kind="ExternalInput").ap()
    b1_d = nc.dram_tensor("b1", [E, HH], F32, kind="ExternalInput").ap()
    w2_d = nc.dram_tensor("w2", [E, HH, D], F32, kind="ExternalInput").ap()
    b2_d = nc.dram_tensor("b2", [E, D], F32, kind="ExternalInput").ap()
    wr_d = nc.dram_tensor("wr", [D, E], F32, kind="ExternalInput").ap()
    br_d = nc.dram_tensor("br", [E], F32, kind="ExternalInput").ap()
    out_d = nc.dram_tensor("out", [NSEQ, T, D], F32, kind="ExternalOutput").ap()
    xn_scr = [nc.dram_tensor(f"xn_scr{k}", [T + 128, D], F32, kind="Internal").ap()
              for k in range(NSEQ)]
    g_scr = [nc.dram_tensor(f"g_scr{k}", [T + 128, D], BF16, kind="Internal").ap()
             for k in range(NSEQ)]
    s_scr = nc.dram_tensor("s_scr", [NSEQ, 128, DK * T], BF16, kind="Internal").ap()
    tok_scr = [nc.dram_tensor(f"tok_scr{k}", [SLOTS + T + 128], U32, kind="Internal").ap()
               for k in range(NSEQ)]

    with ExitStack() as ctx:
        en = ctx.enter_context
        identf = en(nc.sbuf_tensor("identf", [128, 128], F32))
        iota8 = en(nc.sbuf_tensor("iota8", [128, 8], F32))
        rev8 = en(nc.sbuf_tensor("rev8", [128, 8], F32))
        ones8f = en(nc.sbuf_tensor("ones8f", [E, 1], F32))
        ones8b = en(nc.sbuf_tensor("ones8b", [E, 1], BF16))
        onesf = en(nc.sbuf_tensor("onesf", [1, 128], F32))
        wr_sb = en(nc.sbuf_tensor("wr_sb", [128, DK * E], F32))
        br_row = en(nc.sbuf_tensor("br_row", [1, E], F32))
        brbc = en(nc.sbuf_tensor("brbc", [128, E], F32))
        b1c = en(nc.sbuf_tensor("b1c", [128, E * HJ], F32))
        rb1c = en(nc.sbuf_tensor("rb1c", [128, E * HJ], BF16))
        bdc = en(nc.sbuf_tensor("bdc", [128, DK], F32))
        bic = en(nc.sbuf_tensor("bic", [128, DK], F32))
        ckc = en(nc.sbuf_tensor("ckc", [128, E * DK], F32))
        mrows = en(nc.sbuf_tensor("mrows", [E, T], BF16))
        stats = en(nc.sbuf_tensor("stats", [128, 16], F32))
        r_sb = en(nc.sbuf_tensor("r_sb", [128, 16], F32))
        eqt = en(nc.sbuf_tensor("eqt", [128, 16], F32))
        idxs = en(nc.sbuf_tensor("idxs", [128, 16], F32))
        mtok = en(nc.sbuf_tensor("mtok", [128, 16], F32))
        capoff = en(nc.sbuf_tensor("capoff", [E, 1], F32))
        scatf = en(nc.sbuf_tensor("scatf", [128, NG], F32))
        scatidx = en(nc.sbuf_tensor("scatidx", [128, NG], U32))
        tokids = en(nc.sbuf_tensor("tokids", [128, NG], U32))
        tokinit = en(nc.sbuf_tensor("tokinit", [128, 1], U32))
        idxcols = en(nc.sbuf_tensor("idxcols", [128, NSEQ * SLOTS // 128], U32))
        b2rows = en(nc.sbuf_tensor("b2rows", [E, D], BF16))
        c_rows = en(nc.sbuf_tensor("c_rows", [E, D], F32))
        rowst = en(nc.sbuf_tensor("rowst", [1, D], F32))
        wdt_sb = en(nc.sbuf_tensor("wdt_sb", [128, 2 * DK * D], BF16))
        xnw = en(nc.sbuf_tensor("xnw", [128, DK * W], BF16))
        AQ = DK * HS
        arena = en(nc.sbuf_tensor("arena", [128, 4 * AQ], BF16))
        w1b = [arena[:, 0 * AQ:1 * AQ], arena[:, 1 * AQ:2 * AQ]]
        w2b = [arena[:, 2 * AQ:3 * AQ], arena[:, 3 * AQ:4 * AQ]]
        dec_a = arena[:, 0:DK * T]
        inj_a = arena[:, 2 * AQ:2 * AQ + DK * T]
        h_big = en(nc.sbuf_tensor("h_big", [128, HJR * W], BF16))
        zrow = h_big[:, 0:D]
        out_exp = [en(nc.sbuf_tensor(f"out_exp{k}", [128, DK * CAP], F32)) for k in range(NSEQ)]
        xi_sl = [en(nc.sbuf_tensor(f"xi_sl{k}", [128, DK * CAP], BF16)) for k in range(NSEQ)]
        cwb = [xi_sl[0][:, 0:CHJR * D], xi_sl[1][:, 0:CHJR * D]]
        gstg = [en(nc.sbuf_tensor(f"gstg{k}", [128, D], F32)) for k in range(2)]
        gt_st = [en(nc.sbuf_tensor(f"gt_st{k}", [128, D], BF16)) for k in range(2)]
        scr = en(nc.sbuf_tensor("scr", [128, 4 * D], F32))
        xstage = [scr[:, 0:D], scr[:, D:2 * D]]
        xstage4 = [xstage[0], xstage[1], gstg[0][:], gstg[1][:]]
        sqv = scr[:, 2 * D:3 * D]
        xnTf = scr[:, 3 * D:4 * D]
        cums = scr[0:E, 0:T]
        slotrow = scr[0:1, 2 * D:2 * D + T]
        outg = [scr[:, 2 * D:3 * D], scr[:, 3 * D:4 * D]]
        cg_t0 = [xi_sl[0][:, 0:D], xi_sl[0][:, D:2 * D]]
        cs_t0 = [xi_sl[1][:, 0:D], xi_sl[1][:, D:2 * D]]
        pbuf = en(nc.sbuf_tensor("pbuf", [128, 256], F32))

        pb = [en(nc.psum_tensor(f"pb{k}", [128, 512], F32)) for k in range(8)]

        block = en(nc.Block())
        p = Prog(nc, ctx)
        cld = p.sem("cld")
        wtc = p.sem("wtc")
        crw = p.sem("crw")
        xl = [p.sem("xl0"), p.sem("xl1")]
        gw = [p.sem("gw0"), p.sem("gw1")]
        so = [p.sem("so0"), p.sem("so1")]
        xns = [p.sem("xns0"), p.sem("xns1")]
        ssp = [p.sem("ssp0"), p.sem("ssp1")]
        cgl = [p.sem("cgl0"), p.sem("cgl1")]
        csl = [p.sem("csl0"), p.sem("csl1")]
        ga = [p.sem("ga0"), p.sem("ga1"), p.sem("ga2"), p.sem("ga3")]
        scg = [p.sem("scg0"), p.sem("scg1")]
        idp = p.sem("idp")
        xdp = p.sem("xdp")
        b2l = p.sem("b2l")
        ids = [p.sem("ids0"), p.sem("ids1")]
        idr = [p.sem("idr0"), p.sem("idr1")]
        pe = p.sem("pe")
        sc = p.sem("sc")
        dv = p.sem("dv")
        gp = p.sem("gp")

        p.op("scalar", lambda s: s.dma_start(
            out=wr_sb[:].rearrange("p (dk e) -> p dk e", dk=DK),
            in_=wr_d.rearrange("(dk p) e -> p dk e", p=128)), inc=cld, k=16)
        p.op("scalar", lambda s: s.dma_start(out=br_row[:], in_=br_d[None, :]), inc=cld, k=16)
        p.op("gpsimd", lambda g: g.dma_start(out=b2rows[:], in_=b2_d), inc=b2l, k=16)
        b1st = out_exp[1][0:HJ, 0:E * 128]
        bdst = out_exp[0][0:DK, 0:128]
        bist = out_exp[0][0:DK, 128:256]
        for i in range(E):
            p.op("scalar", lambda s, i=i: s.dma_start(
                out=b1st[:, i * 128:(i + 1) * 128],
                in_=b1_d[i].rearrange("(hj p) -> hj p", p=128)), inc=cld, k=16)
        p.op("scalar", lambda s: s.dma_start(
            out=bdst, in_=bd_d.rearrange("(ek p) -> ek p", p=128)), inc=cld, k=16)
        p.op("scalar", lambda s: s.dma_start(
            out=bist, in_=bi_d.rearrange("(ek p) -> ek p", p=128)), inc=cld, k=16)
        t_const = cld.n

        p.op("gpsimd", lambda g: g.iota(iota8[:], pattern=[[1, 8]], base=0, channel_multiplier=0,
                                        allow_small_or_imprecise_dtypes=True))
        p.op("gpsimd", lambda g: g.iota(rev8[:], pattern=[[-1, 8]], base=8, channel_multiplier=0,
                                        allow_small_or_imprecise_dtypes=True))
        p.op("gpsimd", lambda g: g.memset(ones8f[:], 1.0))
        p.op("gpsimd", lambda g: g.memset(ones8b[:], 1.0))
        p.op("gpsimd", lambda g: g.memset(onesf[:], 1.0))
        p.op("gpsimd", lambda g: g.memset(zrow, 0.0))
        p.op("gpsimd", lambda g: g.memset(sqv, 0.0))
        p.op("gpsimd", lambda g: g.iota(capoff[:], pattern=[[1, 1]], base=-1, channel_multiplier=CAP,
                                        allow_small_or_imprecise_dtypes=True))
        p.op("gpsimd", lambda g: g.iota(tokids[:], pattern=[[128, NG]], base=0, channel_multiplier=1,
                                        allow_small_or_imprecise_dtypes=True))
        p.op("gpsimd", lambda g: g.iota(tokinit[:], pattern=[[1, 1]], base=T,
                                        channel_multiplier=1, allow_small_or_imprecise_dtypes=True))
        p.op("gpsimd", lambda g: g.iota(identf[:], pattern=[[-1, 128]], base=0, channel_multiplier=1,
                                        allow_small_or_imprecise_dtypes=True), inc=gp)
        t_gpinit = gp.n
        p.wait("vector", gp, t_gpinit)
        p.op("vector", lambda v: v.tensor_single_scalar(identf[:], identf[:], 0.0, op=ALU.is_equal), inc=dv)
        t_ident = dv.n

        p.wait("gpsimd", gp, t_gpinit)
        for h in range(NSEQ):
            p.op("gpsimd", lambda g, h=h: g.dma_start(
                out=xn_scr[h][T:T + 128, :], in_=sqv), inc=xdp, k=16)
        prefill_q = []
        for h in range(NSEQ):
            for c in range(SLOTS // 128):
                prefill_q.append(lambda g, h=h, c=c: g.dma_start(
                    out=tok_scr[h][c * 128:(c + 1) * 128][:, None], in_=tokinit[:, 0:1]))
            for g0 in range(NG):
                prefill_q.append(lambda g, h=h, g0=g0: g.dma_start(
                    out=g_scr[h][g0 * 128:(g0 + 1) * 128, :], in_=zrow))
        PPG = (len(prefill_q) + NG - 1) // NG
        t_prefill = None

        p.wait("tensor", cld, t_const)
        p.wait("tensor", dv, t_ident)
        for i in range(E):
            p.op("tensor", lambda te, i=i: te.transpose(
                pb[3][:, i * HJ:(i + 1) * HJ], b1st[:, i * 128:(i + 1) * 128],
                identf[0:HJ, 0:HJ]))
        p.op("tensor", lambda te: te.transpose(
            pb[3][:, E * HJ:E * HJ + DK], bdst, identf[0:DK, 0:DK]))
        p.op("tensor", lambda te: te.transpose(
            pb[3][:, E * HJ + DK:E * HJ + 2 * DK], bist, identf[0:DK, 0:DK]), inc=pe)
        t_b1tr = pe.n
        p.wait("vector", pe, t_b1tr)
        p.seq("vector", dv, lambda v: v.tensor_copy(b1c[:], pb[3][:, 0:E * HJ]))
        p.seq("vector", dv, lambda v: v.tensor_copy(bdc[:], pb[3][:, E * HJ:E * HJ + DK]))
        p.seq("vector", dv, lambda v: v.tensor_copy(bic[:], pb[3][:, E * HJ + DK:E * HJ + 2 * DK]))
        t_b1c = dv.n
        p.wait("scalar", dv, t_b1c)
        p.op("scalar", lambda s: s.activation(rb1c[:], b1c[:], AF.Relu), inc=sc)
        t_rb1 = sc.n

        p.wait("tensor", cld, t_const)
        p.wait("tensor", gp, t_gpinit)
        p.op("tensor", lambda te: te.matmul(pb[2][:, 0:E], lhsT=onesf[:], rhs=br_row[:],
                                            start=True, stop=True), inc=pe)
        t_brmm = pe.n
        p.wait("vector", pe, t_brmm)
        p.seq("vector", dv, lambda v: v.tensor_copy(brbc[:], pb[2][:, 0:E]))
        t_brbc = dv.n

        w_srcs = (wd_d, wi_d)
        t_wtld = {}
        for m in range(2):
            for ek in range(DK):
                ci = m * DK + ek
                p.wait("sync", pe, t_wtld.get(ci - 2, 0))
                p.op("sync", lambda s, m=m, ek=ek, ci=ci: s.dma_start(
                    out=xstage[ci % 2], in_=w_srcs[m][ek * 128:(ek + 1) * 128, :]),
                    inc=xl[ci % 2], k=16)
                t_ch = xl[ci % 2].n
                p.wait("tensor", xl[ci % 2], t_ch)
                p.wait("tensor", dv, dv.n)
                if ci == 0:
                    p.wait("tensor", dv, t_ident)
                for dk in range(DK):
                    p.op("tensor", lambda te, dk=dk, ci=ci: te.transpose(
                        pb[dk // 4][:, (dk % 4) * 128:(dk % 4 + 1) * 128],
                        xstage[ci % 2][:, dk * 128:(dk + 1) * 128], identf[:]),
                        inc=pe if dk == DK - 1 else None)
                t_wtld[ci] = pe.n
                p.wait("vector", pe, pe.n)
                p.wait("vector", dv, dv.n)
                for dk in range(DK):
                    p.op("vector", lambda v, dk=dk, m=m, ek=ek: v.tensor_copy(
                        wdt_sb[:, m * DK * D + dk * D + ek * 128:m * DK * D + dk * D + (ek + 1) * 128],
                        pb[dk // 4][:, (dk % 4) * 128:(dk % 4 + 1) * 128]),
                        inc=dv if dk == DK - 1 else None)
        t_wdt = dv.n

        def wdt_sl(m, dk, ek):
            o = m * DK * D + dk * D + ek * 128
            return wdt_sb[:, o:o + 128]

        cblk = 0
        t_cw_used = {}
        t_crow = 0
        t_idxr = {0: 0, 1: 0}
        t_spill = {0: 0, 1: 0}
        t_xn_ev = 0
        t_bank_sc = 0
        t_bank_dv = 0

        def emit_cpass_block():
            nonlocal cblk, t_crow
            if cblk >= NCB:
                return
            i, cb = divmod(cblk, HH // CHS)
            buf = cblk % 2
            p.wait("gpsimd", pe, t_cw_used.get(cblk - 2, 0))
            p.op("gpsimd", lambda g, i=i, cb=cb, buf=buf: g.dma_start(
                out=cwb[buf].rearrange("p (hjr d) -> p hjr d", hjr=CHJR),
                in_=w2_d[i, cb * CHS:(cb + 1) * CHS, :].rearrange("(hjr p) d -> p hjr d", p=128)),
                inc=gw[buf], k=16)
            t_w = gw[buf].n
            p.wait("tensor", gw[buf], t_w)
            if cblk == 0:
                p.wait("tensor", sc, t_rb1)
            if cb == 0:
                p.wait("tensor", dv, t_crow)
            for hjr in range(CHJR):
                hj = cb * CHJR + hjr
                for dh in range(DH):
                    p.op("tensor", lambda te, i=i, hj=hj, hjr=hjr, dh=dh, buf=buf: te.matmul(
                        pb[6 + dh][0:1, 0:DW],
                        lhsT=rb1c[:, i * HJ + hj:i * HJ + hj + 1],
                        rhs=cwb[buf][:, hjr * D + dh * 512:hjr * D + dh * 512 + DW],
                        start=(hj == 0), stop=(hj == HJ - 1)),
                        inc=pe if (hjr == CHJR - 1 and dh == DH - 1) else None)
            t_cw_used[cblk] = pe.n
            if cb == HH // CHS - 1:
                t_cmm = pe.n
                p.wait("vector", pe, t_cmm)
                p.wait("vector", crw, crw.n)
                p.wait("vector", dv, dv.n)
                for dh in range(DH):
                    p.op("vector", lambda v, dh=dh: v.tensor_copy(
                        rowst[:, dh * 512:dh * 512 + DW], pb[6 + dh][0:1, 0:DW]),
                        inc=dv if dh == DH - 1 else None)
                t_crow = dv.n
                p.wait("sync", dv, t_crow)
                p.op("sync", lambda s, i=i: s.dma_start(out=c_rows[i:i + 1, :], in_=rowst[:]),
                     inc=crw, k=16)
            cblk += 1

        tB = {}
        t_ln = {}
        t_ssmw_pe = 0
        t_ssm_sc = {0: 0, 1: 0}
        t_ssm_dv = {0: 0, 1: 0}
        t_idx_dv = 0
        t_idx_pe = 0
        for h in range(NSEQ):
            gi0 = h * NG
            p.wait("sync", xns[0], xns[0].n)
            p.wait("sync", pe, tB.get((gi0 - 4, 'tr'), 0))
            p.wait("sync", pe, t_wtld.get(2 * DK - 2, 0))
            p.wait("sync", pe, t_idx_pe)
            p.op("sync", lambda s, h=h: s.dma_start(
                out=xstage4[0], in_=x_d[h, 0:128, :]), inc=xl[0], k=16)
            xg_t = [xl[0].n]
            t_spl = {}

            def emit_LN(g, h=h):
                for _ in range(PPG):
                    if prefill_q:
                        p.op("gpsimd", prefill_q.pop(0), inc=idp, k=16)
                par = g % 2
                gi = h * NG + g
                st = stats[:, par * 8:par * 8 + 8]
                xg_g = xstage4[g % 4]
                sq_g = sqv if par == 0 else out_exp[0][:, D:2 * D]
                if g + 1 < NG:
                    p.wait("sync", xns[(g + 1) % 2], xns[(g + 1) % 2].n)
                    p.wait("sync", pe, tB.get((gi - 3, 'tr'), 0))
                    if gi == 0:
                        p.wait("sync", pe, t_wtld[2 * DK - 1])
                    p.op("sync", lambda s, g=g: s.dma_start(
                        out=xstage4[(g + 1) % 4], in_=x_d[h, (g + 1) * 128:(g + 2) * 128, :]),
                        inc=xl[(g + 1) % 2], k=16)
                    xg_t.append(xl[(g + 1) % 2].n)
                p.wait("vector", xl[par], xg_t[g])
                if g == 0 and h == 0:
                    p.wait("vector", xdp, 32)
                p.seq("vector", dv, lambda v: v.reduce_sum(st[:, 0:1], xg_g, axis=AX.X))
                p.seq("vector", dv, lambda v: v.tensor_mul(sq_g, xg_g, xg_g))
                p.seq("vector", dv, lambda v: v.reduce_sum(st[:, 1:2], sq_g, axis=AX.X))
                p.seq("vector", dv, lambda v: v.tensor_scalar_mul(st[:, 0:1], st[:, 0:1], 1.0 / D))
                p.seq("vector", dv, lambda v: v.tensor_mul(st[:, 7:8], st[:, 0:1], st[:, 0:1]))
                p.seq("vector", dv, lambda v: v.tensor_scalar(
                    st[:, 2:3], st[:, 1:2], 1.0 / D, None, op0=ALU.mult))
                p.seq("vector", dv, lambda v: v.tensor_tensor(
                    st[:, 2:3], st[:, 2:3], st[:, 7:8], op=ALU.subtract))
                p.seq("vector", dv, lambda v: v.tensor_scalar(
                    st[:, 2:3], st[:, 2:3], EPS, None, op0=ALU.add))
                t_var = dv.n
                p.wait("scalar", dv, t_var)
                p.seq("scalar", sc, lambda s: s.activation(st[:, 3:4], st[:, 2:3], AF.Sqrt))
                t_sd = sc.n
                p.wait("vector", sc, t_sd)
                p.seq("vector", dv, lambda v: v.reciprocal(st[:, 4:5], st[:, 3:4]))
                p.seq("vector", dv, lambda v: v.tensor_scalar(
                    xg_g, xg_g, st[:, 0:1], st[:, 4:5],
                    op0=ALU.subtract, op1=ALU.mult))
                t_xnst = dv.n
                t_ln[gi] = t_xnst
                p.wait("sync", dv, t_xnst)
                p.op("sync", lambda s, g=g, xg_g=xg_g: s.dma_start(
                    out=xn_scr[h][g * 128:(g + 1) * 128, :], in_=xg_g), inc=xns[par], k=16)
                t_spl[g] = xns[par].n

            def emit_Btr(g, h=h):
                par = g % 2
                gi = h * NG + g
                pair = (pb[0], pb[1]) if par == 0 else (pb[2], pb[3])
                xnTf_g = xnTf if par == 0 else out_exp[0][:, 0:D]
                p.wait("tensor", dv, t_ln[gi])
                p.wait("tensor", dv, tB.get((gi - 2, 'evac'), 0))
                if par == 1:
                    p.wait("tensor", sc, t_ssm_sc[1])
                    p.wait("tensor", dv, t_ssm_dv[1])
                if h == 0 and g < 2:
                    p.wait("tensor", dv, t_wdt)
                for dk in range(DK):
                    p.op("tensor", lambda te, dk=dk, g=g: te.transpose(
                        pair[dk // 4][:, (dk % 4) * 128:(dk % 4 + 1) * 128],
                        xstage4[g % 4][:, dk * 128:(dk + 1) * 128], identf[:]),
                        inc=pe if dk == DK - 1 else None)
                tB[(gi, 'tr')] = pe.n
                p.wait("vector", pe, tB[(gi, 'tr')])
                if g % GPW == 0:
                    p.wait("vector", pe, t_ssmw_pe)
                p.wait("vector", dv, dv.n)
                gw_ = g % GPW
                for dk in range(DK):
                    p.op("vector", lambda v, dk=dk, xnTf_g=xnTf_g, pair=pair: v.tensor_copy(
                        xnTf_g[:, dk * 128:(dk + 1) * 128],
                        pair[dk // 4][:, (dk % 4) * 128:(dk % 4 + 1) * 128]),
                        inc=dv if dk == DK - 1 else None)
                tB[(gi, 'evT')] = dv.n
                for dk in range(DK):
                    p.op("vector", lambda v, dk=dk, gw_=gw_, pair=pair: v.tensor_copy(
                        xnw[:, dk * W + gw_ * 128:dk * W + (gw_ + 1) * 128],
                        pair[dk // 4][:, (dk % 4) * 128:(dk % 4 + 1) * 128]),
                        inc=dv if dk == DK - 1 else None)
                tB[(gi, 'evac')] = dv.n

            def emit_Bmask(g, h=h):
                par = g % 2
                gi = h * NG + g
                rtb = pb[5]
                xnTf_g = xnTf if par == 0 else out_exp[0][:, 0:D]
                st = stats[:, par * 8:par * 8 + 8]
                p.wait("tensor", dv, tB[(gi, 'evT')])
                p.wait("tensor", dv, tB.get((gi - 1, 'radd'), 0))
                p.wait("tensor", dv, t_ssm_dv[0])
                for dk in range(DK):
                    p.op("tensor", lambda te, dk=dk, xnTf_g=xnTf_g, rtb=rtb: te.matmul(
                        rtb[:, 0:E], lhsT=xnTf_g[:, dk * 128:(dk + 1) * 128],
                        rhs=wr_sb[:, dk * E:(dk + 1) * E],
                        start=(dk == 0), stop=(dk == DK - 1)),
                        inc=pe if dk == DK - 1 else None)
                t_rmm = pe.n
                p.wait("vector", pe, t_rmm)
                if g == 0 and h == 0:
                    p.wait("vector", dv, t_brbc)
                rsb = r_sb[:, par * 8:par * 8 + 8]
                eq = eqt[:, par * 8:par * 8 + 8]
                ix = idxs[:, par * 8:par * 8 + 8]
                mt = mtok[:, par * 8:par * 8 + 8]
                p.seq("vector", dv, lambda v, rtb=rtb: v.tensor_add(rsb[:, 0:E], rtb[:, 0:E], brbc[:]))
                tB[(gi, 'radd')] = dv.n
                p.seq("vector", dv, lambda v: v.reduce_max(st[:, 5:6], rsb[:, 0:E], axis=AX.X))
                p.seq("vector", dv, lambda v: v.tensor_scalar(
                    eq[:, 0:E], rsb[:, 0:E], st[:, 5:6], None, op0=ALU.is_equal))
                p.seq("vector", dv, lambda v: v.tensor_mul(ix[:, 0:E], eq[:, 0:E], rev8[:, 0:E]))
                p.seq("vector", dv, lambda v: v.tensor_scalar(
                    ix[:, 0:E], ix[:, 0:E], -1.0, 8.0, op0=ALU.mult, op1=ALU.add))
                p.seq("vector", dv, lambda v: v.tensor_reduce(
                    st[:, 6:7], ix[:, 0:E], axis=AX.X, op=ALU.min))
                p.seq("vector", dv, lambda v: v.tensor_scalar(
                    mt[:, 0:E], iota8[:, 0:E], st[:, 6:7], None, op0=ALU.is_equal))
                t_mtok = dv.n
                p.wait("tensor", dv, t_mtok)
                p.wait("tensor", dv, tB.get((gi - 1, 'mrows'), 0))
                p.op("tensor", lambda te, mt=mt, rtb=rtb: te.transpose(
                    rtb[0:E, 8:136], mt[:, 0:E], identf[:]), inc=pe)
                t_mtr = pe.n
                p.wait("vector", pe, t_mtr)
                p.seq("vector", dv, lambda v, g=g, rtb=rtb: v.tensor_copy(
                    mrows[:, g * 128:(g + 1) * 128], rtb[0:E, 8:136]))
                tB[(gi, 'mrows')] = dv.n

            def emit_SSMW(w_, h=h):
                nonlocal t_ssmw_pe
                p.wait("tensor", dv, tB[(h * NG + w_ * GPW + GPW - 1, 'evac')])
                p.wait("tensor", dv, t_wdt)
                if w_ == 0 and h > 0:
                    p.wait("tensor", ssp[h - 1], 16)
                    p.wait("tensor", dv, t_idx_dv)
                for m in range(2):
                    banks = (pb[4], pb[2]) if m == 0 else (pb[5], pb[3])
                    if m == 1:
                        p.wait("tensor", dv, tB[(h * NG + w_ * GPW + GPW - 1, 'mrows')])
                    for ek in range(DK):
                        bank = banks[ek % 2]
                        if m == 0:
                            p.wait("tensor", sc, t_ssm_sc[ek % 2])
                        else:
                            p.wait("tensor", dv, t_ssm_dv[ek % 2])
                        for dk in range(DK):
                            p.op("tensor", lambda te, m=m, dk=dk, ek=ek, bank=bank: te.matmul(
                                bank[:, 0:W], lhsT=wdt_sl(m, dk, ek),
                                rhs=xnw[:, dk * W:(dk + 1) * W],
                                start=(dk == 0), stop=(dk == DK - 1)),
                                inc=pe if dk == DK - 1 else None)
                        t_mm = pe.n
                        if m == 0:
                            p.wait("scalar", pe, t_mm)
                            p.wait("scalar", sc, sc.n)
                            p.op("scalar", lambda s, ek=ek, w_=w_, bank=bank: s.activation(
                                dec_a[:, ek * T + w_ * W:ek * T + (w_ + 1) * W], bank[:, 0:W],
                                AF.Sigmoid, bias=bdc[:, ek:ek + 1]), inc=sc)
                            t_ssm_sc[ek % 2] = sc.n
                        else:
                            p.wait("vector", pe, t_mm)
                            p.wait("vector", dv, dv.n)
                            p.op("vector", lambda v, ek=ek, w_=w_, bank=bank: v.tensor_scalar(
                                inj_a[:, ek * T + w_ * W:ek * T + (w_ + 1) * W], bank[:, 0:W],
                                bic[:, ek:ek + 1], None, op0=ALU.add), inc=dv)
                            t_ssm_dv[ek % 2] = dv.n
                t_ssmw_pe = pe.n

            LAG = 2
            for it in range(NG + LAG + 1 if SL >= 1 else 0):
                if it < NG:
                    emit_LN(it)
                gm = it - LAG - 1
                if 0 <= gm < NG:
                    emit_Bmask(gm)
                    if gm % GPW == GPW - 1:
                        emit_SSMW(gm // GPW)
                if LAG <= it < NG + LAG:
                    emit_Btr(it - LAG)
                if 0 <= gm < NG:
                    for _ in range(CPG):
                        emit_cpass_block()
            t_A0 = dv.n
            t_ssm_all_sc = sc.n

            if SL >= 2:
                p.wait("vector", dv, t_A0)
                p.wait("vector", pe, pe.n)
                p.wait("vector", xns[0], xns[0].n)
                p.wait("vector", xns[1], xns[1].n)
                p.seq("vector", dv, lambda v: v.tensor_tensor_scan(
                    cums, mrows[:], mrows[:], 0.0, op0=ALU.add, op1=ALU.max))
                p.seq("vector", dv, lambda v: v.tensor_scalar(
                    cums, cums, capoff[:, 0:1], None, op0=ALU.add))
                p.seq("vector", dv, lambda v: v.tensor_mul(cums, cums, mrows[:]))
                t_slots = dv.n
                p.wait("tensor", dv, t_slots)
                p.wait("tensor", sc, t_ssm_all_sc)
                for n in range(T // 512):
                    p.op("tensor", lambda te, n=n: te.matmul(
                        pb[4][0:1, 0:512], lhsT=ones8f[:], rhs=cums[:, n * 512:(n + 1) * 512],
                        start=True, stop=True), inc=pe)
                    t_sr = pe.n
                    p.wait("vector", pe, t_sr)
                    p.seq("vector", dv, lambda v, n=n: v.tensor_copy(
                        slotrow[:, n * 512:(n + 1) * 512], pb[4][0:1, 0:512]))
                    p.wait("tensor", dv, dv.n)
                t_srow = dv.n
                p.wait("tensor", dv, t_srow)
                for g in range(NG):
                    p.op("tensor", lambda te, g=g: te.transpose(
                        pb[4][:, g:g + 1], slotrow[0:1, g * 128:(g + 1) * 128], identf[0:1, 0:1]),
                        inc=pe if g == NG - 1 else None)
                t_sct = pe.n
                t_idx_pe = t_sct
                p.wait("vector", pe, t_sct)
                p.wait("vector", ids[0], ids[0].n)
                p.wait("vector", ids[1], ids[1].n)
                p.seq("vector", dv, lambda v: v.tensor_copy(scatf[:], pb[4][:, 0:NG]))
                p.seq("vector", dv, lambda v: v.tensor_copy(scatidx[:], scatf[:]))
                t_sidx = dv.n
                t_idx_dv = t_sidx
                p.wait("gpsimd", idp, idp.n)
                p.wait("gpsimd", dv, t_sidx)
                for g in range(NG):
                    p.op("gpsimd", lambda gg, h=h, g=g: gg.indirect_dma_start(
                        out=tok_scr[h][:][:, None],
                        out_offset=IndirectOffsetOnAxis(ap=scatidx[:, g:g + 1], axis=0),
                        in_=tokids[:, g:g + 1], in_offset=None), inc=ids[h], k=16)
                t_scat = ids[h].n
                p.wait("gpsimd", ids[h], t_scat)
                for c in range(SLOTS // 128):
                    p.op("gpsimd", lambda s, h=h, c=c: s.dma_start(
                        out=idxcols[:, h * (SLOTS // 128) + c:h * (SLOTS // 128) + c + 1],
                        in_=tok_scr[h][c * 128:(c + 1) * 128][:, None]), inc=idr[h], k=16)
                t_idxr[h] = idr[h].n

                p.wait("vector", sc, t_ssm_all_sc)
                for ek in range(DK):
                    p.seq("vector", dv, lambda v, ek=ek: v.tensor_tensor_scan(
                        dec_a[:, ek * T:(ek + 1) * T], dec_a[:, ek * T:(ek + 1) * T],
                        inj_a[:, ek * T:(ek + 1) * T], 0.0, op0=ALU.mult, op1=ALU.add))
                t_scan = dv.n
                p.wait("gpsimd", dv, t_scan)
                p.op("gpsimd", lambda s, h=h: s.dma_start(
                    out=s_scr[h], in_=dec_a), inc=ssp[h], k=16)
                t_spill[h] = dv.n

        t_ck = 0
        if SL >= 2:
            p.wait("tensor", crw, crw.n)
            p.wait("tensor", b2l, b2l.n)
            sum8c = stats[:, 8:9]
            for dh in range(DH):
                p.wait("tensor", dv, dv.n)
                p.op("tensor", lambda te, dh=dh: te.matmul(
                    pb[6][0:1, 0:DW], lhsT=ones8f[:], rhs=c_rows[:, dh * 512:dh * 512 + DW],
                    start=True, stop=False))
                p.op("tensor", lambda te, dh=dh: te.matmul(
                    pb[6][0:1, 0:DW], lhsT=ones8b[:], rhs=b2rows[:, dh * 512:dh * 512 + DW],
                    start=False, stop=True), inc=pe)
                t_s8 = pe.n
                p.wait("vector", pe, t_s8)
                p.seq("vector", dv, lambda v, dh=dh: v.tensor_copy(
                    rowst[:, dh * 512:dh * 512 + DW], pb[6][0:1, 0:DW]))
            for dk in range(DK):
                p.wait("tensor", dv, dv.n)
                p.op("tensor", lambda te, dk=dk: te.transpose(
                    pb[7][:, 0:E], c_rows[0:E, dk * 128:(dk + 1) * 128], identf[0:E, 0:E]))
                p.op("tensor", lambda te, dk=dk: te.transpose(
                    pb[7][:, E:E + 1], rowst[0:1, dk * 128:(dk + 1) * 128], identf[0:1, 0:1]), inc=pe)
                t_st8 = pe.n
                p.wait("vector", pe, t_st8)
                p.seq("vector", dv, lambda v: v.tensor_copy(sum8c, pb[7][:, E:E + 1]))
                p.seq("vector", dv, lambda v, dk=dk: v.tensor_scalar(
                    ckc[:, dk:dk + (E - 1) * DK + 1:DK],
                    pb[7][:, 0:E], sum8c, -1.0, op0=ALU.subtract, op1=ALU.mult))
            t_ck = dv.n

        if SL >= 2:
            p.wait("gpsimd", dv, max(t_spill[0], t_spill[NSEQ - 1]))
            p.wait("gpsimd", ssp[0], 16)
            p.wait("gpsimd", ssp[NSEQ - 1], 16)
            p.wait("gpsimd", pe, pe.n)
            p.op("gpsimd", lambda g: g.dma_start(
                out=w1b[0].rearrange("p (dk q) -> p dk q", dk=DK),
                in_=w1_d[0, :, 0:HS].rearrange("(dk p) q -> p dk q", p=128)), inc=gw[0], k=16)
            p.op("gpsimd", lambda g: g.dma_start(
                out=w2b[0].rearrange("p (hjr d) -> p hjr d", hjr=HJR),
                in_=w2_d[0, 0:HS, :].rearrange("(hjr p) d -> p hjr d", p=128)), inc=gw[0], k=16)
            wblk_t = {0: (0, gw[0].n)}

            PH = ([4, 5], [0, 1])
            AB = ([0, 1, 2, 3], [4, 5, 6, 7])
            t_evac_a = 0
            t_evac_b = 0
            t_blk_lastmm = {}
            t_gstg_pe = {0: 0, 1: 0, 2: 0, 3: 0}
            t_scg = {0: 0, 1: 0}
            t_xi_all = {}
            t_sctr_last = 0
            t_gev_last = 0
            it = 0
            gstg4f = [gstg[0][:], gstg[1][:], xstage[0], xstage[1]]

            def emit_gathers(i):
                for hw in range(NSEQ):
                    for gg in range(NCG):
                        gb = (hw * NCG + gg) % 4
                        p.wait("gpsimd", idr[hw], t_idxr[hw])
                        p.wait("gpsimd", xns[0], xns[0].n)
                        p.wait("gpsimd", xns[1], xns[1].n)
                        p.wait("gpsimd", pe, t_gstg_pe.get(gb, 0))
                        p.op("gpsimd", lambda g2, i=i, hw=hw, gg=gg, gb=gb: g2.indirect_dma_start(
                            out=gstg4f[gb], out_offset=None,
                            in_=xn_scr[hw], in_offset=IndirectOffsetOnAxis(
                                ap=idxcols[:, hw * (SLOTS // 128) + i * NCG + gg:
                                           hw * (SLOTS // 128) + i * NCG + gg + 1], axis=0)),
                            inc=ga[gb], k=16)
                        t_g = ga[gb].n
                        p.wait("tensor", ga[gb], t_g)
                        p.wait("tensor", dv, dv.n)
                        for dk in range(DK):
                            p.op("tensor", lambda te, dk=dk, gb=gb: te.transpose(
                                pb[2 + dk // 4][:, (dk % 4) * 128:(dk % 4 + 1) * 128],
                                gstg4f[gb][:, dk * 128:(dk + 1) * 128], identf[:]),
                                inc=pe if dk == DK - 1 else None)
                        t_xtr = pe.n
                        t_gstg_pe[gb] = t_xtr
                        p.wait("vector", pe, t_xtr)
                        p.wait("vector", dv, dv.n)
                        for dk in range(DK):
                            p.op("vector", lambda v, dk=dk, hw=hw, gg=gg: v.tensor_copy(
                                xi_sl[hw][:, dk * CAP + gg * 128:dk * CAP + (gg + 1) * 128],
                                pb[2 + dk // 4][:, (dk % 4) * 128:(dk % 4 + 1) * 128]),
                                inc=dv if dk == DK - 1 else None)
                    t_xi_all[(i, hw)] = dv.n

            emit_gathers(0)
            for i in range(E):
                for hs in range(NHS):
                    blk = i * NHS + hs
                    buf = blk % 2
                    nblk_i, nblk_hs = (i, hs + 1) if hs + 1 < NHS else (i + 1, 0)
                    if nblk_i < E:
                        nbuf = (blk + 1) % 2
                        p.wait("gpsimd", pe, t_blk_lastmm.get(blk - 1, 0))
                        p.op("gpsimd", lambda g, i2=nblk_i, hs2=nblk_hs, nbuf=nbuf: g.dma_start(
                            out=w1b[nbuf].rearrange("p (dk q) -> p dk q", dk=DK),
                            in_=w1_d[i2, :, hs2 * HS:(hs2 + 1) * HS].rearrange("(dk p) q -> p dk q", p=128)),
                            inc=gw[nbuf], k=16)
                        p.op("gpsimd", lambda g, i2=nblk_i, hs2=nblk_hs, nbuf=nbuf: g.dma_start(
                            out=w2b[nbuf].rearrange("p (hjr d) -> p hjr d", hjr=HJR),
                            in_=w2_d[i2, hs2 * HS:(hs2 + 1) * HS, :].rearrange("(hjr p) d -> p hjr d", p=128)),
                            inc=gw[nbuf], k=16)
                        wblk_t[blk + 1] = (nbuf, gw[nbuf].n)

                    for hw in range(NSEQ):
                        par = it % 2
                        ph = PH[par]
                        ab = AB[par]
                        bb = AB[1 - par]
                        wbuf_s, wbuf_t = wblk_t[blk]
                        p.wait("tensor", gw[wbuf_s], wbuf_t)
                        p.wait("tensor", dv, t_xi_all[(i, hw)])
                        t_relu = {}
                        for hjr in range(HJR):
                            bank = ph[hjr % 2]
                            if hjr < 2:
                                p.wait("tensor", dv, t_evac_a)
                            else:
                                p.wait("tensor", sc, t_relu[hjr - 2])
                            for dk in range(DK):
                                p.op("tensor", lambda te, dk=dk, hjr=hjr, buf=buf, hw=hw, bank=bank: te.matmul(
                                    pb[bank][:, 0:W],
                                    lhsT=w1b[buf][:, dk * HS + hjr * 128:dk * HS + (hjr + 1) * 128],
                                    rhs=xi_sl[hw][:, dk * CAP:dk * CAP + W],
                                    start=(dk == 0), stop=(dk == DK - 1)),
                                    inc=pe if dk == DK - 1 else None)
                            t_mm1 = pe.n
                            hj_abs = hs * HJR + hjr
                            p.wait("scalar", pe, t_mm1)
                            p.wait("scalar", sc, sc.n)
                            p.op("scalar", lambda s, hjr=hjr, i=i, hj_abs=hj_abs, bank=bank: s.activation(
                                h_big[:, hjr * W:(hjr + 1) * W], pb[bank][:, 0:W], AF.Relu,
                                bias=b1c[:, i * HJ + hj_abs:i * HJ + hj_abs + 1]), inc=sc)
                            t_relu[hjr] = sc.n
                        p.wait("tensor", dv, t_evac_b)
                        if hs == 0 and hw == 0:
                            p.wait("tensor", dv, t_gev_last)
                        for hjr in range(HJR):
                            p.wait("tensor", sc, t_relu[hjr])
                            for dk in range(DK // 2):
                                p.op("tensor", lambda te, dk=dk, hjr=hjr, buf=buf, ab=ab: te.matmul(
                                    pb[ab[dk]][:, 0:W],
                                    lhsT=w2b[buf][:, hjr * D + dk * 128:hjr * D + (dk + 1) * 128],
                                    rhs=h_big[:, hjr * W:(hjr + 1) * W],
                                    start=(hjr == 0), stop=(hjr == HJR - 1)),
                                    inc=pe if (hjr == HJR - 1 and dk == DK // 2 - 1) else None)
                        t_mm2a = pe.n
                        for hjr in range(HJR):
                            for dk in range(DK // 2, DK):
                                p.op("tensor", lambda te, dk=dk, hjr=hjr, buf=buf, bb=bb: te.matmul(
                                    pb[bb[dk - DK // 2]][:, 0:W],
                                    lhsT=w2b[buf][:, hjr * D + dk * 128:hjr * D + (dk + 1) * 128],
                                    rhs=h_big[:, hjr * W:(hjr + 1) * W],
                                    start=(hjr == 0), stop=(hjr == HJR - 1)),
                                    inc=pe if (hjr == HJR - 1 and dk == DK - 1) else None)
                        t_mm2b = pe.n
                        t_blk_lastmm[blk] = t_mm2b

                        def emit_evac(banks, dk0, t_mm, hs=hs, hw=hw):
                            p.wait("vector", pe, t_mm)
                            p.wait("vector", dv, dv.n)
                            for j in range(DK // 2):
                                dk = dk0 + j
                                o0 = dk * CAP
                                if hs == 0:
                                    p.op("vector", lambda v, j=j, o0=o0, banks=banks, hw=hw: v.tensor_copy(
                                        out_exp[hw][:, o0:o0 + W], pb[banks[j]][:, 0:W]),
                                        inc=dv if j == DK // 2 - 1 else None)
                                else:
                                    p.op("vector", lambda v, j=j, o0=o0, banks=banks, hw=hw: v.tensor_add(
                                        out_exp[hw][:, o0:o0 + W], pb[banks[j]][:, 0:W],
                                        out_exp[hw][:, o0:o0 + W]),
                                        inc=dv if j == DK // 2 - 1 else None)
                            return dv.n

                        t_evac_a = emit_evac(ab, 0, t_mm2a)
                        t_evac_b = emit_evac(bb, DK // 2, t_mm2b)
                        it += 1

                if i + 1 < E:
                    emit_gathers(i + 1)
                p.wait("scalar", dv, t_evac_b)
                p.wait("scalar", dv, t_ck)
                p.wait("scalar", sc, sc.n)
                t_sil = {}
                for hw in range(NSEQ):
                    for dk in range(DK):
                        p.op("scalar", lambda s, dk=dk, i=i, hw=hw: s.activation(
                            out_exp[hw][:, dk * CAP:(dk + 1) * CAP],
                            out_exp[hw][:, dk * CAP:(dk + 1) * CAP],
                            AF.Silu, bias=ckc[:, i * DK + dk:i * DK + dk + 1]),
                            inc=sc if dk == DK - 1 else None)
                    t_sil[hw] = sc.n
                for hw in range(NSEQ):
                    for gg in range(NCG):
                        gb = (hw * NCG + gg) % 2
                        p.wait("tensor", sc, t_sil[hw])
                        p.wait("tensor", dv, dv.n)
                        for dk in range(DK):
                            p.op("tensor", lambda te, dk=dk, gg=gg, hw=hw: te.transpose(
                                pb[2 + dk // 4][:, (dk % 4) * 128:(dk % 4 + 1) * 128],
                                out_exp[hw][:, dk * CAP + gg * 128:dk * CAP + (gg + 1) * 128],
                                identf[:]),
                                inc=pe if dk == DK - 1 else None)
                        t_gtr = pe.n
                        t_sctr_last = t_gtr
                        p.wait("vector", pe, t_gtr)
                        p.wait("vector", dv, dv.n)
                        p.wait("vector", scg[gb], t_scg[gb])
                        for dk in range(DK):
                            p.op("vector", lambda v, dk=dk, gb=gb: v.tensor_copy(
                                gt_st[gb][:, dk * 128:(dk + 1) * 128],
                                pb[2 + dk // 4][:, (dk % 4) * 128:(dk % 4 + 1) * 128]),
                                inc=dv if dk == DK - 1 else None)
                        t_gev = dv.n
                        t_gev_last = t_gev
                        p.wait("gpsimd", dv, t_gev)
                        p.op("gpsimd", lambda g2, i=i, hw=hw, gg=gg, gb=gb: g2.indirect_dma_start(
                            out=g_scr[hw], out_offset=IndirectOffsetOnAxis(
                                ap=idxcols[:, hw * (SLOTS // 128) + i * NCG + gg:
                                           hw * (SLOTS // 128) + i * NCG + gg + 1], axis=0),
                            in_=gt_st[gb][:], in_offset=None), inc=scg[gb], k=16)
                        t_scg[gb] = scg[gb].n

        for h in range(NSEQ if SL >= 5 else 0):
            xg_t2 = []
            cg_t = []
            cs_t = []
            s_fm = s_scr[h].rearrange("p (dk t) -> p dk t", dk=DK)
            p.wait("sync", scg[0], scg[0].n)
            p.wait("sync", scg[1], scg[1].n)
            p.wait("sync", ssp[h], 16)
            p.wait("sync", pe, pe.n)
            p.wait("sync", dv, dv.n)
            p.op("sync", lambda s, h=h: s.dma_start(out=xstage[0], in_=x_d[h, 0:128, :]), inc=xl[0], k=16)
            xg_t2.append(xl[0].n)
            p.op("sync", lambda s, h=h: s.dma_start(out=cg_t0[0], in_=g_scr[h][0:128, :]), inc=cgl[0], k=16)
            cg_t.append(cgl[0].n)
            p.op("sync", lambda s, h=h, s_fm=s_fm: s.dma_start(
                out=cs_t0[0].rearrange("p (dk t) -> p dk t", dk=DK),
                in_=s_fm[:, :, 0:128]), inc=csl[0], k=16)
            cs_t.append(csl[0].n)
            t_pstore = {}
            t_pmul = {}
            so0_base = {0: so[0].n, 1: so[1].n}
            for g in range(NG):
                gb = g % 2
                pair = (pb[0], pb[1]) if gb == 0 else (pb[2], pb[3])
                if g + 1 < NG:
                    p.wait("sync", dv, dv.n)
                    p.op("sync", lambda s, h=h, g=g: s.dma_start(
                        out=xstage[(g + 1) % 2], in_=x_d[h, (g + 1) * 128:(g + 2) * 128, :]),
                        inc=xl[(g + 1) % 2], k=16)
                    xg_t2.append(xl[(g + 1) % 2].n)
                    p.op("sync", lambda s, h=h, g=g: s.dma_start(
                        out=cg_t0[(g + 1) % 2], in_=g_scr[h][(g + 1) * 128:(g + 2) * 128, :]),
                        inc=cgl[(g + 1) % 2], k=16)
                    cg_t.append(cgl[(g + 1) % 2].n)
                    p.op("sync", lambda s, h=h, g=g, s_fm=s_fm: s.dma_start(
                        out=cs_t0[(g + 1) % 2].rearrange("p (dk t) -> p dk t", dk=DK),
                        in_=s_fm[:, :, (g + 1) * 128:(g + 2) * 128]), inc=csl[(g + 1) % 2], k=16)
                    cs_t.append(csl[(g + 1) % 2].n)
                p.wait("vector", csl[gb], cs_t[g])
                p.wait("vector", dv, dv.n)
                p.op("vector", lambda v, gb=gb: v.tensor_copy(gstg[gb][:], cs_t0[gb]), inc=dv)
                t_sc32 = dv.n
                p.wait("tensor", dv, t_sc32)
                p.wait("tensor", dv, t_pmul.get(g - 2, 0))
                for dk in range(DK):
                    p.op("tensor", lambda te, dk=dk, gb=gb: te.transpose(
                        pair[dk // 4][:, (dk % 4) * 128:(dk % 4 + 1) * 128],
                        gstg[gb][:, dk * 128:(dk + 1) * 128], identf[:]),
                        inc=pe if dk == DK - 1 else None)
                t_str = pe.n
                p.wait("vector", pe, t_str)
                p.wait("vector", cgl[gb], cg_t[g])
                p.wait("vector", xl[gb], xg_t2[g])
                p.wait("vector", so[gb], t_pstore.get(g - 2, so0_base[gb]))
                p.wait("vector", dv, dv.n)
                for dk in range(DK):
                    p.op("vector", lambda v, dk=dk, gb=gb: v.tensor_mul(
                        cg_t0[gb][:, dk * 128:(dk + 1) * 128],
                        pair[dk // 4][:, (dk % 4) * 128:(dk % 4 + 1) * 128],
                        cg_t0[gb][:, dk * 128:(dk + 1) * 128]),
                        inc=dv if dk == DK - 1 else None)
                t_pmul[g] = dv.n
                ob = outg[gb]
                p.wait("vector", dv, t_pmul[g])
                p.op("vector", lambda v, gb=gb, ob=ob: v.tensor_add(
                    ob, cg_t0[gb], xstage[gb]), inc=dv)
                t_add = dv.n
                p.wait("scalar", dv, t_add)
                p.op("scalar", lambda s, h=h, g=g, ob=ob: s.dma_start(
                    out=out_d[h, g * 128:(g + 1) * 128, :], in_=ob), inc=so[gb], k=16)
                t_pstore[g] = so[gb].n

        p.wait("sync", so[0], so[0].n)
        p.wait("sync", so[1], so[1].n)

        p.install(block)
    return nc


def kernel(L0_attn_x_seq, L0_attn_W_decay, L0_attn_b_decay, L0_attn_W_input,
           L0_attn_b_input, L0_ffn_W1, L0_ffn_b1, L0_ffn_W2, L0_ffn_b2,
           L0_ffn_Wr, L0_ffn_br):
    x = np.ascontiguousarray(np.asarray(L0_attn_x_seq, dtype=np.float32))
    wd = np.ascontiguousarray(np.asarray(L0_attn_W_decay, dtype=np.float32))
    bd = np.ascontiguousarray(np.asarray(L0_attn_b_decay, dtype=np.float32))
    wi = np.ascontiguousarray(np.asarray(L0_attn_W_input, dtype=np.float32))
    bi = np.ascontiguousarray(np.asarray(L0_attn_b_input, dtype=np.float32))
    w1 = np.ascontiguousarray(np.asarray(L0_ffn_W1, dtype=np.float32))
    b1 = np.ascontiguousarray(np.asarray(L0_ffn_b1, dtype=np.float32))
    w2 = np.ascontiguousarray(np.asarray(L0_ffn_W2, dtype=np.float32))
    b2 = np.ascontiguousarray(np.asarray(L0_ffn_b2, dtype=np.float32))
    wr = np.ascontiguousarray(np.asarray(L0_ffn_Wr, dtype=np.float32))
    br = np.ascontiguousarray(np.asarray(L0_ffn_br, dtype=np.float32))

    n_cores = 8
    nc = build()
    in_maps = []
    for c in range(n_cores):
        in_maps.append(dict(
            x=x[2 * c:2 * c + 2], wd=wd, bd=bd, wi=wi, bi=bi,
            w1=w1, b1=b1, w2=w2, b2=b2, wr=wr, br=br))
    res = run_bass_kernel_spmd(nc, in_maps, core_ids=list(range(n_cores)))
    out = np.concatenate(
        [np.asarray(r["out"], dtype=np.float32).reshape(2, 2048, 1024) for r in res.results],
        axis=0)
    return out


# version 18
# speedup vs baseline: 1.2307x; 1.2307x over previous
import os
from contextlib import ExitStack

import numpy as np
import concourse.bass as bass
import concourse.mybir as mybir
from concourse.bass import IndirectOffsetOnAxis
from concourse.bass_utils import run_bass_kernel_spmd

F32 = mybir.dt.float32
BF16 = mybir.dt.bfloat16
U32 = mybir.dt.uint32
AF = mybir.ActivationFunctionType
ALU = mybir.AluOpType
AX = mybir.AxisListType

ENGINES = ("sync", "scalar", "vector", "tensor", "gpsimd")
EPS = 1e-5


class Sem:
    def __init__(self, nc, ctx, name):
        self.h = ctx.enter_context(nc.semaphore(name))
        self.n = 0

    def inc(self, k=1):
        self.n += k
        return self.n


class Prog:
    def __init__(self, nc, ctx):
        self.nc = nc
        self.ctx = ctx
        self.q = {e: [] for e in ENGINES}

    def sem(self, name):
        return Sem(self.nc, self.ctx, name)

    def op(self, engine, fn, inc=None, k=1):
        def wrapped(eng, fn=fn, inc=inc, k=k):
            inst = fn(eng)
            if inc is not None:
                inst.then_inc(inc.h, k)
        self.q[engine].append(wrapped)
        return inc.inc(k) if inc is not None else None

    def seq(self, engine, sem, fn):
        self.wait(engine, sem, sem.n)
        return self.op(engine, fn, inc=sem)

    def wait(self, engine, sem, val):
        if val > 0:
            self.q[engine].append(lambda eng, s=sem, v=val: eng.wait_ge(s.h, v))

    def install(self, block):
        for name in ENGINES:
            fns = self.q[name]
            getattr(block, name)(lambda eng, fns=fns: [fn(eng) for fn in fns])


def build(D=1024, HH=4096, E=8, T=2048, NSEQ=2, HS=1024, W=512, CAP=512, stage="ALL"):
    SL = {"INIT": 0, "A0": 1, "FFN": 2, "ALL": 5}[stage]
    DK = D // 128
    HJ = HH // 128
    NHS = HH // HS
    HJR = HS // 128
    NG = T // 128
    DH = max(1, D // 512)
    DW = min(512, D)
    NCG = CAP // 128
    SLOTS = E * CAP
    GPW = W // 128
    CHS = min(512, HS)
    CHJR = CHS // 128
    NCB = HH // CHS * E
    CPG = max(1, NCB // (NSEQ * NG))
    assert CAP == W

    nc = bass.Bass("TRN2", target_bir_lowering=False,
                   detect_race_conditions=os.environ.get("BASS_RD", "0") == "1")
    x_d = nc.dram_tensor("x", [NSEQ, T, D], F32, kind="ExternalInput").ap()
    wd_d = nc.dram_tensor("wd", [D, D], F32, kind="ExternalInput").ap()
    bd_d = nc.dram_tensor("bd", [D], F32, kind="ExternalInput").ap()
    wi_d = nc.dram_tensor("wi", [D, D], F32, kind="ExternalInput").ap()
    bi_d = nc.dram_tensor("bi", [D], F32, kind="ExternalInput").ap()
    w1_d = nc.dram_tensor("w1", [E, D, HH], F32, kind="ExternalInput").ap()
    b1_d = nc.dram_tensor("b1", [E, HH], F32, kind="ExternalInput").ap()
    w2_d = nc.dram_tensor("w2", [E, HH, D], F32, kind="ExternalInput").ap()
    b2_d = nc.dram_tensor("b2", [E, D], F32, kind="ExternalInput").ap()
    wr_d = nc.dram_tensor("wr", [D, E], F32, kind="ExternalInput").ap()
    br_d = nc.dram_tensor("br", [E], F32, kind="ExternalInput").ap()
    out_d = nc.dram_tensor("out", [NSEQ, T, D], F32, kind="ExternalOutput").ap()
    xn_scr = [nc.dram_tensor(f"xn_scr{k}", [T + 128, D], F32, kind="Internal").ap()
              for k in range(NSEQ)]
    g_scr = [nc.dram_tensor(f"g_scr{k}", [T + 128, D], BF16, kind="Internal").ap()
             for k in range(NSEQ)]
    s_scr = nc.dram_tensor("s_scr", [NSEQ, 128, DK * T], BF16, kind="Internal").ap()
    tok_scr = [nc.dram_tensor(f"tok_scr{k}", [SLOTS + T + 128], U32, kind="Internal").ap()
               for k in range(NSEQ)]

    with ExitStack() as ctx:
        en = ctx.enter_context
        identf = en(nc.sbuf_tensor("identf", [128, 128], F32))
        iota8 = en(nc.sbuf_tensor("iota8", [128, 8], F32))
        rev8 = en(nc.sbuf_tensor("rev8", [128, 8], F32))
        ones8f = en(nc.sbuf_tensor("ones8f", [E, 1], F32))
        ones8b = en(nc.sbuf_tensor("ones8b", [E, 1], BF16))
        onesf = en(nc.sbuf_tensor("onesf", [1, 128], F32))
        wr_sb = en(nc.sbuf_tensor("wr_sb", [128, DK * E], F32))
        br_row = en(nc.sbuf_tensor("br_row", [1, E], F32))
        brbc = en(nc.sbuf_tensor("brbc", [128, E], F32))
        b1c = en(nc.sbuf_tensor("b1c", [128, E * HJ], F32))
        rb1c = en(nc.sbuf_tensor("rb1c", [128, E * HJ], BF16))
        bdc = en(nc.sbuf_tensor("bdc", [128, DK], F32))
        bic = en(nc.sbuf_tensor("bic", [128, DK], F32))
        ckc = en(nc.sbuf_tensor("ckc", [128, E * DK], F32))
        mrows = en(nc.sbuf_tensor("mrows", [E, T], BF16))
        stats = en(nc.sbuf_tensor("stats", [128, 16], F32))
        r_sb = en(nc.sbuf_tensor("r_sb", [128, 16], F32))
        eqt = en(nc.sbuf_tensor("eqt", [128, 16], F32))
        idxs = en(nc.sbuf_tensor("idxs", [128, 16], F32))
        mtok = en(nc.sbuf_tensor("mtok", [128, 16], F32))
        capoff = en(nc.sbuf_tensor("capoff", [E, 1], F32))
        scatf = en(nc.sbuf_tensor("scatf", [128, NG], F32))
        scatidx = en(nc.sbuf_tensor("scatidx", [128, NG], U32))
        tokids = en(nc.sbuf_tensor("tokids", [128, NG], U32))
        tokinit = en(nc.sbuf_tensor("tokinit", [128, 1], U32))
        idxcols = en(nc.sbuf_tensor("idxcols", [128, NSEQ * SLOTS // 128], U32))
        b2rows = en(nc.sbuf_tensor("b2rows", [E, D], BF16))
        c_rows = en(nc.sbuf_tensor("c_rows", [E, D], F32))
        rowst = en(nc.sbuf_tensor("rowst", [1, D], F32))
        wdt_sb = en(nc.sbuf_tensor("wdt_sb", [128, 2 * DK * D], BF16))
        xnw = en(nc.sbuf_tensor("xnw", [128, DK * W], BF16))
        AQ = DK * HS
        arena = en(nc.sbuf_tensor("arena", [128, 4 * AQ], BF16))
        w1b = [arena[:, 0 * AQ:1 * AQ], arena[:, 1 * AQ:2 * AQ]]
        w2b = [arena[:, 2 * AQ:3 * AQ], arena[:, 3 * AQ:4 * AQ]]
        dec_a = arena[:, 0:DK * T]
        inj_a = arena[:, 2 * AQ:2 * AQ + DK * T]
        h_big = en(nc.sbuf_tensor("h_big", [128, HJR * W], BF16))
        zrow = h_big[:, 0:D]
        out_exp = [en(nc.sbuf_tensor(f"out_exp{k}", [128, DK * CAP], F32)) for k in range(NSEQ)]
        xi_sl = [en(nc.sbuf_tensor(f"xi_sl{k}", [128, DK * CAP], BF16)) for k in range(NSEQ)]
        cwb = [xi_sl[0][:, 0:CHJR * D], xi_sl[1][:, 0:CHJR * D]]
        gstg = [en(nc.sbuf_tensor(f"gstg{k}", [128, D], F32)) for k in range(2)]
        gt_st = [en(nc.sbuf_tensor(f"gt_st{k}", [128, D], BF16)) for k in range(2)]
        scr = en(nc.sbuf_tensor("scr", [128, 4 * D], F32))
        xstage = [scr[:, 0:D], scr[:, D:2 * D]]
        xstage4 = [xstage[0], xstage[1], gstg[0][:], gstg[1][:]]
        sqv = scr[:, 2 * D:3 * D]
        xnTf = scr[:, 3 * D:4 * D]
        cums = scr[0:E, 0:T]
        slotrow = scr[0:1, 2 * D:2 * D + T]
        outg = [scr[:, 2 * D:3 * D], scr[:, 3 * D:4 * D]]
        cg_t0 = [xi_sl[0][:, 0:D], xi_sl[0][:, D:2 * D]]
        cs_t0 = [xi_sl[1][:, 0:D], xi_sl[1][:, D:2 * D]]
        pbuf = en(nc.sbuf_tensor("pbuf", [128, 256], F32))

        pb = [en(nc.psum_tensor(f"pb{k}", [128, 512], F32)) for k in range(8)]

        block = en(nc.Block())
        p = Prog(nc, ctx)
        cld = p.sem("cld")
        wtc = p.sem("wtc")
        crw = p.sem("crw")
        xl = [p.sem("xl0"), p.sem("xl1")]
        gw = [p.sem("gw0"), p.sem("gw1")]
        so = [p.sem("so0"), p.sem("so1")]
        xns = [p.sem("xns0"), p.sem("xns1")]
        ssp = [p.sem("ssp0"), p.sem("ssp1")]
        cgl = [p.sem("cgl0"), p.sem("cgl1")]
        csl = [p.sem("csl0"), p.sem("csl1")]
        ga = [p.sem("ga0"), p.sem("ga1"), p.sem("ga2"), p.sem("ga3")]
        scg = [p.sem("scg0"), p.sem("scg1")]
        idp = p.sem("idp")
        xdp = p.sem("xdp")
        b2l = p.sem("b2l")
        ids = [p.sem("ids0"), p.sem("ids1")]
        idr = [p.sem("idr0"), p.sem("idr1")]
        pe = p.sem("pe")
        sc = p.sem("sc")
        dv = p.sem("dv")
        gp = p.sem("gp")

        p.op("scalar", lambda s: s.dma_start(
            out=wr_sb[:].rearrange("p (dk e) -> p dk e", dk=DK),
            in_=wr_d.rearrange("(dk p) e -> p dk e", p=128)), inc=cld, k=16)
        p.op("scalar", lambda s: s.dma_start(out=br_row[:], in_=br_d[None, :]), inc=cld, k=16)
        p.op("gpsimd", lambda g: g.dma_start(out=b2rows[:], in_=b2_d), inc=b2l, k=16)
        b1st = out_exp[1][0:HJ, 0:E * 128]
        bdst = out_exp[0][0:DK, 0:128]
        bist = out_exp[0][0:DK, 128:256]
        for i in range(E):
            p.op("scalar", lambda s, i=i: s.dma_start(
                out=b1st[:, i * 128:(i + 1) * 128],
                in_=b1_d[i].rearrange("(hj p) -> hj p", p=128)), inc=cld, k=16)
        p.op("scalar", lambda s: s.dma_start(
            out=bdst, in_=bd_d.rearrange("(ek p) -> ek p", p=128)), inc=cld, k=16)
        p.op("scalar", lambda s: s.dma_start(
            out=bist, in_=bi_d.rearrange("(ek p) -> ek p", p=128)), inc=cld, k=16)
        t_const = cld.n

        p.op("gpsimd", lambda g: g.iota(iota8[:], pattern=[[1, 8]], base=0, channel_multiplier=0,
                                        allow_small_or_imprecise_dtypes=True))
        p.op("gpsimd", lambda g: g.iota(rev8[:], pattern=[[-1, 8]], base=8, channel_multiplier=0,
                                        allow_small_or_imprecise_dtypes=True))
        p.op("gpsimd", lambda g: g.memset(ones8f[:], 1.0))
        p.op("gpsimd", lambda g: g.memset(ones8b[:], 1.0))
        p.op("gpsimd", lambda g: g.memset(onesf[:], 1.0))
        p.op("gpsimd", lambda g: g.memset(zrow, 0.0))
        p.op("gpsimd", lambda g: g.memset(sqv, 0.0))
        p.op("gpsimd", lambda g: g.iota(capoff[:], pattern=[[1, 1]], base=-1, channel_multiplier=CAP,
                                        allow_small_or_imprecise_dtypes=True))
        p.op("gpsimd", lambda g: g.iota(tokids[:], pattern=[[128, NG]], base=0, channel_multiplier=1,
                                        allow_small_or_imprecise_dtypes=True))
        p.op("gpsimd", lambda g: g.iota(tokinit[:], pattern=[[1, 1]], base=T,
                                        channel_multiplier=1, allow_small_or_imprecise_dtypes=True))
        p.op("gpsimd", lambda g: g.iota(identf[:], pattern=[[-1, 128]], base=0, channel_multiplier=1,
                                        allow_small_or_imprecise_dtypes=True), inc=gp)
        t_gpinit = gp.n
        p.wait("vector", gp, t_gpinit)
        p.op("vector", lambda v: v.tensor_single_scalar(identf[:], identf[:], 0.0, op=ALU.is_equal), inc=dv)
        t_ident = dv.n

        p.wait("gpsimd", gp, t_gpinit)
        for h in range(NSEQ):
            p.op("gpsimd", lambda g, h=h: g.dma_start(
                out=xn_scr[h][T:T + 128, :], in_=sqv), inc=xdp, k=16)
        prefill_q = []
        for h in range(NSEQ):
            for c in range(SLOTS // 128):
                prefill_q.append(lambda g, h=h, c=c: g.dma_start(
                    out=tok_scr[h][c * 128:(c + 1) * 128][:, None], in_=tokinit[:, 0:1]))
            for g0 in range(NG):
                prefill_q.append(lambda g, h=h, g0=g0: g.dma_start(
                    out=g_scr[h][g0 * 128:(g0 + 1) * 128, :], in_=zrow))
        PPG = (len(prefill_q) + NG - 1) // NG
        t_prefill = None

        p.wait("tensor", cld, t_const)
        p.wait("tensor", dv, t_ident)
        for i in range(E):
            p.op("tensor", lambda te, i=i: te.transpose(
                pb[3][:, i * HJ:(i + 1) * HJ], b1st[:, i * 128:(i + 1) * 128],
                identf[0:HJ, 0:HJ]))
        p.op("tensor", lambda te: te.transpose(
            pb[3][:, E * HJ:E * HJ + DK], bdst, identf[0:DK, 0:DK]))
        p.op("tensor", lambda te: te.transpose(
            pb[3][:, E * HJ + DK:E * HJ + 2 * DK], bist, identf[0:DK, 0:DK]), inc=pe)
        t_b1tr = pe.n
        p.wait("vector", pe, t_b1tr)
        p.seq("vector", dv, lambda v: v.tensor_copy(b1c[:], pb[3][:, 0:E * HJ]))
        p.seq("vector", dv, lambda v: v.tensor_copy(bdc[:], pb[3][:, E * HJ:E * HJ + DK]))
        p.seq("vector", dv, lambda v: v.tensor_copy(bic[:], pb[3][:, E * HJ + DK:E * HJ + 2 * DK]))
        t_b1c = dv.n
        p.wait("scalar", dv, t_b1c)
        p.op("scalar", lambda s: s.activation(rb1c[:], b1c[:], AF.Relu), inc=sc)
        t_rb1 = sc.n

        p.wait("tensor", cld, t_const)
        p.wait("tensor", gp, t_gpinit)
        p.op("tensor", lambda te: te.matmul(pb[2][:, 0:E], lhsT=onesf[:], rhs=br_row[:],
                                            start=True, stop=True), inc=pe)
        t_brmm = pe.n
        p.wait("vector", pe, t_brmm)
        p.seq("vector", dv, lambda v: v.tensor_copy(brbc[:], pb[2][:, 0:E]))
        t_brbc = dv.n

        w_srcs = (wd_d, wi_d)
        t_wtld = {}
        for m in range(2):
            for ek in range(DK):
                ci = m * DK + ek
                p.wait("sync", pe, t_wtld.get(ci - 2, 0))
                p.op("sync", lambda s, m=m, ek=ek, ci=ci: s.dma_start(
                    out=xstage[ci % 2], in_=w_srcs[m][ek * 128:(ek + 1) * 128, :]),
                    inc=xl[ci % 2], k=16)
                t_ch = xl[ci % 2].n
                p.wait("tensor", xl[ci % 2], t_ch)
                p.wait("tensor", dv, dv.n)
                if ci == 0:
                    p.wait("tensor", dv, t_ident)
                for dk in range(DK):
                    p.op("tensor", lambda te, dk=dk, ci=ci: te.transpose(
                        pb[dk // 4][:, (dk % 4) * 128:(dk % 4 + 1) * 128],
                        xstage[ci % 2][:, dk * 128:(dk + 1) * 128], identf[:]),
                        inc=pe if dk == DK - 1 else None)
                t_wtld[ci] = pe.n
                p.wait("vector", pe, pe.n)
                p.wait("vector", dv, dv.n)
                for dk in range(DK):
                    p.op("vector", lambda v, dk=dk, m=m, ek=ek: v.tensor_copy(
                        wdt_sb[:, m * DK * D + dk * D + ek * 128:m * DK * D + dk * D + (ek + 1) * 128],
                        pb[dk // 4][:, (dk % 4) * 128:(dk % 4 + 1) * 128]),
                        inc=dv if dk == DK - 1 else None)
        t_wdt = dv.n

        def wdt_sl(m, dk, ek):
            o = m * DK * D + dk * D + ek * 128
            return wdt_sb[:, o:o + 128]

        cblk = 0
        t_cw_used = {}
        t_crow = 0
        t_idxr = {0: 0, 1: 0}
        t_spill = {0: 0, 1: 0}
        t_xn_ev = 0
        t_bank_sc = 0
        t_bank_dv = 0

        def emit_cpass_block():
            nonlocal cblk, t_crow
            if cblk >= NCB:
                return
            i, cb = divmod(cblk, HH // CHS)
            buf = cblk % 2
            p.wait("gpsimd", pe, t_cw_used.get(cblk - 2, 0))
            p.op("gpsimd", lambda g, i=i, cb=cb, buf=buf: g.dma_start(
                out=cwb[buf].rearrange("p (hjr d) -> p hjr d", hjr=CHJR),
                in_=w2_d[i, cb * CHS:(cb + 1) * CHS, :].rearrange("(hjr p) d -> p hjr d", p=128)),
                inc=gw[buf], k=16)
            t_w = gw[buf].n
            p.wait("tensor", gw[buf], t_w)
            if cblk == 0:
                p.wait("tensor", sc, t_rb1)
            if cb == 0:
                p.wait("tensor", dv, t_crow)
            for hjr in range(CHJR):
                hj = cb * CHJR + hjr
                for dh in range(DH):
                    p.op("tensor", lambda te, i=i, hj=hj, hjr=hjr, dh=dh, buf=buf: te.matmul(
                        pb[6 + dh][0:1, 0:DW],
                        lhsT=rb1c[:, i * HJ + hj:i * HJ + hj + 1],
                        rhs=cwb[buf][:, hjr * D + dh * 512:hjr * D + dh * 512 + DW],
                        start=(hj == 0), stop=(hj == HJ - 1)),
                        inc=pe if (hjr == CHJR - 1 and dh == DH - 1) else None)
            t_cw_used[cblk] = pe.n
            if cb == HH // CHS - 1:
                t_cmm = pe.n
                p.wait("vector", pe, t_cmm)
                p.wait("vector", crw, crw.n)
                p.wait("vector", dv, dv.n)
                for dh in range(DH):
                    p.op("vector", lambda v, dh=dh: v.tensor_copy(
                        rowst[:, dh * 512:dh * 512 + DW], pb[6 + dh][0:1, 0:DW]),
                        inc=dv if dh == DH - 1 else None)
                t_crow = dv.n
                p.wait("sync", dv, t_crow)
                p.op("sync", lambda s, i=i: s.dma_start(out=c_rows[i:i + 1, :], in_=rowst[:]),
                     inc=crw, k=16)
            cblk += 1

        tB = {}
        t_ln = {}
        t_ssmw_pe = 0
        t_ssm_sc = {0: 0, 1: 0}
        t_ssm_dv = {0: 0, 1: 0}
        t_idx_dv = 0
        t_idx_pe = 0
        for h in range(NSEQ):
            gi0 = h * NG
            p.wait("sync", xns[0], xns[0].n)
            p.wait("sync", pe, tB.get((gi0 - 4, 'tr'), 0))
            p.wait("sync", pe, t_wtld.get(2 * DK - 2, 0))
            p.wait("sync", pe, t_idx_pe)
            p.op("sync", lambda s, h=h: s.dma_start(
                out=xstage4[0], in_=x_d[h, 0:128, :]), inc=xl[0], k=16)
            xg_t = [xl[0].n]
            t_spl = {}

            def emit_LN(g, h=h):
                for _ in range(PPG):
                    if prefill_q:
                        p.op("gpsimd", prefill_q.pop(0), inc=idp, k=16)
                par = g % 2
                gi = h * NG + g
                st = stats[:, par * 8:par * 8 + 8]
                xg_g = xstage4[g % 4]
                sq_g = sqv if par == 0 else out_exp[0][:, D:2 * D]
                if g + 1 < NG:
                    p.wait("sync", xns[(g + 1) % 2], xns[(g + 1) % 2].n)
                    p.wait("sync", pe, tB.get((gi - 3, 'tr'), 0))
                    if gi == 0:
                        p.wait("sync", pe, t_wtld[2 * DK - 1])
                    p.op("sync", lambda s, g=g: s.dma_start(
                        out=xstage4[(g + 1) % 4], in_=x_d[h, (g + 1) * 128:(g + 2) * 128, :]),
                        inc=xl[(g + 1) % 2], k=16)
                    xg_t.append(xl[(g + 1) % 2].n)
                p.wait("vector", xl[par], xg_t[g])
                if g == 0 and h == 0:
                    p.wait("vector", xdp, 32)
                p.seq("vector", dv, lambda v: v.reduce_sum(st[:, 0:1], xg_g, axis=AX.X))
                p.seq("vector", dv, lambda v: v.tensor_mul(sq_g, xg_g, xg_g))
                p.seq("vector", dv, lambda v: v.reduce_sum(st[:, 1:2], sq_g, axis=AX.X))
                p.seq("vector", dv, lambda v: v.tensor_scalar_mul(st[:, 0:1], st[:, 0:1], 1.0 / D))
                p.seq("vector", dv, lambda v: v.tensor_mul(st[:, 7:8], st[:, 0:1], st[:, 0:1]))
                p.seq("vector", dv, lambda v: v.tensor_scalar(
                    st[:, 2:3], st[:, 1:2], 1.0 / D, None, op0=ALU.mult))
                p.seq("vector", dv, lambda v: v.tensor_tensor(
                    st[:, 2:3], st[:, 2:3], st[:, 7:8], op=ALU.subtract))
                p.seq("vector", dv, lambda v: v.tensor_scalar(
                    st[:, 2:3], st[:, 2:3], EPS, None, op0=ALU.add))
                t_var = dv.n
                p.wait("scalar", dv, t_var)
                p.seq("scalar", sc, lambda s: s.activation(st[:, 3:4], st[:, 2:3], AF.Sqrt))
                t_sd = sc.n
                p.wait("vector", sc, t_sd)
                p.seq("vector", dv, lambda v: v.reciprocal(st[:, 4:5], st[:, 3:4]))
                p.seq("vector", dv, lambda v: v.tensor_scalar(
                    xg_g, xg_g, st[:, 0:1], st[:, 4:5],
                    op0=ALU.subtract, op1=ALU.mult))
                t_xnst = dv.n
                t_ln[gi] = t_xnst
                p.wait("sync", dv, t_xnst)
                p.op("sync", lambda s, g=g, xg_g=xg_g: s.dma_start(
                    out=xn_scr[h][g * 128:(g + 1) * 128, :], in_=xg_g), inc=xns[par], k=16)
                t_spl[g] = xns[par].n

            def emit_Btr(g, h=h):
                par = g % 2
                gi = h * NG + g
                pair = (pb[0], pb[1]) if par == 0 else (pb[2], pb[3])
                xnTf_g = xnTf if par == 0 else out_exp[0][:, 0:D]
                p.wait("tensor", dv, t_ln[gi])
                p.wait("tensor", dv, tB.get((gi - 2, 'evac'), 0))
                if par == 1:
                    p.wait("tensor", sc, t_ssm_sc[1])
                    p.wait("tensor", dv, t_ssm_dv[1])
                if h == 0 and g < 2:
                    p.wait("tensor", dv, t_wdt)
                for dk in range(DK):
                    p.op("tensor", lambda te, dk=dk, g=g: te.transpose(
                        pair[dk // 4][:, (dk % 4) * 128:(dk % 4 + 1) * 128],
                        xstage4[g % 4][:, dk * 128:(dk + 1) * 128], identf[:]),
                        inc=pe if dk == DK - 1 else None)
                tB[(gi, 'tr')] = pe.n
                p.wait("vector", pe, tB[(gi, 'tr')])
                if g % GPW == 0:
                    p.wait("vector", pe, t_ssmw_pe)
                p.wait("vector", dv, dv.n)
                gw_ = g % GPW
                for dk in range(DK):
                    p.op("vector", lambda v, dk=dk, xnTf_g=xnTf_g, pair=pair: v.tensor_copy(
                        xnTf_g[:, dk * 128:(dk + 1) * 128],
                        pair[dk // 4][:, (dk % 4) * 128:(dk % 4 + 1) * 128]),
                        inc=dv if dk == DK - 1 else None)
                tB[(gi, 'evT')] = dv.n
                for dk in range(DK):
                    p.op("vector", lambda v, dk=dk, gw_=gw_, pair=pair: v.tensor_copy(
                        xnw[:, dk * W + gw_ * 128:dk * W + (gw_ + 1) * 128],
                        pair[dk // 4][:, (dk % 4) * 128:(dk % 4 + 1) * 128]),
                        inc=dv if dk == DK - 1 else None)
                tB[(gi, 'evac')] = dv.n

            def emit_Bmask(g, h=h):
                par = g % 2
                gi = h * NG + g
                rtb = pb[5]
                xnTf_g = xnTf if par == 0 else out_exp[0][:, 0:D]
                st = stats[:, par * 8:par * 8 + 8]
                p.wait("tensor", dv, tB[(gi, 'evT')])
                p.wait("tensor", dv, tB.get((gi - 1, 'radd'), 0))
                p.wait("tensor", dv, t_ssm_dv[0])
                for dk in range(DK):
                    p.op("tensor", lambda te, dk=dk, xnTf_g=xnTf_g, rtb=rtb: te.matmul(
                        rtb[:, 0:E], lhsT=xnTf_g[:, dk * 128:(dk + 1) * 128],
                        rhs=wr_sb[:, dk * E:(dk + 1) * E],
                        start=(dk == 0), stop=(dk == DK - 1)),
                        inc=pe if dk == DK - 1 else None)
                t_rmm = pe.n
                p.wait("vector", pe, t_rmm)
                if g == 0 and h == 0:
                    p.wait("vector", dv, t_brbc)
                rsb = r_sb[:, par * 8:par * 8 + 8]
                eq = eqt[:, par * 8:par * 8 + 8]
                ix = idxs[:, par * 8:par * 8 + 8]
                mt = mtok[:, par * 8:par * 8 + 8]
                p.seq("vector", dv, lambda v, rtb=rtb: v.tensor_add(rsb[:, 0:E], rtb[:, 0:E], brbc[:]))
                tB[(gi, 'radd')] = dv.n
                p.seq("vector", dv, lambda v: v.reduce_max(st[:, 5:6], rsb[:, 0:E], axis=AX.X))
                p.seq("vector", dv, lambda v: v.tensor_scalar(
                    eq[:, 0:E], rsb[:, 0:E], st[:, 5:6], None, op0=ALU.is_equal))
                p.seq("vector", dv, lambda v: v.tensor_mul(ix[:, 0:E], eq[:, 0:E], rev8[:, 0:E]))
                p.seq("vector", dv, lambda v: v.tensor_scalar(
                    ix[:, 0:E], ix[:, 0:E], -1.0, 8.0, op0=ALU.mult, op1=ALU.add))
                p.seq("vector", dv, lambda v: v.tensor_reduce(
                    st[:, 6:7], ix[:, 0:E], axis=AX.X, op=ALU.min))
                p.seq("vector", dv, lambda v: v.tensor_scalar(
                    mt[:, 0:E], iota8[:, 0:E], st[:, 6:7], None, op0=ALU.is_equal))
                t_mtok = dv.n
                p.wait("tensor", dv, t_mtok)
                p.wait("tensor", dv, tB.get((gi - 1, 'mrows'), 0))
                p.op("tensor", lambda te, mt=mt, rtb=rtb: te.transpose(
                    rtb[0:E, 8:136], mt[:, 0:E], identf[:]), inc=pe)
                t_mtr = pe.n
                p.wait("vector", pe, t_mtr)
                p.seq("vector", dv, lambda v, g=g, rtb=rtb: v.tensor_copy(
                    mrows[:, g * 128:(g + 1) * 128], rtb[0:E, 8:136]))
                tB[(gi, 'mrows')] = dv.n

            def emit_SSMW(w_, h=h):
                nonlocal t_ssmw_pe
                p.wait("tensor", dv, tB[(h * NG + w_ * GPW + GPW - 1, 'evac')])
                p.wait("tensor", dv, t_wdt)
                if w_ == 0 and h > 0:
                    p.wait("tensor", ssp[h - 1], 16)
                    p.wait("tensor", dv, t_idx_dv)
                for m in range(2):
                    banks = (pb[4], pb[2]) if m == 0 else (pb[5], pb[3])
                    if m == 1:
                        p.wait("tensor", dv, tB[(h * NG + w_ * GPW + GPW - 1, 'mrows')])
                    for ek in range(DK):
                        bank = banks[ek % 2]
                        if m == 0:
                            p.wait("tensor", sc, t_ssm_sc[ek % 2])
                        else:
                            p.wait("tensor", dv, t_ssm_dv[ek % 2])
                        for dk in range(DK):
                            p.op("tensor", lambda te, m=m, dk=dk, ek=ek, bank=bank: te.matmul(
                                bank[:, 0:W], lhsT=wdt_sl(m, dk, ek),
                                rhs=xnw[:, dk * W:(dk + 1) * W],
                                start=(dk == 0), stop=(dk == DK - 1)),
                                inc=pe if dk == DK - 1 else None)
                        t_mm = pe.n
                        if m == 0:
                            p.wait("scalar", pe, t_mm)
                            p.wait("scalar", sc, sc.n)
                            p.op("scalar", lambda s, ek=ek, w_=w_, bank=bank: s.activation(
                                dec_a[:, ek * T + w_ * W:ek * T + (w_ + 1) * W], bank[:, 0:W],
                                AF.Sigmoid, bias=bdc[:, ek:ek + 1]), inc=sc)
                            t_ssm_sc[ek % 2] = sc.n
                        else:
                            p.wait("vector", pe, t_mm)
                            p.wait("vector", dv, dv.n)
                            p.op("vector", lambda v, ek=ek, w_=w_, bank=bank: v.tensor_scalar(
                                inj_a[:, ek * T + w_ * W:ek * T + (w_ + 1) * W], bank[:, 0:W],
                                bic[:, ek:ek + 1], None, op0=ALU.add), inc=dv)
                            t_ssm_dv[ek % 2] = dv.n
                t_ssmw_pe = pe.n

            LAG = 2
            for it in range(NG + LAG + 1 if SL >= 1 else 0):
                if it < NG:
                    emit_LN(it)
                gm = it - LAG - 1
                if 0 <= gm < NG:
                    emit_Bmask(gm)
                    if gm % GPW == GPW - 1:
                        emit_SSMW(gm // GPW)
                if LAG <= it < NG + LAG:
                    emit_Btr(it - LAG)
                if 0 <= gm < NG:
                    for _ in range(CPG):
                        emit_cpass_block()
            t_A0 = dv.n
            t_ssm_all_sc = sc.n

            if SL >= 2:
                p.wait("vector", dv, t_A0)
                p.wait("vector", pe, pe.n)
                p.wait("vector", xns[0], xns[0].n)
                p.wait("vector", xns[1], xns[1].n)
                p.seq("vector", dv, lambda v: v.tensor_tensor_scan(
                    cums, mrows[:], mrows[:], 0.0, op0=ALU.add, op1=ALU.max))
                p.seq("vector", dv, lambda v: v.tensor_scalar(
                    cums, cums, capoff[:, 0:1], None, op0=ALU.add))
                p.seq("vector", dv, lambda v: v.tensor_mul(cums, cums, mrows[:]))
                t_slots = dv.n
                p.wait("tensor", dv, t_slots)
                p.wait("tensor", sc, t_ssm_all_sc)
                for n in range(T // 512):
                    p.op("tensor", lambda te, n=n: te.matmul(
                        pb[4][0:1, 0:512], lhsT=ones8f[:], rhs=cums[:, n * 512:(n + 1) * 512],
                        start=True, stop=True), inc=pe)
                    t_sr = pe.n
                    p.wait("vector", pe, t_sr)
                    p.seq("vector", dv, lambda v, n=n: v.tensor_copy(
                        slotrow[:, n * 512:(n + 1) * 512], pb[4][0:1, 0:512]))
                    p.wait("tensor", dv, dv.n)
                t_srow = dv.n
                p.wait("tensor", dv, t_srow)
                for g in range(NG):
                    p.op("tensor", lambda te, g=g: te.transpose(
                        pb[4][:, g:g + 1], slotrow[0:1, g * 128:(g + 1) * 128], identf[0:1, 0:1]),
                        inc=pe if g == NG - 1 else None)
                t_sct = pe.n
                t_idx_pe = t_sct
                p.wait("vector", pe, t_sct)
                p.wait("vector", ids[0], ids[0].n)
                p.wait("vector", ids[1], ids[1].n)
                p.seq("vector", dv, lambda v: v.tensor_copy(scatf[:], pb[4][:, 0:NG]))
                p.seq("vector", dv, lambda v: v.tensor_copy(scatidx[:], scatf[:]))
                t_sidx = dv.n
                t_idx_dv = t_sidx
                p.wait("gpsimd", idp, idp.n)
                p.wait("gpsimd", dv, t_sidx)
                for g in range(NG):
                    p.op("gpsimd", lambda gg, h=h, g=g: gg.indirect_dma_start(
                        out=tok_scr[h][:][:, None],
                        out_offset=IndirectOffsetOnAxis(ap=scatidx[:, g:g + 1], axis=0),
                        in_=tokids[:, g:g + 1], in_offset=None), inc=ids[h], k=16)
                t_scat = ids[h].n
                p.wait("gpsimd", ids[h], t_scat)
                for c in range(SLOTS // 128):
                    p.op("gpsimd", lambda s, h=h, c=c: s.dma_start(
                        out=idxcols[:, h * (SLOTS // 128) + c:h * (SLOTS // 128) + c + 1],
                        in_=tok_scr[h][c * 128:(c + 1) * 128][:, None]), inc=idr[h], k=16)
                t_idxr[h] = idr[h].n

                p.wait("vector", sc, t_ssm_all_sc)
                for ek in range(DK):
                    p.seq("vector", dv, lambda v, ek=ek: v.tensor_tensor_scan(
                        dec_a[:, ek * T:(ek + 1) * T], dec_a[:, ek * T:(ek + 1) * T],
                        inj_a[:, ek * T:(ek + 1) * T], 0.0, op0=ALU.mult, op1=ALU.add))
                t_scan = dv.n
                p.wait("gpsimd", dv, t_scan)
                p.op("gpsimd", lambda s, h=h: s.dma_start(
                    out=s_scr[h], in_=dec_a), inc=ssp[h], k=16)
                t_spill[h] = dv.n

        t_ck = 0
        if SL >= 2:
            p.wait("tensor", crw, crw.n)
            p.wait("tensor", b2l, b2l.n)
            sum8c = stats[:, 8:9]
            for dh in range(DH):
                p.wait("tensor", dv, dv.n)
                p.op("tensor", lambda te, dh=dh: te.matmul(
                    pb[6][0:1, 0:DW], lhsT=ones8f[:], rhs=c_rows[:, dh * 512:dh * 512 + DW],
                    start=True, stop=False))
                p.op("tensor", lambda te, dh=dh: te.matmul(
                    pb[6][0:1, 0:DW], lhsT=ones8b[:], rhs=b2rows[:, dh * 512:dh * 512 + DW],
                    start=False, stop=True), inc=pe)
                t_s8 = pe.n
                p.wait("vector", pe, t_s8)
                p.seq("vector", dv, lambda v, dh=dh: v.tensor_copy(
                    rowst[:, dh * 512:dh * 512 + DW], pb[6][0:1, 0:DW]))
            for dk in range(DK):
                p.wait("tensor", dv, dv.n)
                p.op("tensor", lambda te, dk=dk: te.transpose(
                    pb[7][:, 0:E], c_rows[0:E, dk * 128:(dk + 1) * 128], identf[0:E, 0:E]))
                p.op("tensor", lambda te, dk=dk: te.transpose(
                    pb[7][:, E:E + 1], rowst[0:1, dk * 128:(dk + 1) * 128], identf[0:1, 0:1]), inc=pe)
                t_st8 = pe.n
                p.wait("vector", pe, t_st8)
                p.seq("vector", dv, lambda v: v.tensor_copy(sum8c, pb[7][:, E:E + 1]))
                p.seq("vector", dv, lambda v, dk=dk: v.tensor_scalar(
                    ckc[:, dk:dk + (E - 1) * DK + 1:DK],
                    pb[7][:, 0:E], sum8c, -1.0, op0=ALU.subtract, op1=ALU.mult))
            t_ck = dv.n

        if SL >= 2:
            p.wait("gpsimd", dv, max(t_spill[0], t_spill[NSEQ - 1]))
            p.wait("gpsimd", ssp[0], 16)
            p.wait("gpsimd", ssp[NSEQ - 1], 16)
            p.wait("gpsimd", pe, pe.n)
            p.op("gpsimd", lambda g: g.dma_start(
                out=w1b[0].rearrange("p (dk q) -> p dk q", dk=DK),
                in_=w1_d[0, :, 0:HS].rearrange("(dk p) q -> p dk q", p=128)), inc=gw[0], k=16)
            p.op("gpsimd", lambda g: g.dma_start(
                out=w2b[0].rearrange("p (hjr d) -> p hjr d", hjr=HJR),
                in_=w2_d[0, 0:HS, :].rearrange("(hjr p) d -> p hjr d", p=128)), inc=gw[0], k=16)
            wblk_t = {0: (0, gw[0].n)}

            PH = ([4, 5], [0, 1])
            AB = ([0, 1, 2, 3], [4, 5, 6, 7])
            t_evac_a = 0
            t_evac_b = 0
            t_blk_lastmm = {}
            t_gstg_pe = {0: 0, 1: 0, 2: 0, 3: 0}
            t_scg = {0: 0, 1: 0}
            t_xi_all = {}
            t_sctr_last = 0
            t_gev_last = 0
            it = 0
            gstg4f = [gstg[0][:], gstg[1][:], xstage[0], xstage[1]]
            t_trpair = {0: 0, 1: 0}

            def emit_gathers(i):
                t_trpair[0] = max(t_trpair[0], dv.n)
                t_trpair[1] = max(t_trpair[1], dv.n)
                for hw in range(NSEQ):
                    for gg in range(NCG):
                        idx = hw * NCG + gg
                        gb = idx % 4
                        pr = idx % 2
                        base = 2 - 2 * pr
                        p.wait("gpsimd", idr[hw], t_idxr[hw])
                        p.wait("gpsimd", xns[0], xns[0].n)
                        p.wait("gpsimd", xns[1], xns[1].n)
                        p.wait("gpsimd", pe, t_gstg_pe.get(gb, 0))
                        p.op("gpsimd", lambda g2, i=i, hw=hw, gg=gg, gb=gb: g2.indirect_dma_start(
                            out=gstg4f[gb], out_offset=None,
                            in_=xn_scr[hw], in_offset=IndirectOffsetOnAxis(
                                ap=idxcols[:, hw * (SLOTS // 128) + i * NCG + gg:
                                           hw * (SLOTS // 128) + i * NCG + gg + 1], axis=0)),
                            inc=ga[gb], k=16)
                        t_g = ga[gb].n
                        p.wait("tensor", ga[gb], t_g)
                        p.wait("tensor", dv, t_trpair[pr])
                        for dk in range(DK):
                            p.op("tensor", lambda te, dk=dk, gb=gb, base=base: te.transpose(
                                pb[base + dk // 4][:, (dk % 4) * 128:(dk % 4 + 1) * 128],
                                gstg4f[gb][:, dk * 128:(dk + 1) * 128], identf[:]),
                                inc=pe if dk == DK - 1 else None)
                        t_xtr = pe.n
                        t_gstg_pe[gb] = t_xtr
                        p.wait("vector", pe, t_xtr)
                        p.wait("vector", dv, dv.n)
                        for dk in range(DK):
                            p.op("vector", lambda v, dk=dk, hw=hw, gg=gg, base=base: v.tensor_copy(
                                xi_sl[hw][:, dk * CAP + gg * 128:dk * CAP + (gg + 1) * 128],
                                pb[base + dk // 4][:, (dk % 4) * 128:(dk % 4 + 1) * 128]),
                                inc=dv if dk == DK - 1 else None)
                        t_trpair[pr] = dv.n
                    t_xi_all[(i, hw)] = dv.n

            emit_gathers(0)
            for i in range(E):
                for hs in range(NHS):
                    blk = i * NHS + hs
                    buf = blk % 2
                    nblk_i, nblk_hs = (i, hs + 1) if hs + 1 < NHS else (i + 1, 0)
                    if nblk_i < E:
                        nbuf = (blk + 1) % 2
                        p.wait("gpsimd", pe, t_blk_lastmm.get(blk - 1, 0))
                        p.op("gpsimd", lambda g, i2=nblk_i, hs2=nblk_hs, nbuf=nbuf: g.dma_start(
                            out=w1b[nbuf].rearrange("p (dk q) -> p dk q", dk=DK),
                            in_=w1_d[i2, :, hs2 * HS:(hs2 + 1) * HS].rearrange("(dk p) q -> p dk q", p=128)),
                            inc=gw[nbuf], k=16)
                        p.op("gpsimd", lambda g, i2=nblk_i, hs2=nblk_hs, nbuf=nbuf: g.dma_start(
                            out=w2b[nbuf].rearrange("p (hjr d) -> p hjr d", hjr=HJR),
                            in_=w2_d[i2, hs2 * HS:(hs2 + 1) * HS, :].rearrange("(hjr p) d -> p hjr d", p=128)),
                            inc=gw[nbuf], k=16)
                        wblk_t[blk + 1] = (nbuf, gw[nbuf].n)

                    for hw in range(NSEQ):
                        par = it % 2
                        ph = PH[par]
                        ab = AB[par]
                        bb = AB[1 - par]
                        wbuf_s, wbuf_t = wblk_t[blk]
                        p.wait("tensor", gw[wbuf_s], wbuf_t)
                        p.wait("tensor", dv, t_xi_all[(i, hw)])
                        t_relu = {}
                        for hjr in range(HJR):
                            bank = ph[hjr % 2]
                            if hjr < 2:
                                p.wait("tensor", dv, t_evac_a)
                            else:
                                p.wait("tensor", sc, t_relu[hjr - 2])
                            for dk in range(DK):
                                p.op("tensor", lambda te, dk=dk, hjr=hjr, buf=buf, hw=hw, bank=bank: te.matmul(
                                    pb[bank][:, 0:W],
                                    lhsT=w1b[buf][:, dk * HS + hjr * 128:dk * HS + (hjr + 1) * 128],
                                    rhs=xi_sl[hw][:, dk * CAP:dk * CAP + W],
                                    start=(dk == 0), stop=(dk == DK - 1)),
                                    inc=pe if dk == DK - 1 else None)
                            t_mm1 = pe.n
                            hj_abs = hs * HJR + hjr
                            p.wait("scalar", pe, t_mm1)
                            p.wait("scalar", sc, sc.n)
                            p.op("scalar", lambda s, hjr=hjr, i=i, hj_abs=hj_abs, bank=bank: s.activation(
                                h_big[:, hjr * W:(hjr + 1) * W], pb[bank][:, 0:W], AF.Relu,
                                bias=b1c[:, i * HJ + hj_abs:i * HJ + hj_abs + 1]), inc=sc)
                            t_relu[hjr] = sc.n
                        p.wait("tensor", dv, t_evac_b)
                        if hs == 0 and hw == 0:
                            p.wait("tensor", dv, t_gev_last)
                        for hjr in range(HJR):
                            p.wait("tensor", sc, t_relu[hjr])
                            for dk in range(DK // 2):
                                p.op("tensor", lambda te, dk=dk, hjr=hjr, buf=buf, ab=ab: te.matmul(
                                    pb[ab[dk]][:, 0:W],
                                    lhsT=w2b[buf][:, hjr * D + dk * 128:hjr * D + (dk + 1) * 128],
                                    rhs=h_big[:, hjr * W:(hjr + 1) * W],
                                    start=(hjr == 0), stop=(hjr == HJR - 1)),
                                    inc=pe if (hjr == HJR - 1 and dk == DK // 2 - 1) else None)
                        t_mm2a = pe.n
                        for hjr in range(HJR):
                            for dk in range(DK // 2, DK):
                                p.op("tensor", lambda te, dk=dk, hjr=hjr, buf=buf, bb=bb: te.matmul(
                                    pb[bb[dk - DK // 2]][:, 0:W],
                                    lhsT=w2b[buf][:, hjr * D + dk * 128:hjr * D + (dk + 1) * 128],
                                    rhs=h_big[:, hjr * W:(hjr + 1) * W],
                                    start=(hjr == 0), stop=(hjr == HJR - 1)),
                                    inc=pe if (hjr == HJR - 1 and dk == DK - 1) else None)
                        t_mm2b = pe.n
                        t_blk_lastmm[blk] = t_mm2b

                        def emit_evac(banks, dk0, t_mm, hs=hs, hw=hw):
                            p.wait("vector", pe, t_mm)
                            p.wait("vector", dv, dv.n)
                            for j in range(DK // 2):
                                dk = dk0 + j
                                o0 = dk * CAP
                                if hs == 0:
                                    p.op("vector", lambda v, j=j, o0=o0, banks=banks, hw=hw: v.tensor_copy(
                                        out_exp[hw][:, o0:o0 + W], pb[banks[j]][:, 0:W]),
                                        inc=dv if j == DK // 2 - 1 else None)
                                else:
                                    p.op("vector", lambda v, j=j, o0=o0, banks=banks, hw=hw: v.tensor_add(
                                        out_exp[hw][:, o0:o0 + W], pb[banks[j]][:, 0:W],
                                        out_exp[hw][:, o0:o0 + W]),
                                        inc=dv if j == DK // 2 - 1 else None)
                            return dv.n

                        t_evac_a = emit_evac(ab, 0, t_mm2a)
                        t_evac_b = emit_evac(bb, DK // 2, t_mm2b)
                        it += 1

                if i + 1 < E:
                    emit_gathers(i + 1)
                p.wait("scalar", dv, t_evac_b)
                p.wait("scalar", dv, t_ck)
                p.wait("scalar", sc, sc.n)
                t_sil = {}
                for hw in range(NSEQ):
                    for dk in range(DK):
                        p.op("scalar", lambda s, dk=dk, i=i, hw=hw: s.activation(
                            out_exp[hw][:, dk * CAP:(dk + 1) * CAP],
                            out_exp[hw][:, dk * CAP:(dk + 1) * CAP],
                            AF.Silu, bias=ckc[:, i * DK + dk:i * DK + dk + 1]),
                            inc=sc if dk == DK - 1 else None)
                    t_sil[hw] = sc.n
                for hw in range(NSEQ):
                    for gg in range(NCG):
                        idx = hw * NCG + gg
                        gb = idx % 2
                        pr = idx % 2
                        base = 2 - 2 * pr
                        p.wait("tensor", sc, t_sil[hw])
                        p.wait("tensor", dv, t_trpair[pr])
                        for dk in range(DK):
                            p.op("tensor", lambda te, dk=dk, gg=gg, hw=hw, base=base: te.transpose(
                                pb[base + dk // 4][:, (dk % 4) * 128:(dk % 4 + 1) * 128],
                                out_exp[hw][:, dk * CAP + gg * 128:dk * CAP + (gg + 1) * 128],
                                identf[:]),
                                inc=pe if dk == DK - 1 else None)
                        t_gtr = pe.n
                        t_sctr_last = t_gtr
                        p.wait("vector", pe, t_gtr)
                        p.wait("vector", dv, dv.n)
                        p.wait("vector", scg[gb], t_scg[gb])
                        for dk in range(DK):
                            p.op("vector", lambda v, dk=dk, gb=gb, base=base: v.tensor_copy(
                                gt_st[gb][:, dk * 128:(dk + 1) * 128],
                                pb[base + dk // 4][:, (dk % 4) * 128:(dk % 4 + 1) * 128]),
                                inc=dv if dk == DK - 1 else None)
                        t_trpair[pr] = dv.n
                        t_gev = dv.n
                        t_gev_last = t_gev
                        p.wait("gpsimd", dv, t_gev)
                        p.op("gpsimd", lambda g2, i=i, hw=hw, gg=gg, gb=gb: g2.indirect_dma_start(
                            out=g_scr[hw], out_offset=IndirectOffsetOnAxis(
                                ap=idxcols[:, hw * (SLOTS // 128) + i * NCG + gg:
                                           hw * (SLOTS // 128) + i * NCG + gg + 1], axis=0),
                            in_=gt_st[gb][:], in_offset=None), inc=scg[gb], k=16)
                        t_scg[gb] = scg[gb].n

        for h in range(NSEQ if SL >= 5 else 0):
            xg_t2 = []
            cg_t = []
            cs_t = []
            s_fm = s_scr[h].rearrange("p (dk t) -> p dk t", dk=DK)
            p.wait("sync", scg[0], scg[0].n)
            p.wait("sync", scg[1], scg[1].n)
            p.wait("sync", ssp[h], 16)
            p.wait("sync", pe, pe.n)
            p.wait("sync", dv, dv.n)
            p.op("sync", lambda s, h=h: s.dma_start(out=xstage[0], in_=x_d[h, 0:128, :]), inc=xl[0], k=16)
            xg_t2.append(xl[0].n)
            p.op("sync", lambda s, h=h: s.dma_start(out=cg_t0[0], in_=g_scr[h][0:128, :]), inc=cgl[0], k=16)
            cg_t.append(cgl[0].n)
            p.op("sync", lambda s, h=h, s_fm=s_fm: s.dma_start(
                out=cs_t0[0].rearrange("p (dk t) -> p dk t", dk=DK),
                in_=s_fm[:, :, 0:128]), inc=csl[0], k=16)
            cs_t.append(csl[0].n)
            t_pstore = {}
            t_pmul = {}
            so0_base = {0: so[0].n, 1: so[1].n}
            for g in range(NG):
                gb = g % 2
                pair = (pb[0], pb[1]) if gb == 0 else (pb[2], pb[3])
                if g + 1 < NG:
                    p.wait("sync", dv, dv.n)
                    p.op("sync", lambda s, h=h, g=g: s.dma_start(
                        out=xstage[(g + 1) % 2], in_=x_d[h, (g + 1) * 128:(g + 2) * 128, :]),
                        inc=xl[(g + 1) % 2], k=16)
                    xg_t2.append(xl[(g + 1) % 2].n)
                    p.op("sync", lambda s, h=h, g=g: s.dma_start(
                        out=cg_t0[(g + 1) % 2], in_=g_scr[h][(g + 1) * 128:(g + 2) * 128, :]),
                        inc=cgl[(g + 1) % 2], k=16)
                    cg_t.append(cgl[(g + 1) % 2].n)
                    p.op("sync", lambda s, h=h, g=g, s_fm=s_fm: s.dma_start(
                        out=cs_t0[(g + 1) % 2].rearrange("p (dk t) -> p dk t", dk=DK),
                        in_=s_fm[:, :, (g + 1) * 128:(g + 2) * 128]), inc=csl[(g + 1) % 2], k=16)
                    cs_t.append(csl[(g + 1) % 2].n)
                p.wait("vector", csl[gb], cs_t[g])
                p.wait("vector", dv, dv.n)
                p.op("vector", lambda v, gb=gb: v.tensor_copy(gstg[gb][:], cs_t0[gb]), inc=dv)
                t_sc32 = dv.n
                p.wait("tensor", dv, t_sc32)
                p.wait("tensor", dv, t_pmul.get(g - 2, 0))
                for dk in range(DK):
                    p.op("tensor", lambda te, dk=dk, gb=gb: te.transpose(
                        pair[dk // 4][:, (dk % 4) * 128:(dk % 4 + 1) * 128],
                        gstg[gb][:, dk * 128:(dk + 1) * 128], identf[:]),
                        inc=pe if dk == DK - 1 else None)
                t_str = pe.n
                p.wait("vector", pe, t_str)
                p.wait("vector", cgl[gb], cg_t[g])
                p.wait("vector", xl[gb], xg_t2[g])
                p.wait("vector", so[gb], t_pstore.get(g - 2, so0_base[gb]))
                p.wait("vector", dv, dv.n)
                for dk in range(DK):
                    p.op("vector", lambda v, dk=dk, gb=gb: v.tensor_mul(
                        cg_t0[gb][:, dk * 128:(dk + 1) * 128],
                        pair[dk // 4][:, (dk % 4) * 128:(dk % 4 + 1) * 128],
                        cg_t0[gb][:, dk * 128:(dk + 1) * 128]),
                        inc=dv if dk == DK - 1 else None)
                t_pmul[g] = dv.n
                ob = outg[gb]
                p.wait("vector", dv, t_pmul[g])
                p.op("vector", lambda v, gb=gb, ob=ob: v.tensor_add(
                    ob, cg_t0[gb], xstage[gb]), inc=dv)
                t_add = dv.n
                p.wait("scalar", dv, t_add)
                p.op("scalar", lambda s, h=h, g=g, ob=ob: s.dma_start(
                    out=out_d[h, g * 128:(g + 1) * 128, :], in_=ob), inc=so[gb], k=16)
                t_pstore[g] = so[gb].n

        p.wait("sync", so[0], so[0].n)
        p.wait("sync", so[1], so[1].n)

        p.install(block)
    return nc


def kernel(L0_attn_x_seq, L0_attn_W_decay, L0_attn_b_decay, L0_attn_W_input,
           L0_attn_b_input, L0_ffn_W1, L0_ffn_b1, L0_ffn_W2, L0_ffn_b2,
           L0_ffn_Wr, L0_ffn_br):
    x = np.ascontiguousarray(np.asarray(L0_attn_x_seq, dtype=np.float32))
    wd = np.ascontiguousarray(np.asarray(L0_attn_W_decay, dtype=np.float32))
    bd = np.ascontiguousarray(np.asarray(L0_attn_b_decay, dtype=np.float32))
    wi = np.ascontiguousarray(np.asarray(L0_attn_W_input, dtype=np.float32))
    bi = np.ascontiguousarray(np.asarray(L0_attn_b_input, dtype=np.float32))
    w1 = np.ascontiguousarray(np.asarray(L0_ffn_W1, dtype=np.float32))
    b1 = np.ascontiguousarray(np.asarray(L0_ffn_b1, dtype=np.float32))
    w2 = np.ascontiguousarray(np.asarray(L0_ffn_W2, dtype=np.float32))
    b2 = np.ascontiguousarray(np.asarray(L0_ffn_b2, dtype=np.float32))
    wr = np.ascontiguousarray(np.asarray(L0_ffn_Wr, dtype=np.float32))
    br = np.ascontiguousarray(np.asarray(L0_ffn_br, dtype=np.float32))

    n_cores = 8
    nc = build()
    in_maps = []
    for c in range(n_cores):
        in_maps.append(dict(
            x=x[2 * c:2 * c + 2], wd=wd, bd=bd, wi=wi, bi=bi,
            w1=w1, b1=b1, w2=w2, b2=b2, wr=wr, br=br))
    res = run_bass_kernel_spmd(nc, in_maps, core_ids=list(range(n_cores)))
    out = np.concatenate(
        [np.asarray(r["out"], dtype=np.float32).reshape(2, 2048, 1024) for r in res.results],
        axis=0)
    return out


# version 19
# speedup vs baseline: 1.2488x; 1.0147x over previous
import os
from contextlib import ExitStack

import numpy as np
import concourse.bass as bass
import concourse.mybir as mybir
from concourse.bass import IndirectOffsetOnAxis
from concourse.bass_utils import run_bass_kernel_spmd

F32 = mybir.dt.float32
BF16 = mybir.dt.bfloat16
U32 = mybir.dt.uint32
AF = mybir.ActivationFunctionType
ALU = mybir.AluOpType
AX = mybir.AxisListType

ENGINES = ("sync", "scalar", "vector", "tensor", "gpsimd")
EPS = 1e-5


class Sem:
    def __init__(self, nc, ctx, name):
        self.h = ctx.enter_context(nc.semaphore(name))
        self.n = 0

    def inc(self, k=1):
        self.n += k
        return self.n


class Prog:
    def __init__(self, nc, ctx):
        self.nc = nc
        self.ctx = ctx
        self.q = {e: [] for e in ENGINES}

    def sem(self, name):
        return Sem(self.nc, self.ctx, name)

    def op(self, engine, fn, inc=None, k=1):
        def wrapped(eng, fn=fn, inc=inc, k=k):
            inst = fn(eng)
            if inc is not None:
                inst.then_inc(inc.h, k)
        self.q[engine].append(wrapped)
        return inc.inc(k) if inc is not None else None

    def seq(self, engine, sem, fn):
        self.wait(engine, sem, sem.n)
        return self.op(engine, fn, inc=sem)

    def wait(self, engine, sem, val):
        if val > 0:
            self.q[engine].append(lambda eng, s=sem, v=val: eng.wait_ge(s.h, v))

    def install(self, block):
        for name in ENGINES:
            fns = self.q[name]
            getattr(block, name)(lambda eng, fns=fns: [fn(eng) for fn in fns])


def build(D=1024, HH=4096, E=8, T=2048, NSEQ=2, HS=1024, W=512, CAP=512, stage="ALL"):
    SL = {"INIT": 0, "A0": 1, "FFN": 2, "ALL": 5}[stage]
    DK = D // 128
    HJ = HH // 128
    NHS = HH // HS
    HJR = HS // 128
    NG = T // 128
    DH = max(1, D // 512)
    DW = min(512, D)
    NCG = CAP // 128
    SLOTS = E * CAP
    GPW = W // 128
    CHS = min(512, HS)
    CHJR = CHS // 128
    NCB = HH // CHS * E
    CPG = max(1, NCB // (NSEQ * NG))
    assert CAP == W

    nc = bass.Bass("TRN2", target_bir_lowering=False,
                   detect_race_conditions=os.environ.get("BASS_RD", "0") == "1")
    x_d = nc.dram_tensor("x", [NSEQ, T, D], F32, kind="ExternalInput").ap()
    wd_d = nc.dram_tensor("wd", [D, D], F32, kind="ExternalInput").ap()
    bd_d = nc.dram_tensor("bd", [D], F32, kind="ExternalInput").ap()
    wi_d = nc.dram_tensor("wi", [D, D], F32, kind="ExternalInput").ap()
    bi_d = nc.dram_tensor("bi", [D], F32, kind="ExternalInput").ap()
    w1_d = nc.dram_tensor("w1", [E, D, HH], F32, kind="ExternalInput").ap()
    b1_d = nc.dram_tensor("b1", [E, HH], F32, kind="ExternalInput").ap()
    w2_d = nc.dram_tensor("w2", [E, HH, D], F32, kind="ExternalInput").ap()
    b2_d = nc.dram_tensor("b2", [E, D], F32, kind="ExternalInput").ap()
    wr_d = nc.dram_tensor("wr", [D, E], F32, kind="ExternalInput").ap()
    br_d = nc.dram_tensor("br", [E], F32, kind="ExternalInput").ap()
    out_d = nc.dram_tensor("out", [NSEQ, T, D], F32, kind="ExternalOutput").ap()
    xn_scr = [nc.dram_tensor(f"xn_scr{k}", [T + 128, D], F32, kind="Internal").ap()
              for k in range(NSEQ)]
    g_scr = [nc.dram_tensor(f"g_scr{k}", [T + 128, D], BF16, kind="Internal").ap()
             for k in range(NSEQ)]
    s_scr = nc.dram_tensor("s_scr", [NSEQ, 128, DK * T], BF16, kind="Internal").ap()
    tok_scr = [nc.dram_tensor(f"tok_scr{k}", [SLOTS + T + 128], U32, kind="Internal").ap()
               for k in range(NSEQ)]

    with ExitStack() as ctx:
        en = ctx.enter_context
        identf = en(nc.sbuf_tensor("identf", [128, 128], F32))
        iota8 = en(nc.sbuf_tensor("iota8", [128, 8], F32))
        rev8 = en(nc.sbuf_tensor("rev8", [128, 8], F32))
        ones8f = en(nc.sbuf_tensor("ones8f", [E, 1], F32))
        ones8b = en(nc.sbuf_tensor("ones8b", [E, 1], BF16))
        onesf = en(nc.sbuf_tensor("onesf", [1, 128], F32))
        wr_sb = en(nc.sbuf_tensor("wr_sb", [128, DK * E], F32))
        br_row = en(nc.sbuf_tensor("br_row", [1, E], F32))
        brbc = en(nc.sbuf_tensor("brbc", [128, E], F32))
        b1c = en(nc.sbuf_tensor("b1c", [128, E * HJ], F32))
        rb1c = en(nc.sbuf_tensor("rb1c", [128, E * HJ], BF16))
        bdc = en(nc.sbuf_tensor("bdc", [128, DK], F32))
        bic = en(nc.sbuf_tensor("bic", [128, DK], F32))
        ckc = en(nc.sbuf_tensor("ckc", [128, E * DK], F32))
        mrows = en(nc.sbuf_tensor("mrows", [E, T], BF16))
        stats = en(nc.sbuf_tensor("stats", [128, 16], F32))
        r_sb = en(nc.sbuf_tensor("r_sb", [128, 16], F32))
        eqt = en(nc.sbuf_tensor("eqt", [128, 16], F32))
        idxs = en(nc.sbuf_tensor("idxs", [128, 16], F32))
        mtok = en(nc.sbuf_tensor("mtok", [128, 16], F32))
        capoff = en(nc.sbuf_tensor("capoff", [E, 1], F32))
        scatf = en(nc.sbuf_tensor("scatf", [128, NG], F32))
        scatidx = en(nc.sbuf_tensor("scatidx", [128, NG], U32))
        tokids = en(nc.sbuf_tensor("tokids", [128, NG], U32))
        tokinit = en(nc.sbuf_tensor("tokinit", [128, 1], U32))
        idxcols = en(nc.sbuf_tensor("idxcols", [128, NSEQ * SLOTS // 128], U32))
        b2rows = en(nc.sbuf_tensor("b2rows", [E, D], BF16))
        c_rows = en(nc.sbuf_tensor("c_rows", [E, D], F32))
        rowst = en(nc.sbuf_tensor("rowst", [1, D], F32))
        wdt_sb = en(nc.sbuf_tensor("wdt_sb", [128, 2 * DK * D], BF16))
        xnw = en(nc.sbuf_tensor("xnw", [128, DK * W], BF16))
        AQ = DK * HS
        arena = en(nc.sbuf_tensor("arena", [128, 4 * AQ], BF16))
        w1b = [arena[:, 0 * AQ:1 * AQ], arena[:, 1 * AQ:2 * AQ]]
        w2b = [arena[:, 2 * AQ:3 * AQ], arena[:, 3 * AQ:4 * AQ]]
        dec_a = arena[:, 0:DK * T]
        inj_a = arena[:, 2 * AQ:2 * AQ + DK * T]
        h_big = en(nc.sbuf_tensor("h_big", [128, HJR * W], BF16))
        zrow = h_big[:, 0:D]
        out_exp = [en(nc.sbuf_tensor(f"out_exp{k}", [128, DK * CAP], F32)) for k in range(NSEQ)]
        xi_sl = [en(nc.sbuf_tensor(f"xi_sl{k}", [128, DK * CAP], BF16)) for k in range(NSEQ)]
        cwb = [xi_sl[0][:, 0:CHJR * D], xi_sl[1][:, 0:CHJR * D]]
        gstg = [en(nc.sbuf_tensor(f"gstg{k}", [128, D], F32)) for k in range(2)]
        gt_st = [en(nc.sbuf_tensor(f"gt_st{k}", [128, D], BF16)) for k in range(2)]
        scr = en(nc.sbuf_tensor("scr", [128, 4 * D], F32))
        xstage = [scr[:, 0:D], scr[:, D:2 * D]]
        xstage4 = [xstage[0], xstage[1], gstg[0][:], gstg[1][:]]
        sqv = scr[:, 2 * D:3 * D]
        xnTf = scr[:, 3 * D:4 * D]
        cums = scr[0:E, 0:T]
        slotrow = scr[0:1, 2 * D:2 * D + T]
        outg = [scr[:, 2 * D:3 * D], scr[:, 3 * D:4 * D]]
        cg_t0 = [xi_sl[0][:, 0:D], xi_sl[0][:, D:2 * D]]
        cs_t0 = [xi_sl[1][:, 0:D], xi_sl[1][:, D:2 * D]]
        pbuf = en(nc.sbuf_tensor("pbuf", [128, 256], F32))

        pb = [en(nc.psum_tensor(f"pb{k}", [128, 512], F32)) for k in range(8)]

        block = en(nc.Block())
        p = Prog(nc, ctx)
        cld = p.sem("cld")
        wtc = p.sem("wtc")
        crw = p.sem("crw")
        xl = [p.sem("xl0"), p.sem("xl1")]
        gw = [p.sem("gw0"), p.sem("gw1")]
        so = [p.sem("so0"), p.sem("so1")]
        xns = [p.sem("xns0"), p.sem("xns1")]
        ssp = [p.sem("ssp0"), p.sem("ssp1")]
        cgl = [p.sem("cgl0"), p.sem("cgl1")]
        csl = [p.sem("csl0"), p.sem("csl1")]
        ga = [p.sem("ga0"), p.sem("ga1"), p.sem("ga2"), p.sem("ga3")]
        scg = [p.sem("scg0"), p.sem("scg1")]
        idp = p.sem("idp")
        xdp = p.sem("xdp")
        b2l = p.sem("b2l")
        ids = [p.sem("ids0"), p.sem("ids1")]
        idr = [p.sem("idr0"), p.sem("idr1")]
        pe = p.sem("pe")
        sc = p.sem("sc")
        dv = p.sem("dv")
        gp = p.sem("gp")

        p.op("scalar", lambda s: s.dma_start(
            out=wr_sb[:].rearrange("p (dk e) -> p dk e", dk=DK),
            in_=wr_d.rearrange("(dk p) e -> p dk e", p=128)), inc=cld, k=16)
        p.op("scalar", lambda s: s.dma_start(out=br_row[:], in_=br_d[None, :]), inc=cld, k=16)
        p.op("gpsimd", lambda g: g.dma_start(out=b2rows[:], in_=b2_d), inc=b2l, k=16)
        b1st = out_exp[1][0:HJ, 0:E * 128]
        bdst = out_exp[0][0:DK, 0:128]
        bist = out_exp[0][0:DK, 128:256]
        for i in range(E):
            p.op("scalar", lambda s, i=i: s.dma_start(
                out=b1st[:, i * 128:(i + 1) * 128],
                in_=b1_d[i].rearrange("(hj p) -> hj p", p=128)), inc=cld, k=16)
        p.op("scalar", lambda s: s.dma_start(
            out=bdst, in_=bd_d.rearrange("(ek p) -> ek p", p=128)), inc=cld, k=16)
        p.op("scalar", lambda s: s.dma_start(
            out=bist, in_=bi_d.rearrange("(ek p) -> ek p", p=128)), inc=cld, k=16)
        t_const = cld.n

        p.op("gpsimd", lambda g: g.iota(iota8[:], pattern=[[1, 8]], base=0, channel_multiplier=0,
                                        allow_small_or_imprecise_dtypes=True))
        p.op("gpsimd", lambda g: g.iota(rev8[:], pattern=[[-1, 8]], base=8, channel_multiplier=0,
                                        allow_small_or_imprecise_dtypes=True))
        p.op("gpsimd", lambda g: g.memset(ones8f[:], 1.0))
        p.op("gpsimd", lambda g: g.memset(ones8b[:], 1.0))
        p.op("gpsimd", lambda g: g.memset(onesf[:], 1.0))
        p.op("gpsimd", lambda g: g.memset(zrow, 0.0))
        p.op("gpsimd", lambda g: g.memset(sqv, 0.0))
        p.op("gpsimd", lambda g: g.iota(capoff[:], pattern=[[1, 1]], base=-1, channel_multiplier=CAP,
                                        allow_small_or_imprecise_dtypes=True))
        p.op("gpsimd", lambda g: g.iota(tokids[:], pattern=[[128, NG]], base=0, channel_multiplier=1,
                                        allow_small_or_imprecise_dtypes=True))
        p.op("gpsimd", lambda g: g.iota(tokinit[:], pattern=[[1, 1]], base=T,
                                        channel_multiplier=1, allow_small_or_imprecise_dtypes=True))
        p.op("gpsimd", lambda g: g.iota(identf[:], pattern=[[-1, 128]], base=0, channel_multiplier=1,
                                        allow_small_or_imprecise_dtypes=True), inc=gp)
        t_gpinit = gp.n
        p.wait("vector", gp, t_gpinit)
        p.op("vector", lambda v: v.tensor_single_scalar(identf[:], identf[:], 0.0, op=ALU.is_equal), inc=dv)
        t_ident = dv.n

        p.wait("gpsimd", gp, t_gpinit)
        for h in range(NSEQ):
            p.op("gpsimd", lambda g, h=h: g.dma_start(
                out=xn_scr[h][T:T + 128, :], in_=sqv), inc=xdp, k=16)
        prefill_q = []
        for h in range(NSEQ):
            for c in range(SLOTS // 128):
                prefill_q.append(lambda g, h=h, c=c: g.dma_start(
                    out=tok_scr[h][c * 128:(c + 1) * 128][:, None], in_=tokinit[:, 0:1]))
            for g0 in range(NG):
                prefill_q.append(lambda g, h=h, g0=g0: g.dma_start(
                    out=g_scr[h][g0 * 128:(g0 + 1) * 128, :], in_=zrow))
        PPG = (len(prefill_q) + NG - 1) // NG
        t_prefill = None

        p.wait("tensor", cld, t_const)
        p.wait("tensor", dv, t_ident)
        for i in range(E):
            p.op("tensor", lambda te, i=i: te.transpose(
                pb[3][:, i * HJ:(i + 1) * HJ], b1st[:, i * 128:(i + 1) * 128],
                identf[0:HJ, 0:HJ]))
        p.op("tensor", lambda te: te.transpose(
            pb[3][:, E * HJ:E * HJ + DK], bdst, identf[0:DK, 0:DK]))
        p.op("tensor", lambda te: te.transpose(
            pb[3][:, E * HJ + DK:E * HJ + 2 * DK], bist, identf[0:DK, 0:DK]), inc=pe)
        t_b1tr = pe.n
        p.wait("vector", pe, t_b1tr)
        p.seq("vector", dv, lambda v: v.tensor_copy(b1c[:], pb[3][:, 0:E * HJ]))
        p.seq("vector", dv, lambda v: v.tensor_copy(bdc[:], pb[3][:, E * HJ:E * HJ + DK]))
        p.seq("vector", dv, lambda v: v.tensor_copy(bic[:], pb[3][:, E * HJ + DK:E * HJ + 2 * DK]))
        t_b1c = dv.n
        p.wait("scalar", dv, t_b1c)
        p.op("scalar", lambda s: s.activation(rb1c[:], b1c[:], AF.Relu), inc=sc)
        t_rb1 = sc.n

        p.wait("tensor", cld, t_const)
        p.wait("tensor", gp, t_gpinit)
        p.op("tensor", lambda te: te.matmul(pb[2][:, 0:E], lhsT=onesf[:], rhs=br_row[:],
                                            start=True, stop=True), inc=pe)
        t_brmm = pe.n
        p.wait("vector", pe, t_brmm)
        p.seq("vector", dv, lambda v: v.tensor_copy(brbc[:], pb[2][:, 0:E]))
        t_brbc = dv.n

        w_srcs = (wd_d, wi_d)
        t_wtld = {}
        for m in range(2):
            for ek in range(DK):
                ci = m * DK + ek
                p.wait("sync", pe, t_wtld.get(ci - 2, 0))
                p.op("sync", lambda s, m=m, ek=ek, ci=ci: s.dma_start(
                    out=xstage[ci % 2], in_=w_srcs[m][ek * 128:(ek + 1) * 128, :]),
                    inc=xl[ci % 2], k=16)
                t_ch = xl[ci % 2].n
                p.wait("tensor", xl[ci % 2], t_ch)
                p.wait("tensor", dv, dv.n)
                if ci == 0:
                    p.wait("tensor", dv, t_ident)
                for dk in range(DK):
                    p.op("tensor", lambda te, dk=dk, ci=ci: te.transpose(
                        pb[dk // 4][:, (dk % 4) * 128:(dk % 4 + 1) * 128],
                        xstage[ci % 2][:, dk * 128:(dk + 1) * 128], identf[:]),
                        inc=pe if dk == DK - 1 else None)
                t_wtld[ci] = pe.n
                p.wait("vector", pe, pe.n)
                p.wait("vector", dv, dv.n)
                for dk in range(DK):
                    p.op("vector", lambda v, dk=dk, m=m, ek=ek: v.tensor_copy(
                        wdt_sb[:, m * DK * D + dk * D + ek * 128:m * DK * D + dk * D + (ek + 1) * 128],
                        pb[dk // 4][:, (dk % 4) * 128:(dk % 4 + 1) * 128]),
                        inc=dv if dk == DK - 1 else None)
        t_wdt = dv.n

        def wdt_sl(m, dk, ek):
            o = m * DK * D + dk * D + ek * 128
            return wdt_sb[:, o:o + 128]

        cblk = 0
        t_cw_used = {}
        t_crow = 0
        t_idxr = {0: 0, 1: 0}
        t_spill = {0: 0, 1: 0}
        t_xn_ev = 0
        t_bank_sc = 0
        t_bank_dv = 0

        def emit_cpass_block():
            nonlocal cblk, t_crow
            if cblk >= NCB:
                return
            i, cb = divmod(cblk, HH // CHS)
            buf = cblk % 2
            p.wait("gpsimd", pe, t_cw_used.get(cblk - 2, 0))
            p.op("gpsimd", lambda g, i=i, cb=cb, buf=buf: g.dma_start(
                out=cwb[buf].rearrange("p (hjr d) -> p hjr d", hjr=CHJR),
                in_=w2_d[i, cb * CHS:(cb + 1) * CHS, :].rearrange("(hjr p) d -> p hjr d", p=128)),
                inc=gw[buf], k=16)
            t_w = gw[buf].n
            p.wait("tensor", gw[buf], t_w)
            if cblk == 0:
                p.wait("tensor", sc, t_rb1)
            if cb == 0:
                p.wait("tensor", dv, t_crow)
            for hjr in range(CHJR):
                hj = cb * CHJR + hjr
                for dh in range(DH):
                    p.op("tensor", lambda te, i=i, hj=hj, hjr=hjr, dh=dh, buf=buf: te.matmul(
                        pb[6 + dh][0:1, 0:DW],
                        lhsT=rb1c[:, i * HJ + hj:i * HJ + hj + 1],
                        rhs=cwb[buf][:, hjr * D + dh * 512:hjr * D + dh * 512 + DW],
                        start=(hj == 0), stop=(hj == HJ - 1)),
                        inc=pe if (hjr == CHJR - 1 and dh == DH - 1) else None)
            t_cw_used[cblk] = pe.n
            if cb == HH // CHS - 1:
                t_cmm = pe.n
                p.wait("vector", pe, t_cmm)
                p.wait("vector", crw, crw.n)
                p.wait("vector", dv, dv.n)
                for dh in range(DH):
                    p.op("vector", lambda v, dh=dh: v.tensor_copy(
                        rowst[:, dh * 512:dh * 512 + DW], pb[6 + dh][0:1, 0:DW]),
                        inc=dv if dh == DH - 1 else None)
                t_crow = dv.n
                p.wait("sync", dv, t_crow)
                p.op("sync", lambda s, i=i: s.dma_start(out=c_rows[i:i + 1, :], in_=rowst[:]),
                     inc=crw, k=16)
            cblk += 1

        tB = {}
        t_ln = {}
        t_sdd = {}
        t_ssmw_pe = 0
        t_ssm_sc = {0: 0, 1: 0}
        t_ssm_dv = {0: 0, 1: 0}
        t_idx_dv = 0
        t_idx_pe = 0
        for h in range(NSEQ):
            gi0 = h * NG
            p.wait("sync", xns[0], xns[0].n)
            p.wait("sync", pe, tB.get((gi0 - 4, 'tr'), 0))
            p.wait("sync", pe, t_wtld.get(2 * DK - 2, 0))
            p.wait("sync", pe, t_idx_pe)
            p.op("sync", lambda s, h=h: s.dma_start(
                out=xstage4[0], in_=x_d[h, 0:128, :]), inc=xl[0], k=16)
            xg_t = [xl[0].n]
            t_spl = {}

            def emit_LNa(g, h=h):
                for _ in range(PPG):
                    if prefill_q:
                        p.op("gpsimd", prefill_q.pop(0), inc=idp, k=16)
                par = g % 2
                gi = h * NG + g
                st = stats[:, par * 8:par * 8 + 8]
                xg_g = xstage4[g % 4]
                sq_g = sqv if par == 0 else out_exp[0][:, D:2 * D]
                if g + 1 < NG:
                    p.wait("sync", xns[(g + 1) % 2], xns[(g + 1) % 2].n)
                    p.wait("sync", pe, tB.get((gi - 3, 'tr'), 0))
                    if gi == 0:
                        p.wait("sync", pe, t_wtld[2 * DK - 1])
                    p.op("sync", lambda s, g=g: s.dma_start(
                        out=xstage4[(g + 1) % 4], in_=x_d[h, (g + 1) * 128:(g + 2) * 128, :]),
                        inc=xl[(g + 1) % 2], k=16)
                    xg_t.append(xl[(g + 1) % 2].n)
                p.wait("vector", xl[par], xg_t[g])
                p.wait("vector", sc, t_sdd.get(gi - 2, 0))
                if g == 0 and h == 0:
                    p.wait("vector", xdp, 32)
                p.seq("vector", dv, lambda v: v.reduce_sum(st[:, 0:1], xg_g, axis=AX.X))
                p.seq("vector", dv, lambda v: v.tensor_mul(sq_g, xg_g, xg_g))
                p.seq("vector", dv, lambda v: v.reduce_sum(st[:, 1:2], sq_g, axis=AX.X))
                p.seq("vector", dv, lambda v: v.tensor_scalar_mul(st[:, 0:1], st[:, 0:1], 1.0 / D))
                p.seq("vector", dv, lambda v: v.tensor_mul(st[:, 7:8], st[:, 0:1], st[:, 0:1]))
                p.seq("vector", dv, lambda v: v.tensor_scalar(
                    st[:, 2:3], st[:, 1:2], 1.0 / D, None, op0=ALU.mult))
                p.seq("vector", dv, lambda v: v.tensor_tensor(
                    st[:, 2:3], st[:, 2:3], st[:, 7:8], op=ALU.subtract))
                p.seq("vector", dv, lambda v: v.tensor_scalar(
                    st[:, 2:3], st[:, 2:3], EPS, None, op0=ALU.add))
                t_var = dv.n
                p.wait("scalar", dv, t_var)
                p.seq("scalar", sc, lambda s: s.activation(st[:, 3:4], st[:, 2:3], AF.Sqrt))
                t_sdd[gi] = sc.n

            def emit_LNb(g, h=h):
                par = g % 2
                gi = h * NG + g
                st = stats[:, par * 8:par * 8 + 8]
                xg_g = xstage4[g % 4]
                p.wait("vector", sc, t_sdd[gi])
                p.seq("vector", dv, lambda v: v.reciprocal(st[:, 4:5], st[:, 3:4]))
                p.seq("vector", dv, lambda v: v.tensor_scalar(
                    xg_g, xg_g, st[:, 0:1], st[:, 4:5],
                    op0=ALU.subtract, op1=ALU.mult))
                t_xnst = dv.n
                t_ln[gi] = t_xnst
                p.wait("sync", dv, t_xnst)
                p.op("sync", lambda s, g=g, xg_g=xg_g: s.dma_start(
                    out=xn_scr[h][g * 128:(g + 1) * 128, :], in_=xg_g), inc=xns[par], k=16)
                t_spl[g] = xns[par].n

            def emit_Btr(g, h=h):
                par = g % 2
                gi = h * NG + g
                pair = (pb[0], pb[1]) if par == 0 else (pb[2], pb[3])
                xnTf_g = xnTf if par == 0 else out_exp[0][:, 0:D]
                p.wait("tensor", dv, t_ln[gi])
                p.wait("tensor", dv, tB.get((gi - 2, 'evac'), 0))
                if par == 1:
                    p.wait("tensor", sc, t_ssm_sc[1])
                    p.wait("tensor", dv, t_ssm_dv[1])
                if h == 0 and g < 2:
                    p.wait("tensor", dv, t_wdt)
                for dk in range(DK):
                    p.op("tensor", lambda te, dk=dk, g=g: te.transpose(
                        pair[dk // 4][:, (dk % 4) * 128:(dk % 4 + 1) * 128],
                        xstage4[g % 4][:, dk * 128:(dk + 1) * 128], identf[:]),
                        inc=pe if dk == DK - 1 else None)
                tB[(gi, 'tr')] = pe.n
                p.wait("vector", pe, tB[(gi, 'tr')])
                if g % GPW == 0:
                    p.wait("vector", pe, t_ssmw_pe)
                p.wait("vector", dv, dv.n)
                gw_ = g % GPW
                for dk in range(DK):
                    p.op("vector", lambda v, dk=dk, xnTf_g=xnTf_g, pair=pair: v.tensor_copy(
                        xnTf_g[:, dk * 128:(dk + 1) * 128],
                        pair[dk // 4][:, (dk % 4) * 128:(dk % 4 + 1) * 128]),
                        inc=dv if dk == DK - 1 else None)
                tB[(gi, 'evT')] = dv.n
                for dk in range(DK):
                    p.op("vector", lambda v, dk=dk, gw_=gw_, pair=pair: v.tensor_copy(
                        xnw[:, dk * W + gw_ * 128:dk * W + (gw_ + 1) * 128],
                        pair[dk // 4][:, (dk % 4) * 128:(dk % 4 + 1) * 128]),
                        inc=dv if dk == DK - 1 else None)
                tB[(gi, 'evac')] = dv.n

            def emit_Bmask(g, h=h):
                par = g % 2
                gi = h * NG + g
                rtb = pb[5]
                xnTf_g = xnTf if par == 0 else out_exp[0][:, 0:D]
                st = stats[:, par * 8:par * 8 + 8]
                p.wait("tensor", dv, tB[(gi, 'evT')])
                p.wait("tensor", dv, tB.get((gi - 1, 'radd'), 0))
                p.wait("tensor", dv, t_ssm_dv[0])
                for dk in range(DK):
                    p.op("tensor", lambda te, dk=dk, xnTf_g=xnTf_g, rtb=rtb: te.matmul(
                        rtb[:, 0:E], lhsT=xnTf_g[:, dk * 128:(dk + 1) * 128],
                        rhs=wr_sb[:, dk * E:(dk + 1) * E],
                        start=(dk == 0), stop=(dk == DK - 1)),
                        inc=pe if dk == DK - 1 else None)
                t_rmm = pe.n
                p.wait("vector", pe, t_rmm)
                if g == 0 and h == 0:
                    p.wait("vector", dv, t_brbc)
                rsb = r_sb[:, par * 8:par * 8 + 8]
                eq = eqt[:, par * 8:par * 8 + 8]
                ix = idxs[:, par * 8:par * 8 + 8]
                mt = mtok[:, par * 8:par * 8 + 8]
                p.seq("vector", dv, lambda v, rtb=rtb: v.tensor_add(rsb[:, 0:E], rtb[:, 0:E], brbc[:]))
                tB[(gi, 'radd')] = dv.n
                p.seq("vector", dv, lambda v: v.reduce_max(st[:, 5:6], rsb[:, 0:E], axis=AX.X))
                p.seq("vector", dv, lambda v: v.tensor_scalar(
                    eq[:, 0:E], rsb[:, 0:E], st[:, 5:6], None, op0=ALU.is_equal))
                p.seq("vector", dv, lambda v: v.tensor_mul(ix[:, 0:E], eq[:, 0:E], rev8[:, 0:E]))
                p.seq("vector", dv, lambda v: v.tensor_scalar(
                    ix[:, 0:E], ix[:, 0:E], -1.0, 8.0, op0=ALU.mult, op1=ALU.add))
                p.seq("vector", dv, lambda v: v.tensor_reduce(
                    st[:, 6:7], ix[:, 0:E], axis=AX.X, op=ALU.min))
                p.seq("vector", dv, lambda v: v.tensor_scalar(
                    mt[:, 0:E], iota8[:, 0:E], st[:, 6:7], None, op0=ALU.is_equal))
                t_mtok = dv.n
                p.wait("tensor", dv, t_mtok)
                p.wait("tensor", dv, tB.get((gi - 1, 'mrows'), 0))
                p.op("tensor", lambda te, mt=mt, rtb=rtb: te.transpose(
                    rtb[0:E, 8:136], mt[:, 0:E], identf[:]), inc=pe)
                t_mtr = pe.n
                p.wait("vector", pe, t_mtr)
                p.seq("vector", dv, lambda v, g=g, rtb=rtb: v.tensor_copy(
                    mrows[:, g * 128:(g + 1) * 128], rtb[0:E, 8:136]))
                tB[(gi, 'mrows')] = dv.n

            def emit_SSMW(w_, h=h):
                nonlocal t_ssmw_pe
                p.wait("tensor", dv, tB[(h * NG + w_ * GPW + GPW - 1, 'evac')])
                p.wait("tensor", dv, t_wdt)
                if w_ == 0 and h > 0:
                    p.wait("tensor", ssp[h - 1], 16)
                    p.wait("tensor", dv, t_idx_dv)
                for m in range(2):
                    banks = (pb[4], pb[2]) if m == 0 else (pb[5], pb[3])
                    if m == 1:
                        p.wait("tensor", dv, tB[(h * NG + w_ * GPW + GPW - 1, 'mrows')])
                    for ek in range(DK):
                        bank = banks[ek % 2]
                        if m == 0:
                            p.wait("tensor", sc, t_ssm_sc[ek % 2])
                        else:
                            p.wait("tensor", dv, t_ssm_dv[ek % 2])
                        for dk in range(DK):
                            p.op("tensor", lambda te, m=m, dk=dk, ek=ek, bank=bank: te.matmul(
                                bank[:, 0:W], lhsT=wdt_sl(m, dk, ek),
                                rhs=xnw[:, dk * W:(dk + 1) * W],
                                start=(dk == 0), stop=(dk == DK - 1)),
                                inc=pe if dk == DK - 1 else None)
                        t_mm = pe.n
                        if m == 0:
                            p.wait("scalar", pe, t_mm)
                            p.wait("scalar", sc, sc.n)
                            p.op("scalar", lambda s, ek=ek, w_=w_, bank=bank: s.activation(
                                dec_a[:, ek * T + w_ * W:ek * T + (w_ + 1) * W], bank[:, 0:W],
                                AF.Sigmoid, bias=bdc[:, ek:ek + 1]), inc=sc)
                            t_ssm_sc[ek % 2] = sc.n
                        else:
                            p.wait("vector", pe, t_mm)
                            p.wait("vector", dv, dv.n)
                            p.op("vector", lambda v, ek=ek, w_=w_, bank=bank: v.tensor_scalar(
                                inj_a[:, ek * T + w_ * W:ek * T + (w_ + 1) * W], bank[:, 0:W],
                                bic[:, ek:ek + 1], None, op0=ALU.add), inc=dv)
                            t_ssm_dv[ek % 2] = dv.n
                t_ssmw_pe = pe.n

            LAG = 2
            for it in range(NG + LAG + 1 if SL >= 1 else 0):
                if it < NG:
                    emit_LNa(it)
                if 0 <= it - 1 < NG:
                    emit_LNb(it - 1)
                gm = it - LAG - 1
                if 0 <= gm < NG:
                    emit_Bmask(gm)
                    if gm % GPW == GPW - 1:
                        emit_SSMW(gm // GPW)
                if LAG <= it < NG + LAG:
                    emit_Btr(it - LAG)
                if 0 <= gm < NG:
                    for _ in range(CPG):
                        emit_cpass_block()
            t_A0 = dv.n
            t_ssm_all_sc = sc.n

            if SL >= 2:
                p.wait("vector", dv, t_A0)
                p.wait("vector", pe, pe.n)
                p.wait("vector", xns[0], xns[0].n)
                p.wait("vector", xns[1], xns[1].n)
                p.seq("vector", dv, lambda v: v.tensor_tensor_scan(
                    cums, mrows[:], mrows[:], 0.0, op0=ALU.add, op1=ALU.max))
                p.seq("vector", dv, lambda v: v.tensor_scalar(
                    cums, cums, capoff[:, 0:1], None, op0=ALU.add))
                p.seq("vector", dv, lambda v: v.tensor_mul(cums, cums, mrows[:]))
                t_slots = dv.n
                p.wait("tensor", dv, t_slots)
                p.wait("tensor", sc, t_ssm_all_sc)
                for n in range(T // 512):
                    p.op("tensor", lambda te, n=n: te.matmul(
                        pb[4][0:1, 0:512], lhsT=ones8f[:], rhs=cums[:, n * 512:(n + 1) * 512],
                        start=True, stop=True), inc=pe)
                    t_sr = pe.n
                    p.wait("vector", pe, t_sr)
                    p.seq("vector", dv, lambda v, n=n: v.tensor_copy(
                        slotrow[:, n * 512:(n + 1) * 512], pb[4][0:1, 0:512]))
                    p.wait("tensor", dv, dv.n)
                t_srow = dv.n
                p.wait("tensor", dv, t_srow)
                for g in range(NG):
                    p.op("tensor", lambda te, g=g: te.transpose(
                        pb[4][:, g:g + 1], slotrow[0:1, g * 128:(g + 1) * 128], identf[0:1, 0:1]),
                        inc=pe if g == NG - 1 else None)
                t_sct = pe.n
                t_idx_pe = t_sct
                p.wait("vector", pe, t_sct)
                p.wait("vector", ids[0], ids[0].n)
                p.wait("vector", ids[1], ids[1].n)
                p.seq("vector", dv, lambda v: v.tensor_copy(scatf[:], pb[4][:, 0:NG]))
                p.seq("vector", dv, lambda v: v.tensor_copy(scatidx[:], scatf[:]))
                t_sidx = dv.n
                t_idx_dv = t_sidx
                p.wait("gpsimd", idp, idp.n)
                p.wait("gpsimd", dv, t_sidx)
                for g in range(NG):
                    p.op("gpsimd", lambda gg, h=h, g=g: gg.indirect_dma_start(
                        out=tok_scr[h][:][:, None],
                        out_offset=IndirectOffsetOnAxis(ap=scatidx[:, g:g + 1], axis=0),
                        in_=tokids[:, g:g + 1], in_offset=None), inc=ids[h], k=16)
                t_scat = ids[h].n
                p.wait("gpsimd", ids[h], t_scat)
                for c in range(SLOTS // 128):
                    p.op("gpsimd", lambda s, h=h, c=c: s.dma_start(
                        out=idxcols[:, h * (SLOTS // 128) + c:h * (SLOTS // 128) + c + 1],
                        in_=tok_scr[h][c * 128:(c + 1) * 128][:, None]), inc=idr[h], k=16)
                t_idxr[h] = idr[h].n

                p.wait("vector", sc, t_ssm_all_sc)
                for ek in range(DK):
                    p.seq("vector", dv, lambda v, ek=ek: v.tensor_tensor_scan(
                        dec_a[:, ek * T:(ek + 1) * T], dec_a[:, ek * T:(ek + 1) * T],
                        inj_a[:, ek * T:(ek + 1) * T], 0.0, op0=ALU.mult, op1=ALU.add))
                t_scan = dv.n
                p.wait("gpsimd", dv, t_scan)
                p.op("gpsimd", lambda s, h=h: s.dma_start(
                    out=s_scr[h], in_=dec_a), inc=ssp[h], k=16)
                t_spill[h] = dv.n

        t_ck = 0
        if SL >= 2:
            p.wait("tensor", crw, crw.n)
            p.wait("tensor", b2l, b2l.n)
            sum8c = stats[:, 8:9]
            for dh in range(DH):
                p.wait("tensor", dv, dv.n)
                p.op("tensor", lambda te, dh=dh: te.matmul(
                    pb[6][0:1, 0:DW], lhsT=ones8f[:], rhs=c_rows[:, dh * 512:dh * 512 + DW],
                    start=True, stop=False))
                p.op("tensor", lambda te, dh=dh: te.matmul(
                    pb[6][0:1, 0:DW], lhsT=ones8b[:], rhs=b2rows[:, dh * 512:dh * 512 + DW],
                    start=False, stop=True), inc=pe)
                t_s8 = pe.n
                p.wait("vector", pe, t_s8)
                p.seq("vector", dv, lambda v, dh=dh: v.tensor_copy(
                    rowst[:, dh * 512:dh * 512 + DW], pb[6][0:1, 0:DW]))
            for dk in range(DK):
                p.wait("tensor", dv, dv.n)
                p.op("tensor", lambda te, dk=dk: te.transpose(
                    pb[7][:, 0:E], c_rows[0:E, dk * 128:(dk + 1) * 128], identf[0:E, 0:E]))
                p.op("tensor", lambda te, dk=dk: te.transpose(
                    pb[7][:, E:E + 1], rowst[0:1, dk * 128:(dk + 1) * 128], identf[0:1, 0:1]), inc=pe)
                t_st8 = pe.n
                p.wait("vector", pe, t_st8)
                p.seq("vector", dv, lambda v: v.tensor_copy(sum8c, pb[7][:, E:E + 1]))
                p.seq("vector", dv, lambda v, dk=dk: v.tensor_scalar(
                    ckc[:, dk:dk + (E - 1) * DK + 1:DK],
                    pb[7][:, 0:E], sum8c, -1.0, op0=ALU.subtract, op1=ALU.mult))
            t_ck = dv.n

        if SL >= 2:
            p.wait("gpsimd", dv, max(t_spill[0], t_spill[NSEQ - 1]))
            p.wait("gpsimd", ssp[0], 16)
            p.wait("gpsimd", ssp[NSEQ - 1], 16)
            p.wait("gpsimd", pe, pe.n)
            p.op("gpsimd", lambda g: g.dma_start(
                out=w1b[0].rearrange("p (dk q) -> p dk q", dk=DK),
                in_=w1_d[0, :, 0:HS].rearrange("(dk p) q -> p dk q", p=128)), inc=gw[0], k=16)
            p.op("gpsimd", lambda g: g.dma_start(
                out=w2b[0].rearrange("p (hjr d) -> p hjr d", hjr=HJR),
                in_=w2_d[0, 0:HS, :].rearrange("(hjr p) d -> p hjr d", p=128)), inc=gw[0], k=16)
            wblk_t = {0: (0, gw[0].n)}

            PH = ([4, 5], [0, 1])
            AB = ([0, 1, 2, 3], [4, 5, 6, 7])
            t_evac_a = 0
            t_evac_b = 0
            t_blk_lastmm = {}
            t_gstg_pe = {0: 0, 1: 0, 2: 0, 3: 0}
            t_scg = {0: 0, 1: 0}
            t_xi_all = {}
            t_sctr_last = 0
            t_gev_last = 0
            it = 0
            gstg4f = [gstg[0][:], gstg[1][:], xstage[0], xstage[1]]
            t_trpair = {0: 0, 1: 0}

            def emit_gathers(i):
                t_trpair[0] = max(t_trpair[0], dv.n)
                t_trpair[1] = max(t_trpair[1], dv.n)
                for hw in range(NSEQ):
                    for gg in range(NCG):
                        idx = hw * NCG + gg
                        gb = idx % 4
                        pr = idx % 2
                        base = 2 - 2 * pr
                        p.wait("gpsimd", idr[hw], t_idxr[hw])
                        p.wait("gpsimd", xns[0], xns[0].n)
                        p.wait("gpsimd", xns[1], xns[1].n)
                        p.wait("gpsimd", pe, t_gstg_pe.get(gb, 0))
                        p.op("gpsimd", lambda g2, i=i, hw=hw, gg=gg, gb=gb: g2.indirect_dma_start(
                            out=gstg4f[gb], out_offset=None,
                            in_=xn_scr[hw], in_offset=IndirectOffsetOnAxis(
                                ap=idxcols[:, hw * (SLOTS // 128) + i * NCG + gg:
                                           hw * (SLOTS // 128) + i * NCG + gg + 1], axis=0)),
                            inc=ga[gb], k=16)
                        t_g = ga[gb].n
                        p.wait("tensor", ga[gb], t_g)
                        p.wait("tensor", dv, t_trpair[pr])
                        for dk in range(DK):
                            p.op("tensor", lambda te, dk=dk, gb=gb, base=base: te.transpose(
                                pb[base + dk // 4][:, (dk % 4) * 128:(dk % 4 + 1) * 128],
                                gstg4f[gb][:, dk * 128:(dk + 1) * 128], identf[:]),
                                inc=pe if dk == DK - 1 else None)
                        t_xtr = pe.n
                        t_gstg_pe[gb] = t_xtr
                        p.wait("vector", pe, t_xtr)
                        p.wait("vector", dv, dv.n)
                        for dk in range(DK):
                            p.op("vector", lambda v, dk=dk, hw=hw, gg=gg, base=base: v.tensor_copy(
                                xi_sl[hw][:, dk * CAP + gg * 128:dk * CAP + (gg + 1) * 128],
                                pb[base + dk // 4][:, (dk % 4) * 128:(dk % 4 + 1) * 128]),
                                inc=dv if dk == DK - 1 else None)
                        t_trpair[pr] = dv.n
                    t_xi_all[(i, hw)] = dv.n

            emit_gathers(0)
            for i in range(E):
                for hs in range(NHS):
                    blk = i * NHS + hs
                    buf = blk % 2
                    nblk_i, nblk_hs = (i, hs + 1) if hs + 1 < NHS else (i + 1, 0)
                    if nblk_i < E:
                        nbuf = (blk + 1) % 2
                        p.wait("gpsimd", pe, t_blk_lastmm.get(blk - 1, 0))
                        p.op("gpsimd", lambda g, i2=nblk_i, hs2=nblk_hs, nbuf=nbuf: g.dma_start(
                            out=w1b[nbuf].rearrange("p (dk q) -> p dk q", dk=DK),
                            in_=w1_d[i2, :, hs2 * HS:(hs2 + 1) * HS].rearrange("(dk p) q -> p dk q", p=128)),
                            inc=gw[nbuf], k=16)
                        p.op("gpsimd", lambda g, i2=nblk_i, hs2=nblk_hs, nbuf=nbuf: g.dma_start(
                            out=w2b[nbuf].rearrange("p (hjr d) -> p hjr d", hjr=HJR),
                            in_=w2_d[i2, hs2 * HS:(hs2 + 1) * HS, :].rearrange("(hjr p) d -> p hjr d", p=128)),
                            inc=gw[nbuf], k=16)
                        wblk_t[blk + 1] = (nbuf, gw[nbuf].n)

                    for hw in range(NSEQ):
                        par = it % 2
                        ph = PH[par]
                        ab = AB[par]
                        bb = AB[1 - par]
                        wbuf_s, wbuf_t = wblk_t[blk]
                        p.wait("tensor", gw[wbuf_s], wbuf_t)
                        p.wait("tensor", dv, t_xi_all[(i, hw)])
                        t_relu = {}
                        for hjr in range(HJR):
                            bank = ph[hjr % 2]
                            if hjr < 2:
                                p.wait("tensor", dv, t_evac_a)
                            else:
                                p.wait("tensor", sc, t_relu[hjr - 2])
                            for dk in range(DK):
                                p.op("tensor", lambda te, dk=dk, hjr=hjr, buf=buf, hw=hw, bank=bank: te.matmul(
                                    pb[bank][:, 0:W],
                                    lhsT=w1b[buf][:, dk * HS + hjr * 128:dk * HS + (hjr + 1) * 128],
                                    rhs=xi_sl[hw][:, dk * CAP:dk * CAP + W],
                                    start=(dk == 0), stop=(dk == DK - 1)),
                                    inc=pe if dk == DK - 1 else None)
                            t_mm1 = pe.n
                            hj_abs = hs * HJR + hjr
                            p.wait("scalar", pe, t_mm1)
                            p.wait("scalar", sc, sc.n)
                            p.op("scalar", lambda s, hjr=hjr, i=i, hj_abs=hj_abs, bank=bank: s.activation(
                                h_big[:, hjr * W:(hjr + 1) * W], pb[bank][:, 0:W], AF.Relu,
                                bias=b1c[:, i * HJ + hj_abs:i * HJ + hj_abs + 1]), inc=sc)
                            t_relu[hjr] = sc.n
                        p.wait("tensor", dv, t_evac_b)
                        if hs == 0 and hw == 0:
                            p.wait("tensor", dv, t_gev_last)
                        for hjr in range(HJR):
                            p.wait("tensor", sc, t_relu[hjr])
                            for dk in range(DK // 2):
                                p.op("tensor", lambda te, dk=dk, hjr=hjr, buf=buf, ab=ab: te.matmul(
                                    pb[ab[dk]][:, 0:W],
                                    lhsT=w2b[buf][:, hjr * D + dk * 128:hjr * D + (dk + 1) * 128],
                                    rhs=h_big[:, hjr * W:(hjr + 1) * W],
                                    start=(hjr == 0), stop=(hjr == HJR - 1)),
                                    inc=pe if (hjr == HJR - 1 and dk == DK // 2 - 1) else None)
                        t_mm2a = pe.n
                        for hjr in range(HJR):
                            for dk in range(DK // 2, DK):
                                p.op("tensor", lambda te, dk=dk, hjr=hjr, buf=buf, bb=bb: te.matmul(
                                    pb[bb[dk - DK // 2]][:, 0:W],
                                    lhsT=w2b[buf][:, hjr * D + dk * 128:hjr * D + (dk + 1) * 128],
                                    rhs=h_big[:, hjr * W:(hjr + 1) * W],
                                    start=(hjr == 0), stop=(hjr == HJR - 1)),
                                    inc=pe if (hjr == HJR - 1 and dk == DK - 1) else None)
                        t_mm2b = pe.n
                        t_blk_lastmm[blk] = t_mm2b

                        def emit_evac(banks, dk0, t_mm, hs=hs, hw=hw):
                            p.wait("vector", pe, t_mm)
                            p.wait("vector", dv, dv.n)
                            for j in range(DK // 2):
                                dk = dk0 + j
                                o0 = dk * CAP
                                if hs == 0:
                                    p.op("vector", lambda v, j=j, o0=o0, banks=banks, hw=hw: v.tensor_copy(
                                        out_exp[hw][:, o0:o0 + W], pb[banks[j]][:, 0:W]),
                                        inc=dv if j == DK // 2 - 1 else None)
                                else:
                                    p.op("vector", lambda v, j=j, o0=o0, banks=banks, hw=hw: v.tensor_add(
                                        out_exp[hw][:, o0:o0 + W], pb[banks[j]][:, 0:W],
                                        out_exp[hw][:, o0:o0 + W]),
                                        inc=dv if j == DK // 2 - 1 else None)
                            return dv.n

                        t_evac_a = emit_evac(ab, 0, t_mm2a)
                        t_evac_b = emit_evac(bb, DK // 2, t_mm2b)
                        it += 1

                if i + 1 < E:
                    emit_gathers(i + 1)
                p.wait("scalar", dv, t_evac_b)
                p.wait("scalar", dv, t_ck)
                p.wait("scalar", sc, sc.n)
                t_sil = {}
                for hw in range(NSEQ):
                    for dk in range(DK):
                        p.op("scalar", lambda s, dk=dk, i=i, hw=hw: s.activation(
                            out_exp[hw][:, dk * CAP:(dk + 1) * CAP],
                            out_exp[hw][:, dk * CAP:(dk + 1) * CAP],
                            AF.Silu, bias=ckc[:, i * DK + dk:i * DK + dk + 1]),
                            inc=sc if dk == DK - 1 else None)
                    t_sil[hw] = sc.n
                for hw in range(NSEQ):
                    for gg in range(NCG):
                        idx = hw * NCG + gg
                        gb = idx % 2
                        pr = idx % 2
                        base = 2 - 2 * pr
                        p.wait("tensor", sc, t_sil[hw])
                        p.wait("tensor", dv, t_trpair[pr])
                        for dk in range(DK):
                            p.op("tensor", lambda te, dk=dk, gg=gg, hw=hw, base=base: te.transpose(
                                pb[base + dk // 4][:, (dk % 4) * 128:(dk % 4 + 1) * 128],
                                out_exp[hw][:, dk * CAP + gg * 128:dk * CAP + (gg + 1) * 128],
                                identf[:]),
                                inc=pe if dk == DK - 1 else None)
                        t_gtr = pe.n
                        t_sctr_last = t_gtr
                        p.wait("vector", pe, t_gtr)
                        p.wait("vector", dv, dv.n)
                        p.wait("vector", scg[gb], t_scg[gb])
                        for dk in range(DK):
                            p.op("vector", lambda v, dk=dk, gb=gb, base=base: v.tensor_copy(
                                gt_st[gb][:, dk * 128:(dk + 1) * 128],
                                pb[base + dk // 4][:, (dk % 4) * 128:(dk % 4 + 1) * 128]),
                                inc=dv if dk == DK - 1 else None)
                        t_trpair[pr] = dv.n
                        t_gev = dv.n
                        t_gev_last = t_gev
                        p.wait("gpsimd", dv, t_gev)
                        p.op("gpsimd", lambda g2, i=i, hw=hw, gg=gg, gb=gb: g2.indirect_dma_start(
                            out=g_scr[hw], out_offset=IndirectOffsetOnAxis(
                                ap=idxcols[:, hw * (SLOTS // 128) + i * NCG + gg:
                                           hw * (SLOTS // 128) + i * NCG + gg + 1], axis=0),
                            in_=gt_st[gb][:], in_offset=None), inc=scg[gb], k=16)
                        t_scg[gb] = scg[gb].n

        for h in range(NSEQ if SL >= 5 else 0):
            xg_t2 = []
            cg_t = []
            cs_t = []
            s_fm = s_scr[h].rearrange("p (dk t) -> p dk t", dk=DK)
            p.wait("sync", ssp[h], 16)
            p.wait("sync", pe, t_gstg_pe[2])
            p.wait("sync", pe, t_blk_lastmm[E * NHS - 1])
            p.wait("sync", dv, dv.n)
            p.op("sync", lambda s, h=h: s.dma_start(out=xstage[0], in_=x_d[h, 0:128, :]), inc=xl[0], k=16)
            xg_t2.append(xl[0].n)
            p.op("sync", lambda s, h=h, s_fm=s_fm: s.dma_start(
                out=cs_t0[0].rearrange("p (dk t) -> p dk t", dk=DK),
                in_=s_fm[:, :, 0:128]), inc=csl[0], k=16)
            cs_t.append(csl[0].n)
            p.wait("sync", scg[0], scg[0].n)
            p.wait("sync", scg[1], scg[1].n)
            p.op("sync", lambda s, h=h: s.dma_start(out=cg_t0[0], in_=g_scr[h][0:128, :]), inc=cgl[0], k=16)
            cg_t.append(cgl[0].n)
            t_pstore = {}
            t_pmul = {}
            so0_base = {0: so[0].n, 1: so[1].n}
            for g in range(NG):
                gb = g % 2
                pair = (pb[0], pb[1]) if gb == 0 else (pb[2], pb[3])
                if g + 1 < NG:
                    p.wait("sync", dv, dv.n)
                    if h == 0 and g == 0:
                        p.wait("sync", pe, t_gstg_pe[3])
                    p.op("sync", lambda s, h=h, g=g: s.dma_start(
                        out=xstage[(g + 1) % 2], in_=x_d[h, (g + 1) * 128:(g + 2) * 128, :]),
                        inc=xl[(g + 1) % 2], k=16)
                    xg_t2.append(xl[(g + 1) % 2].n)
                    p.op("sync", lambda s, h=h, g=g: s.dma_start(
                        out=cg_t0[(g + 1) % 2], in_=g_scr[h][(g + 1) * 128:(g + 2) * 128, :]),
                        inc=cgl[(g + 1) % 2], k=16)
                    cg_t.append(cgl[(g + 1) % 2].n)
                    p.op("sync", lambda s, h=h, g=g, s_fm=s_fm: s.dma_start(
                        out=cs_t0[(g + 1) % 2].rearrange("p (dk t) -> p dk t", dk=DK),
                        in_=s_fm[:, :, (g + 1) * 128:(g + 2) * 128]), inc=csl[(g + 1) % 2], k=16)
                    cs_t.append(csl[(g + 1) % 2].n)
                p.wait("vector", csl[gb], cs_t[g])
                if h == 0 and g < 2:
                    p.wait("vector", pe, t_gstg_pe[g])
                p.wait("vector", dv, dv.n)
                p.op("vector", lambda v, gb=gb: v.tensor_copy(gstg[gb][:], cs_t0[gb]), inc=dv)
                t_sc32 = dv.n
                p.wait("tensor", dv, t_sc32)
                if h == 0 and g < 2:
                    p.wait("tensor", dv, t_trpair[1 - (g % 2)])
                p.wait("tensor", dv, t_pmul.get(g - 2, 0))
                for dk in range(DK):
                    p.op("tensor", lambda te, dk=dk, gb=gb: te.transpose(
                        pair[dk // 4][:, (dk % 4) * 128:(dk % 4 + 1) * 128],
                        gstg[gb][:, dk * 128:(dk + 1) * 128], identf[:]),
                        inc=pe if dk == DK - 1 else None)
                t_str = pe.n
                p.wait("vector", pe, t_str)
                p.wait("vector", cgl[gb], cg_t[g])
                p.wait("vector", xl[gb], xg_t2[g])
                p.wait("vector", so[gb], t_pstore.get(g - 2, so0_base[gb]))
                p.wait("vector", dv, dv.n)
                for dk in range(DK):
                    p.op("vector", lambda v, dk=dk, gb=gb: v.tensor_mul(
                        cg_t0[gb][:, dk * 128:(dk + 1) * 128],
                        pair[dk // 4][:, (dk % 4) * 128:(dk % 4 + 1) * 128],
                        cg_t0[gb][:, dk * 128:(dk + 1) * 128]),
                        inc=dv if dk == DK - 1 else None)
                t_pmul[g] = dv.n
                ob = outg[gb]
                p.wait("vector", dv, t_pmul[g])
                p.op("vector", lambda v, gb=gb, ob=ob: v.tensor_add(
                    ob, cg_t0[gb], xstage[gb]), inc=dv)
                t_add = dv.n
                p.wait("scalar", dv, t_add)
                p.op("scalar", lambda s, h=h, g=g, ob=ob: s.dma_start(
                    out=out_d[h, g * 128:(g + 1) * 128, :], in_=ob), inc=so[gb], k=16)
                t_pstore[g] = so[gb].n

        p.wait("sync", so[0], so[0].n)
        p.wait("sync", so[1], so[1].n)

        p.install(block)
    return nc


def kernel(L0_attn_x_seq, L0_attn_W_decay, L0_attn_b_decay, L0_attn_W_input,
           L0_attn_b_input, L0_ffn_W1, L0_ffn_b1, L0_ffn_W2, L0_ffn_b2,
           L0_ffn_Wr, L0_ffn_br):
    x = np.ascontiguousarray(np.asarray(L0_attn_x_seq, dtype=np.float32))
    wd = np.ascontiguousarray(np.asarray(L0_attn_W_decay, dtype=np.float32))
    bd = np.ascontiguousarray(np.asarray(L0_attn_b_decay, dtype=np.float32))
    wi = np.ascontiguousarray(np.asarray(L0_attn_W_input, dtype=np.float32))
    bi = np.ascontiguousarray(np.asarray(L0_attn_b_input, dtype=np.float32))
    w1 = np.ascontiguousarray(np.asarray(L0_ffn_W1, dtype=np.float32))
    b1 = np.ascontiguousarray(np.asarray(L0_ffn_b1, dtype=np.float32))
    w2 = np.ascontiguousarray(np.asarray(L0_ffn_W2, dtype=np.float32))
    b2 = np.ascontiguousarray(np.asarray(L0_ffn_b2, dtype=np.float32))
    wr = np.ascontiguousarray(np.asarray(L0_ffn_Wr, dtype=np.float32))
    br = np.ascontiguousarray(np.asarray(L0_ffn_br, dtype=np.float32))

    n_cores = 8
    nc = build()
    in_maps = []
    for c in range(n_cores):
        in_maps.append(dict(
            x=x[2 * c:2 * c + 2], wd=wd, bd=bd, wi=wi, bi=bi,
            w1=w1, b1=b1, w2=w2, b2=b2, wr=wr, br=br))
    res = run_bass_kernel_spmd(nc, in_maps, core_ids=list(range(n_cores)))
    out = np.concatenate(
        [np.asarray(r["out"], dtype=np.float32).reshape(2, 2048, 1024) for r in res.results],
        axis=0)
    return out


# version 20 (final)
# speedup vs baseline: 1.2624x; 1.0109x over previous
import os
from contextlib import ExitStack

import numpy as np
import concourse.bass as bass
import concourse.mybir as mybir
from concourse.bass import IndirectOffsetOnAxis
from concourse.bass_utils import run_bass_kernel_spmd

F32 = mybir.dt.float32
BF16 = mybir.dt.bfloat16
U32 = mybir.dt.uint32
AF = mybir.ActivationFunctionType
ALU = mybir.AluOpType
AX = mybir.AxisListType

ENGINES = ("sync", "scalar", "vector", "tensor", "gpsimd")
EPS = 1e-5


class Sem:
    def __init__(self, nc, ctx, name):
        self.h = ctx.enter_context(nc.semaphore(name))
        self.n = 0

    def inc(self, k=1):
        self.n += k
        return self.n


class Prog:
    def __init__(self, nc, ctx):
        self.nc = nc
        self.ctx = ctx
        self.q = {e: [] for e in ENGINES}

    def sem(self, name):
        return Sem(self.nc, self.ctx, name)

    def op(self, engine, fn, inc=None, k=1):
        def wrapped(eng, fn=fn, inc=inc, k=k):
            inst = fn(eng)
            if inc is not None:
                inst.then_inc(inc.h, k)
        self.q[engine].append(wrapped)
        return inc.inc(k) if inc is not None else None

    def seq(self, engine, sem, fn):
        self.wait(engine, sem, sem.n)
        return self.op(engine, fn, inc=sem)

    def wait(self, engine, sem, val):
        if val > 0:
            self.q[engine].append(lambda eng, s=sem, v=val: eng.wait_ge(s.h, v))

    def install(self, block):
        for name in ENGINES:
            fns = self.q[name]
            getattr(block, name)(lambda eng, fns=fns: [fn(eng) for fn in fns])


def build(D=1024, HH=4096, E=8, T=2048, NSEQ=2, HS=1024, W=512, CAP=512, stage="ALL"):
    SL = {"INIT": 0, "A0": 1, "FFN": 2, "ALL": 5}[stage]
    DK = D // 128
    HJ = HH // 128
    NHS = HH // HS
    HJR = HS // 128
    NG = T // 128
    DH = max(1, D // 512)
    DW = min(512, D)
    NCG = CAP // 128
    SLOTS = E * CAP
    GPW = W // 128
    CHS = min(512, HS)
    CHJR = CHS // 128
    NCB = HH // CHS * E
    CPG = max(1, NCB // (NSEQ * NG))
    assert CAP == W

    nc = bass.Bass("TRN2", target_bir_lowering=False,
                   detect_race_conditions=os.environ.get("BASS_RD", "0") == "1")
    x_d = nc.dram_tensor("x", [NSEQ, T, D], F32, kind="ExternalInput").ap()
    wd_d = nc.dram_tensor("wd", [D, D], F32, kind="ExternalInput").ap()
    bd_d = nc.dram_tensor("bd", [D], F32, kind="ExternalInput").ap()
    wi_d = nc.dram_tensor("wi", [D, D], F32, kind="ExternalInput").ap()
    bi_d = nc.dram_tensor("bi", [D], F32, kind="ExternalInput").ap()
    w1_d = nc.dram_tensor("w1", [E, D, HH], F32, kind="ExternalInput").ap()
    b1_d = nc.dram_tensor("b1", [E, HH], F32, kind="ExternalInput").ap()
    w2_d = nc.dram_tensor("w2", [E, HH, D], F32, kind="ExternalInput").ap()
    b2_d = nc.dram_tensor("b2", [E, D], F32, kind="ExternalInput").ap()
    wr_d = nc.dram_tensor("wr", [D, E], F32, kind="ExternalInput").ap()
    br_d = nc.dram_tensor("br", [E], F32, kind="ExternalInput").ap()
    out_d = nc.dram_tensor("out", [NSEQ, T, D], F32, kind="ExternalOutput").ap()
    xn_scr = [nc.dram_tensor(f"xn_scr{k}", [T + 128, D], F32, kind="Internal").ap()
              for k in range(NSEQ)]
    g_scr = [nc.dram_tensor(f"g_scr{k}", [T + 128, D], BF16, kind="Internal").ap()
             for k in range(NSEQ)]
    s_scr = nc.dram_tensor("s_scr", [NSEQ, 128, DK * T], BF16, kind="Internal").ap()
    tok_scr = [nc.dram_tensor(f"tok_scr{k}", [SLOTS + T + 128], U32, kind="Internal").ap()
               for k in range(NSEQ)]

    with ExitStack() as ctx:
        en = ctx.enter_context
        identf = en(nc.sbuf_tensor("identf", [128, 128], F32))
        iota8 = en(nc.sbuf_tensor("iota8", [128, 8], F32))
        rev8 = en(nc.sbuf_tensor("rev8", [128, 8], F32))
        ones8f = en(nc.sbuf_tensor("ones8f", [E, 1], F32))
        ones8b = en(nc.sbuf_tensor("ones8b", [E, 1], BF16))
        onesf = en(nc.sbuf_tensor("onesf", [1, 128], F32))
        wr_sb = en(nc.sbuf_tensor("wr_sb", [128, DK * E], F32))
        br_row = en(nc.sbuf_tensor("br_row", [1, E], F32))
        brbc = en(nc.sbuf_tensor("brbc", [128, E], F32))
        b1c = en(nc.sbuf_tensor("b1c", [128, E * HJ], F32))
        rb1c = en(nc.sbuf_tensor("rb1c", [128, E * HJ], BF16))
        bdc = en(nc.sbuf_tensor("bdc", [128, DK], F32))
        bic = en(nc.sbuf_tensor("bic", [128, DK], F32))
        ckc = en(nc.sbuf_tensor("ckc", [128, E * DK], F32))
        mrows = en(nc.sbuf_tensor("mrows", [E, T], BF16))
        stats = en(nc.sbuf_tensor("stats", [128, 16], F32))
        r_sb = en(nc.sbuf_tensor("r_sb", [128, 16], F32))
        eqt = en(nc.sbuf_tensor("eqt", [128, 16], F32))
        idxs = en(nc.sbuf_tensor("idxs", [128, 16], F32))
        mtok = en(nc.sbuf_tensor("mtok", [128, 16], F32))
        capoff = en(nc.sbuf_tensor("capoff", [E, 1], F32))
        scatf = en(nc.sbuf_tensor("scatf", [128, NG], F32))
        scatidx = en(nc.sbuf_tensor("scatidx", [128, NG], U32))
        tokids = en(nc.sbuf_tensor("tokids", [128, NG], U32))
        tokinit = en(nc.sbuf_tensor("tokinit", [128, 1], U32))
        idxcols = en(nc.sbuf_tensor("idxcols", [128, NSEQ * SLOTS // 128], U32))
        b2rows = en(nc.sbuf_tensor("b2rows", [E, D], BF16))
        c_rows = en(nc.sbuf_tensor("c_rows", [E, D], F32))
        rowst = en(nc.sbuf_tensor("rowst", [1, D], F32))
        wdt_sb = en(nc.sbuf_tensor("wdt_sb", [128, 2 * DK * D], BF16))
        xnw = en(nc.sbuf_tensor("xnw", [128, DK * W], BF16))
        AQ = DK * HS
        arena = en(nc.sbuf_tensor("arena", [128, 4 * AQ], BF16))
        w1b = [arena[:, 0 * AQ:1 * AQ], arena[:, 1 * AQ:2 * AQ]]
        w2b = [arena[:, 2 * AQ:3 * AQ], arena[:, 3 * AQ:4 * AQ]]
        dec_a = arena[:, 0:DK * T]
        inj_a = arena[:, 2 * AQ:2 * AQ + DK * T]
        h_big = en(nc.sbuf_tensor("h_big", [128, HJR * W], BF16))
        zrow = h_big[:, 0:D]
        out_exp = [en(nc.sbuf_tensor(f"out_exp{k}", [128, DK * CAP], F32)) for k in range(NSEQ)]
        xi_sl = [en(nc.sbuf_tensor(f"xi_sl{k}", [128, DK * CAP], BF16)) for k in range(NSEQ)]
        cwb = [xi_sl[0][:, 0:CHJR * D], xi_sl[1][:, 0:CHJR * D]]
        gstg = [en(nc.sbuf_tensor(f"gstg{k}", [128, D], F32)) for k in range(2)]
        gt_st = [en(nc.sbuf_tensor(f"gt_st{k}", [128, D], BF16)) for k in range(2)]
        scr = en(nc.sbuf_tensor("scr", [128, 4 * D], F32))
        xstage = [scr[:, 0:D], scr[:, D:2 * D]]
        xstage4 = [xstage[0], xstage[1], gstg[0][:], gstg[1][:]]
        sqv = scr[:, 2 * D:3 * D]
        xnTf = scr[:, 3 * D:4 * D]
        cums = scr[0:E, 0:T]
        slotrow = scr[0:1, 2 * D:2 * D + T]
        outg = [scr[:, 2 * D:3 * D], scr[:, 3 * D:4 * D]]
        cg_t0 = [xi_sl[0][:, 0:D], xi_sl[0][:, D:2 * D]]
        cs_t0 = [xi_sl[1][:, 0:D], xi_sl[1][:, D:2 * D]]
        pbuf = en(nc.sbuf_tensor("pbuf", [128, 256], F32))

        pb = [en(nc.psum_tensor(f"pb{k}", [128, 512], F32)) for k in range(8)]

        block = en(nc.Block())
        p = Prog(nc, ctx)
        cld = p.sem("cld")
        wtc = p.sem("wtc")
        crw = p.sem("crw")
        xl = [p.sem("xl0"), p.sem("xl1")]
        gw = [p.sem("gw0"), p.sem("gw1")]
        so = [p.sem("so0"), p.sem("so1")]
        xns = [p.sem("xns0"), p.sem("xns1")]
        ssp = [p.sem("ssp0"), p.sem("ssp1")]
        cgl = [p.sem("cgl0"), p.sem("cgl1")]
        csl = [p.sem("csl0"), p.sem("csl1")]
        ga = [p.sem("ga0"), p.sem("ga1"), p.sem("ga2"), p.sem("ga3")]
        scg = [p.sem("scg0"), p.sem("scg1")]
        idp = p.sem("idp")
        xdp = p.sem("xdp")
        b2l = p.sem("b2l")
        ids = [p.sem("ids0"), p.sem("ids1")]
        idr = [p.sem("idr0"), p.sem("idr1")]
        pe = p.sem("pe")
        sc = p.sem("sc")
        dv = p.sem("dv")
        gp = p.sem("gp")

        p.op("scalar", lambda s: s.dma_start(
            out=wr_sb[:].rearrange("p (dk e) -> p dk e", dk=DK),
            in_=wr_d.rearrange("(dk p) e -> p dk e", p=128)), inc=cld, k=16)
        p.op("scalar", lambda s: s.dma_start(out=br_row[:], in_=br_d[None, :]), inc=cld, k=16)
        p.op("gpsimd", lambda g: g.dma_start(out=b2rows[:], in_=b2_d), inc=b2l, k=16)
        b1st = out_exp[1][0:HJ, 0:E * 128]
        bdst = out_exp[0][0:DK, 0:128]
        bist = out_exp[0][0:DK, 128:256]
        for i in range(E):
            p.op("scalar", lambda s, i=i: s.dma_start(
                out=b1st[:, i * 128:(i + 1) * 128],
                in_=b1_d[i].rearrange("(hj p) -> hj p", p=128)), inc=cld, k=16)
        p.op("scalar", lambda s: s.dma_start(
            out=bdst, in_=bd_d.rearrange("(ek p) -> ek p", p=128)), inc=cld, k=16)
        p.op("scalar", lambda s: s.dma_start(
            out=bist, in_=bi_d.rearrange("(ek p) -> ek p", p=128)), inc=cld, k=16)
        t_const = cld.n

        p.op("gpsimd", lambda g: g.iota(iota8[:], pattern=[[1, 8]], base=0, channel_multiplier=0,
                                        allow_small_or_imprecise_dtypes=True))
        p.op("gpsimd", lambda g: g.iota(rev8[:], pattern=[[-1, 8]], base=8, channel_multiplier=0,
                                        allow_small_or_imprecise_dtypes=True))
        p.op("gpsimd", lambda g: g.memset(ones8f[:], 1.0))
        p.op("gpsimd", lambda g: g.memset(ones8b[:], 1.0))
        p.op("gpsimd", lambda g: g.memset(onesf[:], 1.0))
        p.op("gpsimd", lambda g: g.memset(zrow, 0.0))
        p.op("gpsimd", lambda g: g.memset(sqv, 0.0))
        p.op("gpsimd", lambda g: g.iota(capoff[:], pattern=[[1, 1]], base=-1, channel_multiplier=CAP,
                                        allow_small_or_imprecise_dtypes=True))
        p.op("gpsimd", lambda g: g.iota(tokids[:], pattern=[[128, NG]], base=0, channel_multiplier=1,
                                        allow_small_or_imprecise_dtypes=True))
        p.op("gpsimd", lambda g: g.iota(tokinit[:], pattern=[[1, 1]], base=T,
                                        channel_multiplier=1, allow_small_or_imprecise_dtypes=True))
        p.op("gpsimd", lambda g: g.iota(identf[:], pattern=[[-1, 128]], base=0, channel_multiplier=1,
                                        allow_small_or_imprecise_dtypes=True), inc=gp)
        t_gpinit = gp.n
        p.wait("vector", gp, t_gpinit)
        p.op("vector", lambda v: v.tensor_single_scalar(identf[:], identf[:], 0.0, op=ALU.is_equal), inc=dv)
        t_ident = dv.n

        p.wait("gpsimd", gp, t_gpinit)
        for h in range(NSEQ):
            p.op("gpsimd", lambda g, h=h: g.dma_start(
                out=xn_scr[h][T:T + 128, :], in_=sqv), inc=xdp, k=16)
        prefill_q = []
        for h in range(NSEQ):
            for c in range(SLOTS // 128):
                prefill_q.append(lambda g, h=h, c=c: g.dma_start(
                    out=tok_scr[h][c * 128:(c + 1) * 128][:, None], in_=tokinit[:, 0:1]))
            for g0 in range(NG):
                prefill_q.append(lambda g, h=h, g0=g0: g.dma_start(
                    out=g_scr[h][g0 * 128:(g0 + 1) * 128, :], in_=zrow))
        PPG = (len(prefill_q) + NG - 1) // NG
        t_prefill = None

        p.wait("tensor", cld, t_const)
        p.wait("tensor", dv, t_ident)
        for i in range(E):
            p.op("tensor", lambda te, i=i: te.transpose(
                pb[3][:, i * HJ:(i + 1) * HJ], b1st[:, i * 128:(i + 1) * 128],
                identf[0:HJ, 0:HJ]))
        p.op("tensor", lambda te: te.transpose(
            pb[3][:, E * HJ:E * HJ + DK], bdst, identf[0:DK, 0:DK]))
        p.op("tensor", lambda te: te.transpose(
            pb[3][:, E * HJ + DK:E * HJ + 2 * DK], bist, identf[0:DK, 0:DK]), inc=pe)
        t_b1tr = pe.n
        p.wait("vector", pe, t_b1tr)
        p.seq("vector", dv, lambda v: v.tensor_copy(b1c[:], pb[3][:, 0:E * HJ]))
        p.seq("vector", dv, lambda v: v.tensor_copy(bdc[:], pb[3][:, E * HJ:E * HJ + DK]))
        p.seq("vector", dv, lambda v: v.tensor_copy(bic[:], pb[3][:, E * HJ + DK:E * HJ + 2 * DK]))
        t_b1c = dv.n
        p.wait("scalar", dv, t_b1c)
        p.op("scalar", lambda s: s.activation(rb1c[:], b1c[:], AF.Relu), inc=sc)
        t_rb1 = sc.n

        p.wait("tensor", cld, t_const)
        p.wait("tensor", gp, t_gpinit)
        p.op("tensor", lambda te: te.matmul(pb[2][:, 0:E], lhsT=onesf[:], rhs=br_row[:],
                                            start=True, stop=True), inc=pe)
        t_brmm = pe.n
        p.wait("vector", pe, t_brmm)
        p.seq("vector", dv, lambda v: v.tensor_copy(brbc[:], pb[2][:, 0:E]))
        t_brbc = dv.n

        w_srcs = (wd_d, wi_d)
        t_wtld = {}
        for m in range(2):
            for ek in range(DK):
                ci = m * DK + ek
                p.wait("sync", pe, t_wtld.get(ci - 2, 0))
                p.op("sync", lambda s, m=m, ek=ek, ci=ci: s.dma_start(
                    out=xstage[ci % 2], in_=w_srcs[m][ek * 128:(ek + 1) * 128, :]),
                    inc=xl[ci % 2], k=16)
                t_ch = xl[ci % 2].n
                p.wait("tensor", xl[ci % 2], t_ch)
                p.wait("tensor", dv, dv.n)
                if ci == 0:
                    p.wait("tensor", dv, t_ident)
                for dk in range(DK):
                    p.op("tensor", lambda te, dk=dk, ci=ci: te.transpose(
                        pb[dk // 4][:, (dk % 4) * 128:(dk % 4 + 1) * 128],
                        xstage[ci % 2][:, dk * 128:(dk + 1) * 128], identf[:]),
                        inc=pe if dk == DK - 1 else None)
                t_wtld[ci] = pe.n
                p.wait("vector", pe, pe.n)
                p.wait("vector", dv, dv.n)
                for dk in range(DK):
                    p.op("vector", lambda v, dk=dk, m=m, ek=ek: v.tensor_copy(
                        wdt_sb[:, m * DK * D + dk * D + ek * 128:m * DK * D + dk * D + (ek + 1) * 128],
                        pb[dk // 4][:, (dk % 4) * 128:(dk % 4 + 1) * 128]),
                        inc=dv if dk == DK - 1 else None)
        t_wdt = dv.n

        def wdt_sl(m, dk, ek):
            o = m * DK * D + dk * D + ek * 128
            return wdt_sb[:, o:o + 128]

        cblk = 0
        t_cw_used = {}
        t_crow = 0
        t_idxr = {0: 0, 1: 0}
        t_spill = {0: 0, 1: 0}
        t_xn_ev = 0
        t_bank_sc = 0
        t_bank_dv = 0

        def emit_cpass_block():
            nonlocal cblk, t_crow
            if cblk >= NCB:
                return
            i, cb = divmod(cblk, HH // CHS)
            buf = cblk % 2
            p.wait("gpsimd", pe, t_cw_used.get(cblk - 2, 0))
            p.op("gpsimd", lambda g, i=i, cb=cb, buf=buf: g.dma_start(
                out=cwb[buf].rearrange("p (hjr d) -> p hjr d", hjr=CHJR),
                in_=w2_d[i, cb * CHS:(cb + 1) * CHS, :].rearrange("(hjr p) d -> p hjr d", p=128)),
                inc=gw[buf], k=16)
            t_w = gw[buf].n
            p.wait("tensor", gw[buf], t_w)
            if cblk == 0:
                p.wait("tensor", sc, t_rb1)
            if cb == 0:
                p.wait("tensor", dv, t_crow)
            for hjr in range(CHJR):
                hj = cb * CHJR + hjr
                for dh in range(DH):
                    p.op("tensor", lambda te, i=i, hj=hj, hjr=hjr, dh=dh, buf=buf: te.matmul(
                        pb[6 + dh][0:1, 0:DW],
                        lhsT=rb1c[:, i * HJ + hj:i * HJ + hj + 1],
                        rhs=cwb[buf][:, hjr * D + dh * 512:hjr * D + dh * 512 + DW],
                        start=(hj == 0), stop=(hj == HJ - 1)),
                        inc=pe if (hjr == CHJR - 1 and dh == DH - 1) else None)
            t_cw_used[cblk] = pe.n
            if cb == HH // CHS - 1:
                t_cmm = pe.n
                p.wait("vector", pe, t_cmm)
                p.wait("vector", crw, crw.n)
                p.wait("vector", dv, dv.n)
                for dh in range(DH):
                    p.op("vector", lambda v, dh=dh: v.tensor_copy(
                        rowst[:, dh * 512:dh * 512 + DW], pb[6 + dh][0:1, 0:DW]),
                        inc=dv if dh == DH - 1 else None)
                t_crow = dv.n
                p.wait("sync", dv, t_crow)
                p.op("sync", lambda s, i=i: s.dma_start(out=c_rows[i:i + 1, :], in_=rowst[:]),
                     inc=crw, k=16)
            cblk += 1

        tB = {}
        t_ln = {}
        t_sdd = {}
        t_ssmw_pe = 0
        t_ssm_sc = {0: 0, 1: 0}
        t_ssm_dv = {0: 0, 1: 0}
        t_idx_dv = 0
        t_idx_pe = 0
        for h in range(NSEQ):
            gi0 = h * NG
            p.wait("sync", xns[0], xns[0].n)
            p.wait("sync", pe, tB.get((gi0 - 4, 'tr'), 0))
            p.wait("sync", pe, t_wtld.get(2 * DK - 2, 0))
            p.wait("sync", pe, t_idx_pe)
            p.op("sync", lambda s, h=h: s.dma_start(
                out=xstage4[0], in_=x_d[h, 0:128, :]), inc=xl[0], k=16)
            xg_t = [xl[0].n]
            t_spl = {}

            def emit_LNa(g, h=h):
                for _ in range(PPG):
                    if prefill_q:
                        p.op("gpsimd", prefill_q.pop(0), inc=idp, k=16)
                par = g % 2
                gi = h * NG + g
                st = stats[:, par * 8:par * 8 + 8]
                xg_g = xstage4[g % 4]
                sq_g = sqv if par == 0 else out_exp[0][:, D:2 * D]
                if g + 1 < NG:
                    p.wait("sync", xns[(g + 1) % 2], xns[(g + 1) % 2].n)
                    p.wait("sync", pe, tB.get((gi - 3, 'tr'), 0))
                    if gi == 0:
                        p.wait("sync", pe, t_wtld[2 * DK - 1])
                    p.op("sync", lambda s, g=g: s.dma_start(
                        out=xstage4[(g + 1) % 4], in_=x_d[h, (g + 1) * 128:(g + 2) * 128, :]),
                        inc=xl[(g + 1) % 2], k=16)
                    xg_t.append(xl[(g + 1) % 2].n)
                p.wait("vector", xl[par], xg_t[g])
                p.wait("vector", sc, t_sdd.get(gi - 2, 0))
                if g == 0 and h == 0:
                    p.wait("vector", xdp, 32)
                p.seq("vector", dv, lambda v: v.reduce_sum(st[:, 0:1], xg_g, axis=AX.X))
                p.seq("vector", dv, lambda v: v.tensor_mul(sq_g, xg_g, xg_g))
                p.seq("vector", dv, lambda v: v.reduce_sum(st[:, 1:2], sq_g, axis=AX.X))
                p.seq("vector", dv, lambda v: v.tensor_scalar_mul(st[:, 0:1], st[:, 0:1], 1.0 / D))
                p.seq("vector", dv, lambda v: v.tensor_mul(st[:, 7:8], st[:, 0:1], st[:, 0:1]))
                p.seq("vector", dv, lambda v: v.tensor_scalar(
                    st[:, 2:3], st[:, 1:2], 1.0 / D, None, op0=ALU.mult))
                p.seq("vector", dv, lambda v: v.tensor_tensor(
                    st[:, 2:3], st[:, 2:3], st[:, 7:8], op=ALU.subtract))
                p.seq("vector", dv, lambda v: v.tensor_scalar(
                    st[:, 2:3], st[:, 2:3], EPS, None, op0=ALU.add))
                t_var = dv.n
                p.wait("scalar", dv, t_var)
                p.seq("scalar", sc, lambda s: s.activation(st[:, 3:4], st[:, 2:3], AF.Sqrt))
                t_sdd[gi] = sc.n

            def emit_LNb(g, h=h):
                par = g % 2
                gi = h * NG + g
                st = stats[:, par * 8:par * 8 + 8]
                xg_g = xstage4[g % 4]
                p.wait("vector", sc, t_sdd[gi])
                p.seq("vector", dv, lambda v: v.reciprocal(st[:, 4:5], st[:, 3:4]))
                p.seq("vector", dv, lambda v: v.tensor_scalar(
                    xg_g, xg_g, st[:, 0:1], st[:, 4:5],
                    op0=ALU.subtract, op1=ALU.mult))
                t_xnst = dv.n
                t_ln[gi] = t_xnst
                p.wait("sync", dv, t_xnst)
                p.op("sync", lambda s, g=g, xg_g=xg_g: s.dma_start(
                    out=xn_scr[h][g * 128:(g + 1) * 128, :], in_=xg_g), inc=xns[par], k=16)
                t_spl[g] = xns[par].n

            def emit_Btr(g, h=h):
                par = g % 2
                gi = h * NG + g
                pair = (pb[0], pb[1]) if par == 0 else (pb[2], pb[3])
                xnTf_g = xnTf if par == 0 else out_exp[0][:, 0:D]
                p.wait("tensor", dv, t_ln[gi])
                p.wait("tensor", dv, tB.get((gi - 2, 'evac'), 0))
                p.wait("tensor", sc, tB.get((gi - 2, 'evsc'), 0))
                if par == 1:
                    p.wait("tensor", sc, t_ssm_sc[1])
                    p.wait("tensor", dv, t_ssm_dv[1])
                if h == 0 and g < 2:
                    p.wait("tensor", dv, t_wdt)
                for dk in range(DK):
                    p.op("tensor", lambda te, dk=dk, g=g: te.transpose(
                        pair[dk // 4][:, (dk % 4) * 128:(dk % 4 + 1) * 128],
                        xstage4[g % 4][:, dk * 128:(dk + 1) * 128], identf[:]),
                        inc=pe if dk == DK - 1 else None)
                tB[(gi, 'tr')] = pe.n
                p.wait("scalar", pe, tB[(gi, 'tr')])
                if g % GPW == 0:
                    p.wait("scalar", pe, t_ssmw_pe)
                p.wait("scalar", sc, sc.n)
                gw_ = g % GPW
                SPL = min(4, DK)
                for dk in range(SPL):
                    p.op("scalar", lambda sce, dk=dk, xnTf_g=xnTf_g, pair=pair: sce.copy(
                        xnTf_g[:, dk * 128:(dk + 1) * 128],
                        pair[0][:, (dk % 4) * 128:(dk % 4 + 1) * 128]))
                for dk in range(SPL):
                    p.op("scalar", lambda sce, dk=dk, gw_=gw_, pair=pair: sce.copy(
                        xnw[:, dk * W + gw_ * 128:dk * W + (gw_ + 1) * 128],
                        pair[0][:, (dk % 4) * 128:(dk % 4 + 1) * 128]),
                        inc=sc if dk == SPL - 1 else None)
                tB[(gi, 'evsc')] = sc.n
                p.wait("vector", pe, tB[(gi, 'tr')])
                if g % GPW == 0:
                    p.wait("vector", pe, t_ssmw_pe)
                p.wait("vector", dv, dv.n)
                for dk in range(SPL, DK):
                    p.op("vector", lambda v, dk=dk, xnTf_g=xnTf_g, pair=pair: v.tensor_copy(
                        xnTf_g[:, dk * 128:(dk + 1) * 128],
                        pair[1][:, (dk % 4) * 128:(dk % 4 + 1) * 128]),
                        inc=dv if dk == DK - 1 else None)
                tB[(gi, 'evT')] = dv.n
                for dk in range(SPL, DK):
                    p.op("vector", lambda v, dk=dk, gw_=gw_, pair=pair: v.tensor_copy(
                        xnw[:, dk * W + gw_ * 128:dk * W + (gw_ + 1) * 128],
                        pair[1][:, (dk % 4) * 128:(dk % 4 + 1) * 128]),
                        inc=dv if dk == DK - 1 else None)
                tB[(gi, 'evac')] = dv.n

            def emit_Bmask(g, h=h):
                par = g % 2
                gi = h * NG + g
                rtb = pb[5]
                xnTf_g = xnTf if par == 0 else out_exp[0][:, 0:D]
                st = stats[:, par * 8:par * 8 + 8]
                p.wait("tensor", dv, tB[(gi, 'evT')])
                p.wait("tensor", sc, tB[(gi, 'evsc')])
                p.wait("tensor", dv, tB.get((gi - 1, 'radd'), 0))
                p.wait("tensor", dv, t_ssm_dv[0])
                for dk in range(DK):
                    p.op("tensor", lambda te, dk=dk, xnTf_g=xnTf_g, rtb=rtb: te.matmul(
                        rtb[:, 0:E], lhsT=xnTf_g[:, dk * 128:(dk + 1) * 128],
                        rhs=wr_sb[:, dk * E:(dk + 1) * E],
                        start=(dk == 0), stop=(dk == DK - 1)),
                        inc=pe if dk == DK - 1 else None)
                t_rmm = pe.n
                p.wait("vector", pe, t_rmm)
                if g == 0 and h == 0:
                    p.wait("vector", dv, t_brbc)
                rsb = r_sb[:, par * 8:par * 8 + 8]
                eq = eqt[:, par * 8:par * 8 + 8]
                ix = idxs[:, par * 8:par * 8 + 8]
                mt = mtok[:, par * 8:par * 8 + 8]
                p.seq("vector", dv, lambda v, rtb=rtb: v.tensor_add(rsb[:, 0:E], rtb[:, 0:E], brbc[:]))
                tB[(gi, 'radd')] = dv.n
                p.seq("vector", dv, lambda v: v.reduce_max(st[:, 5:6], rsb[:, 0:E], axis=AX.X))
                p.seq("vector", dv, lambda v: v.tensor_scalar(
                    eq[:, 0:E], rsb[:, 0:E], st[:, 5:6], None, op0=ALU.is_equal))
                p.seq("vector", dv, lambda v: v.tensor_mul(ix[:, 0:E], eq[:, 0:E], rev8[:, 0:E]))
                p.seq("vector", dv, lambda v: v.tensor_scalar(
                    ix[:, 0:E], ix[:, 0:E], -1.0, 8.0, op0=ALU.mult, op1=ALU.add))
                p.seq("vector", dv, lambda v: v.tensor_reduce(
                    st[:, 6:7], ix[:, 0:E], axis=AX.X, op=ALU.min))
                p.seq("vector", dv, lambda v: v.tensor_scalar(
                    mt[:, 0:E], iota8[:, 0:E], st[:, 6:7], None, op0=ALU.is_equal))
                t_mtok = dv.n
                p.wait("tensor", dv, t_mtok)
                p.wait("tensor", dv, tB.get((gi - 1, 'mrows'), 0))
                p.op("tensor", lambda te, mt=mt, rtb=rtb: te.transpose(
                    rtb[0:E, 8:136], mt[:, 0:E], identf[:]), inc=pe)
                t_mtr = pe.n
                p.wait("vector", pe, t_mtr)
                p.seq("vector", dv, lambda v, g=g, rtb=rtb: v.tensor_copy(
                    mrows[:, g * 128:(g + 1) * 128], rtb[0:E, 8:136]))
                tB[(gi, 'mrows')] = dv.n

            def emit_SSMW(w_, h=h):
                nonlocal t_ssmw_pe
                p.wait("tensor", dv, tB[(h * NG + w_ * GPW + GPW - 1, 'evac')])
                p.wait("tensor", sc, tB[(h * NG + w_ * GPW + GPW - 1, 'evsc')])
                p.wait("tensor", dv, t_wdt)
                if w_ == 0 and h > 0:
                    p.wait("tensor", ssp[h - 1], 16)
                    p.wait("tensor", dv, t_idx_dv)
                for m in range(2):
                    banks = (pb[4], pb[2]) if m == 0 else (pb[5], pb[3])
                    if m == 1:
                        p.wait("tensor", dv, tB[(h * NG + w_ * GPW + GPW - 1, 'mrows')])
                    for ek in range(DK):
                        bank = banks[ek % 2]
                        if m == 0:
                            p.wait("tensor", sc, t_ssm_sc[ek % 2])
                        else:
                            p.wait("tensor", dv, t_ssm_dv[ek % 2])
                        for dk in range(DK):
                            p.op("tensor", lambda te, m=m, dk=dk, ek=ek, bank=bank: te.matmul(
                                bank[:, 0:W], lhsT=wdt_sl(m, dk, ek),
                                rhs=xnw[:, dk * W:(dk + 1) * W],
                                start=(dk == 0), stop=(dk == DK - 1)),
                                inc=pe if dk == DK - 1 else None)
                        t_mm = pe.n
                        if m == 0:
                            p.wait("scalar", pe, t_mm)
                            p.wait("scalar", sc, sc.n)
                            p.op("scalar", lambda s, ek=ek, w_=w_, bank=bank: s.activation(
                                dec_a[:, ek * T + w_ * W:ek * T + (w_ + 1) * W], bank[:, 0:W],
                                AF.Sigmoid, bias=bdc[:, ek:ek + 1]), inc=sc)
                            t_ssm_sc[ek % 2] = sc.n
                        else:
                            p.wait("vector", pe, t_mm)
                            p.wait("vector", dv, dv.n)
                            p.op("vector", lambda v, ek=ek, w_=w_, bank=bank: v.tensor_scalar(
                                inj_a[:, ek * T + w_ * W:ek * T + (w_ + 1) * W], bank[:, 0:W],
                                bic[:, ek:ek + 1], None, op0=ALU.add), inc=dv)
                            t_ssm_dv[ek % 2] = dv.n
                t_ssmw_pe = pe.n

            LAG = 2
            for it in range(NG + LAG + 1 if SL >= 1 else 0):
                if it < NG:
                    emit_LNa(it)
                if 0 <= it - 1 < NG:
                    emit_LNb(it - 1)
                gm = it - LAG - 1
                if 0 <= gm < NG:
                    emit_Bmask(gm)
                    if gm % GPW == GPW - 1:
                        emit_SSMW(gm // GPW)
                if LAG <= it < NG + LAG:
                    emit_Btr(it - LAG)
                if 0 <= gm < NG:
                    for _ in range(CPG):
                        emit_cpass_block()
            t_A0 = dv.n
            t_ssm_all_sc = sc.n

            if SL >= 2:
                p.wait("vector", dv, t_A0)
                p.wait("vector", pe, pe.n)
                p.wait("vector", xns[0], xns[0].n)
                p.wait("vector", xns[1], xns[1].n)
                p.seq("vector", dv, lambda v: v.tensor_tensor_scan(
                    cums, mrows[:], mrows[:], 0.0, op0=ALU.add, op1=ALU.max))
                p.seq("vector", dv, lambda v: v.tensor_scalar(
                    cums, cums, capoff[:, 0:1], None, op0=ALU.add))
                p.seq("vector", dv, lambda v: v.tensor_mul(cums, cums, mrows[:]))
                t_slots = dv.n
                p.wait("tensor", dv, t_slots)
                p.wait("tensor", sc, t_ssm_all_sc)
                for n in range(T // 512):
                    p.op("tensor", lambda te, n=n: te.matmul(
                        pb[4][0:1, 0:512], lhsT=ones8f[:], rhs=cums[:, n * 512:(n + 1) * 512],
                        start=True, stop=True), inc=pe)
                    t_sr = pe.n
                    p.wait("vector", pe, t_sr)
                    p.seq("vector", dv, lambda v, n=n: v.tensor_copy(
                        slotrow[:, n * 512:(n + 1) * 512], pb[4][0:1, 0:512]))
                    p.wait("tensor", dv, dv.n)
                t_srow = dv.n
                p.wait("tensor", dv, t_srow)
                for g in range(NG):
                    p.op("tensor", lambda te, g=g: te.transpose(
                        pb[4][:, g:g + 1], slotrow[0:1, g * 128:(g + 1) * 128], identf[0:1, 0:1]),
                        inc=pe if g == NG - 1 else None)
                t_sct = pe.n
                t_idx_pe = t_sct
                p.wait("vector", pe, t_sct)
                p.wait("vector", ids[0], ids[0].n)
                p.wait("vector", ids[1], ids[1].n)
                p.seq("vector", dv, lambda v: v.tensor_copy(scatf[:], pb[4][:, 0:NG]))
                p.seq("vector", dv, lambda v: v.tensor_copy(scatidx[:], scatf[:]))
                t_sidx = dv.n
                t_idx_dv = t_sidx
                p.wait("gpsimd", idp, idp.n)
                p.wait("gpsimd", dv, t_sidx)
                for g in range(NG):
                    p.op("gpsimd", lambda gg, h=h, g=g: gg.indirect_dma_start(
                        out=tok_scr[h][:][:, None],
                        out_offset=IndirectOffsetOnAxis(ap=scatidx[:, g:g + 1], axis=0),
                        in_=tokids[:, g:g + 1], in_offset=None), inc=ids[h], k=16)
                t_scat = ids[h].n
                p.wait("gpsimd", ids[h], t_scat)
                for c in range(SLOTS // 128):
                    p.op("gpsimd", lambda s, h=h, c=c: s.dma_start(
                        out=idxcols[:, h * (SLOTS // 128) + c:h * (SLOTS // 128) + c + 1],
                        in_=tok_scr[h][c * 128:(c + 1) * 128][:, None]), inc=idr[h], k=16)
                t_idxr[h] = idr[h].n

                p.wait("vector", sc, t_ssm_all_sc)
                for ek in range(DK):
                    p.seq("vector", dv, lambda v, ek=ek: v.tensor_tensor_scan(
                        dec_a[:, ek * T:(ek + 1) * T], dec_a[:, ek * T:(ek + 1) * T],
                        inj_a[:, ek * T:(ek + 1) * T], 0.0, op0=ALU.mult, op1=ALU.add))
                t_scan = dv.n
                p.wait("gpsimd", dv, t_scan)
                p.op("gpsimd", lambda s, h=h: s.dma_start(
                    out=s_scr[h], in_=dec_a), inc=ssp[h], k=16)
                t_spill[h] = dv.n

        t_ck = 0
        if SL >= 2:
            p.wait("tensor", crw, crw.n)
            p.wait("tensor", b2l, b2l.n)
            sum8c = stats[:, 8:9]
            for dh in range(DH):
                p.wait("tensor", dv, dv.n)
                p.op("tensor", lambda te, dh=dh: te.matmul(
                    pb[6][0:1, 0:DW], lhsT=ones8f[:], rhs=c_rows[:, dh * 512:dh * 512 + DW],
                    start=True, stop=False))
                p.op("tensor", lambda te, dh=dh: te.matmul(
                    pb[6][0:1, 0:DW], lhsT=ones8b[:], rhs=b2rows[:, dh * 512:dh * 512 + DW],
                    start=False, stop=True), inc=pe)
                t_s8 = pe.n
                p.wait("vector", pe, t_s8)
                p.seq("vector", dv, lambda v, dh=dh: v.tensor_copy(
                    rowst[:, dh * 512:dh * 512 + DW], pb[6][0:1, 0:DW]))
            for dk in range(DK):
                p.wait("tensor", dv, dv.n)
                p.op("tensor", lambda te, dk=dk: te.transpose(
                    pb[7][:, 0:E], c_rows[0:E, dk * 128:(dk + 1) * 128], identf[0:E, 0:E]))
                p.op("tensor", lambda te, dk=dk: te.transpose(
                    pb[7][:, E:E + 1], rowst[0:1, dk * 128:(dk + 1) * 128], identf[0:1, 0:1]), inc=pe)
                t_st8 = pe.n
                p.wait("vector", pe, t_st8)
                p.seq("vector", dv, lambda v: v.tensor_copy(sum8c, pb[7][:, E:E + 1]))
                p.seq("vector", dv, lambda v, dk=dk: v.tensor_scalar(
                    ckc[:, dk:dk + (E - 1) * DK + 1:DK],
                    pb[7][:, 0:E], sum8c, -1.0, op0=ALU.subtract, op1=ALU.mult))
            t_ck = dv.n

        if SL >= 2:
            p.wait("gpsimd", dv, max(t_spill[0], t_spill[NSEQ - 1]))
            p.wait("gpsimd", ssp[0], 16)
            p.wait("gpsimd", ssp[NSEQ - 1], 16)
            p.wait("gpsimd", pe, pe.n)
            p.op("gpsimd", lambda g: g.dma_start(
                out=w1b[0].rearrange("p (dk q) -> p dk q", dk=DK),
                in_=w1_d[0, :, 0:HS].rearrange("(dk p) q -> p dk q", p=128)), inc=gw[0], k=16)
            p.op("gpsimd", lambda g: g.dma_start(
                out=w2b[0].rearrange("p (hjr d) -> p hjr d", hjr=HJR),
                in_=w2_d[0, 0:HS, :].rearrange("(hjr p) d -> p hjr d", p=128)), inc=gw[0], k=16)
            wblk_t = {0: (0, gw[0].n)}

            PH = ([4, 5], [0, 1])
            AB = ([0, 1, 2, 3], [4, 5, 6, 7])
            t_evac_a = 0
            t_evac_b = 0
            t_blk_lastmm = {}
            t_gstg_pe = {0: 0, 1: 0, 2: 0, 3: 0}
            t_scg = {0: 0, 1: 0}
            t_xi_all = {}
            t_sctr_last = 0
            t_gev_last = 0
            it = 0
            gstg4f = [gstg[0][:], gstg[1][:], xstage[0], xstage[1]]
            t_trpair = {0: 0, 1: 0}

            def emit_gathers(i):
                t_trpair[0] = max(t_trpair[0], dv.n)
                t_trpair[1] = max(t_trpair[1], dv.n)
                for hw in range(NSEQ):
                    for gg in range(NCG):
                        idx = hw * NCG + gg
                        gb = idx % 4
                        pr = idx % 2
                        base = 2 - 2 * pr
                        p.wait("gpsimd", idr[hw], t_idxr[hw])
                        p.wait("gpsimd", xns[0], xns[0].n)
                        p.wait("gpsimd", xns[1], xns[1].n)
                        p.wait("gpsimd", pe, t_gstg_pe.get(gb, 0))
                        p.op("gpsimd", lambda g2, i=i, hw=hw, gg=gg, gb=gb: g2.indirect_dma_start(
                            out=gstg4f[gb], out_offset=None,
                            in_=xn_scr[hw], in_offset=IndirectOffsetOnAxis(
                                ap=idxcols[:, hw * (SLOTS // 128) + i * NCG + gg:
                                           hw * (SLOTS // 128) + i * NCG + gg + 1], axis=0)),
                            inc=ga[gb], k=16)
                        t_g = ga[gb].n
                        p.wait("tensor", ga[gb], t_g)
                        p.wait("tensor", dv, t_trpair[pr])
                        for dk in range(DK):
                            p.op("tensor", lambda te, dk=dk, gb=gb, base=base: te.transpose(
                                pb[base + dk // 4][:, (dk % 4) * 128:(dk % 4 + 1) * 128],
                                gstg4f[gb][:, dk * 128:(dk + 1) * 128], identf[:]),
                                inc=pe if dk == DK - 1 else None)
                        t_xtr = pe.n
                        t_gstg_pe[gb] = t_xtr
                        p.wait("vector", pe, t_xtr)
                        p.wait("vector", dv, dv.n)
                        for dk in range(DK):
                            p.op("vector", lambda v, dk=dk, hw=hw, gg=gg, base=base: v.tensor_copy(
                                xi_sl[hw][:, dk * CAP + gg * 128:dk * CAP + (gg + 1) * 128],
                                pb[base + dk // 4][:, (dk % 4) * 128:(dk % 4 + 1) * 128]),
                                inc=dv if dk == DK - 1 else None)
                        t_trpair[pr] = dv.n
                    t_xi_all[(i, hw)] = dv.n

            emit_gathers(0)
            for i in range(E):
                for hs in range(NHS):
                    blk = i * NHS + hs
                    buf = blk % 2
                    nblk_i, nblk_hs = (i, hs + 1) if hs + 1 < NHS else (i + 1, 0)
                    if nblk_i < E:
                        nbuf = (blk + 1) % 2
                        p.wait("gpsimd", pe, t_blk_lastmm.get(blk - 1, 0))
                        p.op("gpsimd", lambda g, i2=nblk_i, hs2=nblk_hs, nbuf=nbuf: g.dma_start(
                            out=w1b[nbuf].rearrange("p (dk q) -> p dk q", dk=DK),
                            in_=w1_d[i2, :, hs2 * HS:(hs2 + 1) * HS].rearrange("(dk p) q -> p dk q", p=128)),
                            inc=gw[nbuf], k=16)
                        p.op("gpsimd", lambda g, i2=nblk_i, hs2=nblk_hs, nbuf=nbuf: g.dma_start(
                            out=w2b[nbuf].rearrange("p (hjr d) -> p hjr d", hjr=HJR),
                            in_=w2_d[i2, hs2 * HS:(hs2 + 1) * HS, :].rearrange("(hjr p) d -> p hjr d", p=128)),
                            inc=gw[nbuf], k=16)
                        wblk_t[blk + 1] = (nbuf, gw[nbuf].n)

                    for hw in range(NSEQ):
                        par = it % 2
                        ph = PH[par]
                        ab = AB[par]
                        bb = AB[1 - par]
                        wbuf_s, wbuf_t = wblk_t[blk]
                        p.wait("tensor", gw[wbuf_s], wbuf_t)
                        p.wait("tensor", dv, t_xi_all[(i, hw)])
                        t_relu = {}
                        for hjr in range(HJR):
                            bank = ph[hjr % 2]
                            if hjr < 2:
                                p.wait("tensor", dv, t_evac_a)
                            else:
                                p.wait("tensor", sc, t_relu[hjr - 2])
                            for dk in range(DK):
                                p.op("tensor", lambda te, dk=dk, hjr=hjr, buf=buf, hw=hw, bank=bank: te.matmul(
                                    pb[bank][:, 0:W],
                                    lhsT=w1b[buf][:, dk * HS + hjr * 128:dk * HS + (hjr + 1) * 128],
                                    rhs=xi_sl[hw][:, dk * CAP:dk * CAP + W],
                                    start=(dk == 0), stop=(dk == DK - 1)),
                                    inc=pe if dk == DK - 1 else None)
                            t_mm1 = pe.n
                            hj_abs = hs * HJR + hjr
                            p.wait("scalar", pe, t_mm1)
                            p.wait("scalar", sc, sc.n)
                            p.op("scalar", lambda s, hjr=hjr, i=i, hj_abs=hj_abs, bank=bank: s.activation(
                                h_big[:, hjr * W:(hjr + 1) * W], pb[bank][:, 0:W], AF.Relu,
                                bias=b1c[:, i * HJ + hj_abs:i * HJ + hj_abs + 1]), inc=sc)
                            t_relu[hjr] = sc.n
                        p.wait("tensor", dv, t_evac_b)
                        if hs == 0 and hw == 0:
                            p.wait("tensor", dv, t_gev_last)
                        for hjr in range(HJR):
                            p.wait("tensor", sc, t_relu[hjr])
                            for dk in range(DK // 2):
                                p.op("tensor", lambda te, dk=dk, hjr=hjr, buf=buf, ab=ab: te.matmul(
                                    pb[ab[dk]][:, 0:W],
                                    lhsT=w2b[buf][:, hjr * D + dk * 128:hjr * D + (dk + 1) * 128],
                                    rhs=h_big[:, hjr * W:(hjr + 1) * W],
                                    start=(hjr == 0), stop=(hjr == HJR - 1)),
                                    inc=pe if (hjr == HJR - 1 and dk == DK // 2 - 1) else None)
                        t_mm2a = pe.n
                        for hjr in range(HJR):
                            for dk in range(DK // 2, DK):
                                p.op("tensor", lambda te, dk=dk, hjr=hjr, buf=buf, bb=bb: te.matmul(
                                    pb[bb[dk - DK // 2]][:, 0:W],
                                    lhsT=w2b[buf][:, hjr * D + dk * 128:hjr * D + (dk + 1) * 128],
                                    rhs=h_big[:, hjr * W:(hjr + 1) * W],
                                    start=(hjr == 0), stop=(hjr == HJR - 1)),
                                    inc=pe if (hjr == HJR - 1 and dk == DK - 1) else None)
                        t_mm2b = pe.n
                        t_blk_lastmm[blk] = t_mm2b

                        def emit_evac(banks, dk0, t_mm, hs=hs, hw=hw):
                            p.wait("vector", pe, t_mm)
                            p.wait("vector", dv, dv.n)
                            for j in range(DK // 2):
                                dk = dk0 + j
                                o0 = dk * CAP
                                if hs == 0:
                                    p.op("vector", lambda v, j=j, o0=o0, banks=banks, hw=hw: v.tensor_copy(
                                        out_exp[hw][:, o0:o0 + W], pb[banks[j]][:, 0:W]),
                                        inc=dv if j == DK // 2 - 1 else None)
                                else:
                                    p.op("vector", lambda v, j=j, o0=o0, banks=banks, hw=hw: v.tensor_add(
                                        out_exp[hw][:, o0:o0 + W], pb[banks[j]][:, 0:W],
                                        out_exp[hw][:, o0:o0 + W]),
                                        inc=dv if j == DK // 2 - 1 else None)
                            return dv.n

                        t_evac_a = emit_evac(ab, 0, t_mm2a)
                        t_evac_b = emit_evac(bb, DK // 2, t_mm2b)
                        it += 1

                if i + 1 < E:
                    emit_gathers(i + 1)
                p.wait("scalar", dv, t_evac_b)
                p.wait("scalar", dv, t_ck)
                p.wait("scalar", sc, sc.n)
                t_sil = {}
                for hw in range(NSEQ):
                    for dk in range(DK):
                        p.op("scalar", lambda s, dk=dk, i=i, hw=hw: s.activation(
                            out_exp[hw][:, dk * CAP:(dk + 1) * CAP],
                            out_exp[hw][:, dk * CAP:(dk + 1) * CAP],
                            AF.Silu, bias=ckc[:, i * DK + dk:i * DK + dk + 1]),
                            inc=sc if dk == DK - 1 else None)
                    t_sil[hw] = sc.n
                for hw in range(NSEQ):
                    for gg in range(NCG):
                        idx = hw * NCG + gg
                        gb = idx % 2
                        pr = idx % 2
                        base = 2 - 2 * pr
                        p.wait("tensor", sc, t_sil[hw])
                        p.wait("tensor", dv, t_trpair[pr])
                        for dk in range(DK):
                            p.op("tensor", lambda te, dk=dk, gg=gg, hw=hw, base=base: te.transpose(
                                pb[base + dk // 4][:, (dk % 4) * 128:(dk % 4 + 1) * 128],
                                out_exp[hw][:, dk * CAP + gg * 128:dk * CAP + (gg + 1) * 128],
                                identf[:]),
                                inc=pe if dk == DK - 1 else None)
                        t_gtr = pe.n
                        t_sctr_last = t_gtr
                        p.wait("vector", pe, t_gtr)
                        p.wait("vector", dv, dv.n)
                        p.wait("vector", scg[gb], t_scg[gb])
                        for dk in range(DK):
                            p.op("vector", lambda v, dk=dk, gb=gb, base=base: v.tensor_copy(
                                gt_st[gb][:, dk * 128:(dk + 1) * 128],
                                pb[base + dk // 4][:, (dk % 4) * 128:(dk % 4 + 1) * 128]),
                                inc=dv if dk == DK - 1 else None)
                        t_trpair[pr] = dv.n
                        t_gev = dv.n
                        t_gev_last = t_gev
                        p.wait("gpsimd", dv, t_gev)
                        p.op("gpsimd", lambda g2, i=i, hw=hw, gg=gg, gb=gb: g2.indirect_dma_start(
                            out=g_scr[hw], out_offset=IndirectOffsetOnAxis(
                                ap=idxcols[:, hw * (SLOTS // 128) + i * NCG + gg:
                                           hw * (SLOTS // 128) + i * NCG + gg + 1], axis=0),
                            in_=gt_st[gb][:], in_offset=None), inc=scg[gb], k=16)
                        t_scg[gb] = scg[gb].n

        for h in range(NSEQ if SL >= 5 else 0):
            xg_t2 = []
            cg_t = []
            cs_t = []
            s_fm = s_scr[h].rearrange("p (dk t) -> p dk t", dk=DK)
            p.wait("sync", ssp[h], 16)
            p.wait("sync", pe, t_gstg_pe[2])
            p.wait("sync", pe, t_blk_lastmm[E * NHS - 1])
            p.wait("sync", dv, dv.n)
            p.op("sync", lambda s, h=h: s.dma_start(out=xstage[0], in_=x_d[h, 0:128, :]), inc=xl[0], k=16)
            xg_t2.append(xl[0].n)
            p.op("sync", lambda s, h=h, s_fm=s_fm: s.dma_start(
                out=cs_t0[0].rearrange("p (dk t) -> p dk t", dk=DK),
                in_=s_fm[:, :, 0:128]), inc=csl[0], k=16)
            cs_t.append(csl[0].n)
            p.wait("sync", scg[0], scg[0].n)
            p.wait("sync", scg[1], scg[1].n)
            p.op("sync", lambda s, h=h: s.dma_start(out=cg_t0[0], in_=g_scr[h][0:128, :]), inc=cgl[0], k=16)
            cg_t.append(cgl[0].n)
            t_pstore = {}
            t_pmul = {}
            so0_base = {0: so[0].n, 1: so[1].n}
            for g in range(NG):
                gb = g % 2
                pair = (pb[0], pb[1]) if gb == 0 else (pb[2], pb[3])
                if g + 1 < NG:
                    p.wait("sync", dv, dv.n)
                    if h == 0 and g == 0:
                        p.wait("sync", pe, t_gstg_pe[3])
                    p.op("sync", lambda s, h=h, g=g: s.dma_start(
                        out=xstage[(g + 1) % 2], in_=x_d[h, (g + 1) * 128:(g + 2) * 128, :]),
                        inc=xl[(g + 1) % 2], k=16)
                    xg_t2.append(xl[(g + 1) % 2].n)
                    p.op("sync", lambda s, h=h, g=g: s.dma_start(
                        out=cg_t0[(g + 1) % 2], in_=g_scr[h][(g + 1) * 128:(g + 2) * 128, :]),
                        inc=cgl[(g + 1) % 2], k=16)
                    cg_t.append(cgl[(g + 1) % 2].n)
                    p.op("sync", lambda s, h=h, g=g, s_fm=s_fm: s.dma_start(
                        out=cs_t0[(g + 1) % 2].rearrange("p (dk t) -> p dk t", dk=DK),
                        in_=s_fm[:, :, (g + 1) * 128:(g + 2) * 128]), inc=csl[(g + 1) % 2], k=16)
                    cs_t.append(csl[(g + 1) % 2].n)
                p.wait("vector", csl[gb], cs_t[g])
                if h == 0 and g < 2:
                    p.wait("vector", pe, t_gstg_pe[g])
                p.wait("vector", dv, dv.n)
                p.op("vector", lambda v, gb=gb: v.tensor_copy(gstg[gb][:], cs_t0[gb]), inc=dv)
                t_sc32 = dv.n
                p.wait("tensor", dv, t_sc32)
                if h == 0 and g < 2:
                    p.wait("tensor", dv, t_trpair[1 - (g % 2)])
                p.wait("tensor", dv, t_pmul.get(g - 2, 0))
                for dk in range(DK):
                    p.op("tensor", lambda te, dk=dk, gb=gb: te.transpose(
                        pair[dk // 4][:, (dk % 4) * 128:(dk % 4 + 1) * 128],
                        gstg[gb][:, dk * 128:(dk + 1) * 128], identf[:]),
                        inc=pe if dk == DK - 1 else None)
                t_str = pe.n
                p.wait("vector", pe, t_str)
                p.wait("vector", cgl[gb], cg_t[g])
                p.wait("vector", xl[gb], xg_t2[g])
                p.wait("vector", so[gb], t_pstore.get(g - 2, so0_base[gb]))
                p.wait("vector", dv, dv.n)
                for dk in range(DK):
                    p.op("vector", lambda v, dk=dk, gb=gb: v.tensor_mul(
                        cg_t0[gb][:, dk * 128:(dk + 1) * 128],
                        pair[dk // 4][:, (dk % 4) * 128:(dk % 4 + 1) * 128],
                        cg_t0[gb][:, dk * 128:(dk + 1) * 128]),
                        inc=dv if dk == DK - 1 else None)
                t_pmul[g] = dv.n
                ob = outg[gb]
                p.wait("vector", dv, t_pmul[g])
                p.op("vector", lambda v, gb=gb, ob=ob: v.tensor_add(
                    ob, cg_t0[gb], xstage[gb]), inc=dv)
                t_add = dv.n
                p.wait("scalar", dv, t_add)
                p.op("scalar", lambda s, h=h, g=g, ob=ob: s.dma_start(
                    out=out_d[h, g * 128:(g + 1) * 128, :], in_=ob), inc=so[gb], k=16)
                t_pstore[g] = so[gb].n

        p.wait("sync", so[0], so[0].n)
        p.wait("sync", so[1], so[1].n)

        p.install(block)
    return nc


def kernel(L0_attn_x_seq, L0_attn_W_decay, L0_attn_b_decay, L0_attn_W_input,
           L0_attn_b_input, L0_ffn_W1, L0_ffn_b1, L0_ffn_W2, L0_ffn_b2,
           L0_ffn_Wr, L0_ffn_br):
    x = np.ascontiguousarray(np.asarray(L0_attn_x_seq, dtype=np.float32))
    wd = np.ascontiguousarray(np.asarray(L0_attn_W_decay, dtype=np.float32))
    bd = np.ascontiguousarray(np.asarray(L0_attn_b_decay, dtype=np.float32))
    wi = np.ascontiguousarray(np.asarray(L0_attn_W_input, dtype=np.float32))
    bi = np.ascontiguousarray(np.asarray(L0_attn_b_input, dtype=np.float32))
    w1 = np.ascontiguousarray(np.asarray(L0_ffn_W1, dtype=np.float32))
    b1 = np.ascontiguousarray(np.asarray(L0_ffn_b1, dtype=np.float32))
    w2 = np.ascontiguousarray(np.asarray(L0_ffn_W2, dtype=np.float32))
    b2 = np.ascontiguousarray(np.asarray(L0_ffn_b2, dtype=np.float32))
    wr = np.ascontiguousarray(np.asarray(L0_ffn_Wr, dtype=np.float32))
    br = np.ascontiguousarray(np.asarray(L0_ffn_br, dtype=np.float32))

    n_cores = 8
    nc = build()
    in_maps = []
    for c in range(n_cores):
        in_maps.append(dict(
            x=x[2 * c:2 * c + 2], wd=wd, bd=bd, wi=wi, bi=bi,
            w1=w1, b1=b1, w2=w2, b2=b2, wr=wr, br=br))
    res = run_bass_kernel_spmd(nc, in_maps, core_ids=list(range(n_cores)))
    out = np.concatenate(
        [np.asarray(r["out"], dtype=np.float32).reshape(2, 2048, 1024) for r in res.results],
        axis=0)
    return out
